# version 05
# baseline (speedup 1.0000x reference)
import contextlib
import numpy as np
import concourse.bass as bass
import concourse.mybir as mybir
from concourse.bass_utils import run_bass_kernel_spmd

F32 = mybir.dt.float32
BF16 = mybir.dt.bfloat16
AF = mybir.ActivationFunctionType
ALU = mybir.AluOpType
AX = mybir.AxisListType

S = 2048
D = 1024
NT = S // 128
ND = D // 128


class Ctx:
    def __init__(self):
        self.nc = bass.Bass("TRN2", target_bir_lowering=False)
        self.es = contextlib.ExitStack()
        self.scopes = [self.es]
        self.sems = {}
        self.cnt = {}

    def push_scope(self):
        es = contextlib.ExitStack()
        self.scopes.append(es)
        return es

    def pop_scope(self):
        self.scopes.pop().close()

    def sem(self, name):
        if name not in self.sems:
            self.sems[name] = self.es.enter_context(self.nc.semaphore(name))
            self.cnt[name] = 0
        return self.sems[name]

    def sbuf(self, name, shape, dtype):
        return self.scopes[-1].enter_context(self.nc.sbuf_tensor(name, shape, dtype))

    def psum(self, name, shape, dtype=F32):
        return self.scopes[-1].enter_context(self.nc.psum_tensor(name, shape, dtype))

    def close(self):
        while len(self.scopes) > 1:
            self.pop_scope()
        self.es.close()



def emit_ln(ctx, x_res, xhatT, scr, tag):
    nc = ctx.nc
    st = scr["st"]
    hat = ctx.sem("hat_" + tag)
    tr = ctx.sem("tr_" + tag)
    aq = ctx.sem("aq_" + tag)

    with nc.Block() as b:
        @b.vector
        def _(v):
            for t in range(NT):
                v.tensor_reduce(out=st[:, t:t + 1], in_=x_res[:, t * D:(t + 1) * D],
                                axis=AX.X, op=ALU.add)

        @b.scalar
        def _(sc):
            for t in range(NT):
                sc.activation(out=scr["sq"][:], in_=x_res[:, t * D:(t + 1) * D],
                              func=AF.Square, accum_out=st[:, 16 + t:17 + t])

    with nc.Block() as b:
        @b.vector
        def _(v):
            v.tensor_scalar(out=st[:, 32:48], in0=st[:, 0:16], scalar1=1.0 / D,
                            scalar2=None, op0=ALU.mult)
            v.tensor_scalar(out=st[:, 48:64], in0=st[:, 16:32], scalar1=1.0 / D,
                            scalar2=None, op0=ALU.mult)
            v.drain()
            v.tensor_tensor(out=st[:, 64:80], in0=st[:, 32:48], in1=st[:, 32:48],
                            op=ALU.mult)
            v.drain()
            v.tensor_tensor(out=st[:, 80:96], in0=st[:, 48:64], in1=st[:, 64:80],
                            op=ALU.subtract)
            v.drain()
            v.tensor_scalar(out=st[:, 80:96], in0=st[:, 80:96], scalar1=1e-5,
                            scalar2=None, op0=ALU.add)
            v.tensor_scalar(out=st[:, 0:16], in0=st[:, 32:48], scalar1=-1.0,
                            scalar2=None, op0=ALU.mult)

    with nc.Block() as b:
        @b.scalar
        def _(sc):
            sc.activation(out=st[:, 48:64], in_=st[:, 80:96], func=AF.Sqrt)

    with nc.Block() as b:
        @b.vector
        def _(v):
            v.reciprocal(out=st[:, 16:32], in_=st[:, 48:64])
            v.drain()
            for t in range(NT):
                if t >= 2:
                    v.wait_ge(tr, ctx.cnt["tr_" + tag] + (t - 1) * 16)
                v.tensor_scalar(out=scr["xhat2"][:, (t % 2) * D:(t % 2 + 1) * D],
                                in0=x_res[:, t * D:(t + 1) * D],
                                scalar1=st[:, t:t + 1], scalar2=st[:, 16 + t:17 + t],
                                op0=ALU.add, op1=ALU.mult)
                v.drain()
                v.nop().then_inc(hat, 1)

        @b.sync
        def _(sy):
            for t in range(NT):
                sy.wait_ge(hat, ctx.cnt["hat_" + tag] + t + 1)
                sy.dma_start(
                    out=xhatT[:].rearrange("p (k s) -> p k s", k=ND)[:, :, t * 128:(t + 1) * 128],
                    in_=scr["xhat2"][:, (t % 2) * D:(t % 2 + 1) * D],
                    transpose=True).then_inc(tr, 16)
            sy.wait_ge(tr, ctx.cnt["tr_" + tag] + NT * 16).then_inc(aq, 1)

    ctx.cnt["hat_" + tag] += NT
    ctx.cnt["tr_" + tag] += NT * 16
    ctx.cnt["aq_" + tag] += 1



def emit_l0ffn(ctx, x_res, xhatT, aps, scr, experts=True):
    import os
    rpart = int(os.environ.get("RPART", "3"))
    """aps: dict with dram APs: gw [128,64], gu [8*16*128, 2048], dw [8*128, 16384].
    scr: h [128, 16*2048] bf16, guW [128, 2*2048] bf16, dwW [128, 16384] bf16,
         sg [128, 2*512] bf16, comb [128, 128] f32, st [128,96] f32 (reuse),
         pex [128, 8] f32 (exp p scratch), gwB [128, 64] bf16.
    psum: pgu 2x[128,1024], pd 2x[128,1024] (use scr['pgu0'],...)
    """
    nc = ctx.nc
    st = scr["st"]
    comb = scr["comb"]
    s_ld = ctx.sem("ffn_ld")
    s_gmm = ctx.sem("ffn_gmm")
    s_sg = ctx.sem("ffn_sg")
    s_h = ctx.sem("ffn_h")
    s_dmm = ctx.sem("ffn_dmm")
    s_acc = ctx.sem("ffn_acc")
    s_gu = ctx.sem("ffn_gu")
    s_dw = ctx.sem("ffn_dw")
    pgu = [scr["pgu0"], scr["pgu1"]]
    pd = [scr["pd0"], scr["pd1"]]

    s_rlog = ctx.sem("ffn_rlog")
    s_rmax = ctx.sem("ffn_rmax")
    s_rexp = ctx.sem("ffn_rexp")
    s_rdone = ctx.sem("ffn_rdone")
    with nc.Block() as b:
        @b.gpsimd
        def _(g):
            g.dma_start(out=scr["gwB"][:, 0:64], in_=aps["gw"][:, :]).then_inc(s_ld, 16)

        @b.tensor
        def _(t):
            t.wait_ge(s_ld, ctx.cnt["ffn_ld"] + 16)
            for ti in range(NT):
                if ti >= 2:
                    t.wait_ge(s_rdone, ti - 1)
                for kt in range(ND):
                    ins = t.matmul(pgu[ti % 2][:, 0:8],
                             xhatT[:, kt * S + ti * 128: kt * S + (ti + 1) * 128],
                             scr["gwB"][:, kt * 8:(kt + 1) * 8],
                             start=(kt == 0), stop=(kt == ND - 1))
                ins.then_inc(s_rlog, 1)

        @b.vector
        def _(v):
            for ti in range(NT):
                lg = pgu[ti % 2][:, 0:8]
                v.wait_ge(s_rlog, ti + 1)
                if rpart < 2:
                    v.tensor_copy(out=comb[:, ti * 8:(ti + 1) * 8], in_=lg)
                    v.drain()
                    v.nop().then_inc(s_rmax, 1)
                    v.nop().then_inc(s_rdone, 1)
                    continue
                v.tensor_reduce(out=st[:, 0:1], in_=lg, axis=AX.X, op=ALU.max)
                v.drain()
                v.tensor_scalar(out=st[:, 1:2], in0=st[:, 0:1], scalar1=-1.0,
                                scalar2=None, op0=ALU.mult)
                v.drain()
                v.nop().then_inc(s_rmax, 1)
                v.wait_ge(s_rexp, ti + 1)
                pex = scr["pex"][:, 0:8]
                if rpart < 3:
                    v.tensor_copy(out=comb[:, ti * 8:(ti + 1) * 8], in_=pex)
                    v.drain()
                    v.nop().then_inc(s_rdone, 1)
                    continue
                v.tensor_reduce(out=st[:, 2:3], in_=pex, axis=AX.X, op=ALU.max)
                v.drain()
                cslice = comb[:, ti * 8:(ti + 1) * 8]
                v.tensor_scalar(out=cslice, in0=pex, scalar1=st[:, 2:3],
                                scalar2=None, op0=ALU.is_ge)
                v.drain()
                v.scalar_tensor_tensor(out=cslice, in0=cslice, scalar=-1e30,
                                       in1=pex, op0=ALU.mult, op1=ALU.add)
                v.drain()
                v.tensor_reduce(out=st[:, 4:5], in_=cslice, axis=AX.X, op=ALU.max)
                v.drain()
                v.tensor_tensor(out=st[:, 5:6], in0=st[:, 2:3], in1=st[:, 4:5], op=ALU.add)
                v.drain()
                v.reciprocal(out=st[:, 6:7], in_=st[:, 5:6])
                v.drain()
                v.tensor_scalar(out=cslice, in0=pex, scalar1=st[:, 4:5],
                                scalar2=None, op0=ALU.is_ge)
                v.drain()
                v.tensor_tensor(out=cslice, in0=cslice, in1=pex, op=ALU.mult)
                v.drain()
                v.tensor_scalar(out=cslice, in0=cslice, scalar1=st[:, 6:7],
                                scalar2=None, op0=ALU.mult).then_inc(s_rdone, 1)
                v.drain()

        @b.scalar
        def _(sc):
            for ti in range(NT):
                sc.wait_ge(s_rmax, ti + 1)
                if rpart < 2:
                    sc.nop().then_inc(s_rexp, 1)
                    continue
                sc.activation(out=scr["pex"][:, 0:8], in_=pgu[ti % 2][:, 0:8], func=AF.Exp,
                              bias=st[:, 1:2]).then_inc(s_rexp, 1)

    ctx.cnt["ffn_ld"] += 16

    if not experts:
        return

    E = 8
    with nc.Block() as b:
        @b.gpsimd
        def _(g):
            def load_pair(e, f):
                p = e * 16 + f
                if p >= 2:
                    g.wait_ge(s_gmm, ctx.cnt["ffn_gmm"] + (p - 1) * 4)
                g.dma_start(out=scr["guW"][:, (f % 2) * 2048:(f % 2 + 1) * 2048],
                            in_=aps["gu"][p * 128:(p + 1) * 128, :]).then_inc(s_gu, 16)

            for e in range(E):
                load_pair(e, 0)
                load_pair(e, 1)
                if e >= 1:
                    g.wait_ge(s_dmm, ctx.cnt["ffn_dmm"] + e * 16)
                g.dma_start(out=scr["dwW"][:],
                            in_=aps["dw"][e * 128:(e + 1) * 128, :]).then_inc(s_dw, 16)
                for f in range(2, 16):
                    load_pair(e, f)

        @b.tensor
        def _(t):
            for e in range(E):
                for f in range(16):
                    t.wait_ge(s_gu, ctx.cnt["ffn_gu"] + (e * 16 + f + 1) * 16)
                    for c in range(4):
                        cc = e * 64 + f * 4 + c
                        if cc >= 2:
                            t.wait_ge(s_h, ctx.cnt["ffn_h"] + cc - 1)
                        gub = (f % 2) * 2048
                        for kt in range(ND):
                            t.matmul(pgu[c % 2][:, 0:512],
                                     scr["guW"][:, gub + kt * 256: gub + kt * 256 + 128],
                                     xhatT[:, kt * S + c * 512: kt * S + (c + 1) * 512],
                                     start=(kt == 0), stop=(kt == ND - 1))
                        for kt in range(ND):
                            ins = t.matmul(pgu[c % 2][:, 512:1024],
                                     scr["guW"][:, gub + kt * 256 + 128: gub + kt * 256 + 256],
                                     xhatT[:, kt * S + c * 512: kt * S + (c + 1) * 512],
                                     start=(kt == 0), stop=(kt == ND - 1))
                        ins.then_inc(s_gmm, 1)
                t.wait_ge(s_h, ctx.cnt["ffn_h"] + (e + 1) * 64)
                t.wait_ge(s_dw, ctx.cnt["ffn_dw"] + (e + 1) * 16)
                for tt in range(16):
                    if e * 16 + tt >= 2:
                        t.wait_ge(s_acc, ctx.cnt["ffn_acc"] + e * 16 + tt - 1)
                    for ft in range(16):
                        for nn in range(2):
                            ins = t.matmul(pd[tt % 2][:, nn * 512:(nn + 1) * 512],
                                     scr["h"][:, ft * 2048 + tt * 128: ft * 2048 + (tt + 1) * 128],
                                     scr["dwW"][:, ft * 1024 + nn * 512: ft * 1024 + (nn + 1) * 512],
                                     start=(ft == 0), stop=(ft == 15))
                    ins.then_inc(s_dmm, 1)

        @b.scalar
        def _(sc):
            for e in range(E):
                for f in range(16):
                    for c in range(4):
                        cc = e * 64 + f * 4 + c
                        sc.wait_ge(s_gmm, ctx.cnt["ffn_gmm"] + cc + 1)
                        if cc >= 2:
                            sc.wait_ge(s_h, ctx.cnt["ffn_h"] + cc - 1)
                        sc.activation(out=scr["sg"][:, (c % 2) * 512:(c % 2 + 1) * 512],
                                      in_=pgu[c % 2][:, 0:512],
                                      func=AF.Silu).then_inc(s_sg, 1)

        @b.vector
        def _(v):
            for e in range(E):
                for f in range(16):
                    for c in range(4):
                        cc = e * 64 + f * 4 + c
                        v.wait_ge(s_sg, ctx.cnt["ffn_sg"] + cc + 1)
                        v.tensor_tensor(
                            out=scr["h"][:, f * 2048 + c * 512: f * 2048 + (c + 1) * 512],
                            in0=pgu[c % 2][:, 512:1024],
                            in1=scr["sg"][:, (c % 2) * 512:(c % 2 + 1) * 512],
                            op=ALU.mult).then_inc(s_h, 1)
                for tt in range(16):
                    v.wait_ge(s_dmm, ctx.cnt["ffn_dmm"] + e * 16 + tt + 1)
                    v.drain()
                    v.scalar_tensor_tensor(
                        out=x_res[:, tt * D:(tt + 1) * D],
                        in0=pd[tt % 2][:, :],
                        scalar=comb[:, tt * 8 + e: tt * 8 + e + 1],
                        in1=x_res[:, tt * D:(tt + 1) * D],
                        op0=ALU.mult, op1=ALU.add).then_inc(s_acc, 1)

    ctx.cnt["ffn_gu"] += E * 16 * 16
    ctx.cnt["ffn_dw"] += E * 16
    ctx.cnt["ffn_gmm"] += E * 64
    ctx.cnt["ffn_sg"] += E * 64
    ctx.cnt["ffn_h"] += E * 64
    ctx.cnt["ffn_dmm"] += E * 16
    ctx.cnt["ffn_acc"] += E * 16



def prep_l0ffn(inp):
    gw = np.ascontiguousarray(
        inp["L0_ffn_gate_w"].reshape(8, 128, 8).transpose(1, 0, 2).reshape(128, 64))
    guw = inp["L0_ffn_gate_up_w"]
    pairs = np.empty((8, 16, 128, 2048), dtype=np.float32)
    for f in range(16):
        blk = np.concatenate([guw[:, :, f * 128:(f + 1) * 128],
                              guw[:, :, 2048 + f * 128: 2048 + (f + 1) * 128]],
                             axis=2)
        pairs[:, f] = blk.reshape(8, 8, 128, 256).transpose(0, 2, 1, 3).reshape(8, 128, 2048)
    gu = pairs.reshape(8 * 16 * 128, 2048)
    dw = inp["L0_ffn_down_w"].reshape(8, 16, 128, 1024).transpose(0, 2, 1, 3).reshape(8 * 128, 16384)
    return {"gw": gw, "gu": np.ascontiguousarray(gu), "dw": np.ascontiguousarray(dw)}



def emit_l1ffn(ctx, x_res, xhatT, aps, scr, nexp=16):
    nc = ctx.nc
    st = scr["st"]
    comb = scr["comb"]
    pgu = [scr["pgu0"], scr["pgu1"]]
    pd = [scr["pd0"], scr["pd1"]]
    s_gu = ctx.sem("f1_gu")
    s_dw = ctx.sem("f1_dw")
    s_gmm = ctx.sem("f1_gmm")
    s_sg = ctx.sem("f1_sg")
    s_h = ctx.sem("f1_h")
    s_dmm = ctx.sem("f1_dmm")
    s_acc = ctx.sem("f1_acc")
    s_ld = ctx.sem("f1_ld")
    s_glog = ctx.sem("f1_glog")
    s_gsig = ctx.sem("f1_gsig")
    s_gdone = ctx.sem("f1_gdone")

    with nc.Block() as b:
        @b.gpsimd
        def _(g):
            g.dma_start(out=scr["gwB"][:, 0:128], in_=aps["gw"][:, :]).then_inc(s_ld, 16)

        @b.tensor
        def _(t):
            t.wait_ge(s_ld, 16)
            for ti in range(NT):
                if ti >= 2:
                    t.wait_ge(s_gdone, ti - 1)
                for kt in range(ND):
                    ins = t.matmul(pgu[ti % 2][:, 0:16],
                                   xhatT[:, kt * S + ti * 128: kt * S + (ti + 1) * 128],
                                   scr["gwB"][:, kt * 16:(kt + 1) * 16],
                                   start=(kt == 0), stop=(kt == ND - 1))
                ins.then_inc(s_glog, 1)

        @b.scalar
        def _(sc):
            for ti in range(NT):
                sc.wait_ge(s_glog, ti + 1)
                if ti >= 1:
                    sc.wait_ge(s_gdone, ti)
                sc.activation(out=scr["pex"][:, 0:16], in_=pgu[ti % 2][:, 0:16],
                              func=AF.Sigmoid).then_inc(s_gsig, 1)

        @b.vector
        def _(v):
            for ti in range(NT):
                v.wait_ge(s_gsig, ti + 1)
                gs = scr["pex"][:, 0:16]
                cs = comb[:, ti * 16:(ti + 1) * 16]
                v.tensor_copy(out=cs, in_=gs)
                v.drain()
                for k in range(4):
                    v.tensor_reduce(out=st[:, 9:10], in_=cs, axis=AX.X, op=ALU.max)
                    v.drain()
                    if k == 0:
                        v.tensor_copy(out=st[:, 8:9], in_=st[:, 9:10])
                    else:
                        v.tensor_tensor(out=st[:, 8:9], in0=st[:, 8:9], in1=st[:, 9:10],
                                        op=ALU.add)
                    if k < 3:
                        v.tensor_scalar(out=st[:, 10:26], in0=cs, scalar1=st[:, 9:10],
                                        scalar2=None, op0=ALU.is_ge)
                        v.drain()
                        v.scalar_tensor_tensor(out=cs, in0=st[:, 10:26], scalar=-1e30,
                                               in1=cs, op0=ALU.mult, op1=ALU.add)
                        v.drain()
                v.drain()
                v.reciprocal(out=st[:, 8:9], in_=st[:, 8:9])
                v.drain()
                v.tensor_scalar(out=cs, in0=gs, scalar1=st[:, 9:10],
                                scalar2=None, op0=ALU.is_ge)
                v.drain()
                v.tensor_tensor(out=cs, in0=cs, in1=gs, op=ALU.mult)
                v.drain()
                v.tensor_scalar(out=cs, in0=cs, scalar1=st[:, 8:9],
                                scalar2=None, op0=ALU.mult).then_inc(s_gdone, 1)
                v.drain()

    with nc.Block() as b:
        @b.gpsimd
        def _(g):
            def load_pair(p, src_row):
                if p >= 2:
                    g.wait_ge(s_gmm, (p - 1) * 4)
                g.dma_start(out=scr["guW"][:, (p % 2) * 2048:(p % 2 + 1) * 2048],
                            in_=aps[src_row[0]][src_row[1] * 128:(src_row[1] + 1) * 128, :]
                            ).then_inc(s_gu, 16)

            pairs = [("shp", f) for f in range(8)]
            for e in range(nexp):
                pairs += [("ep", e * 4 + f) for f in range(4)]
            pi = 0
            g.dma_start(out=scr["dwW"][:, 0:8192],
                        in_=aps["sho"][:, :]).then_inc(s_dw, 16)
            for f in range(8):
                load_pair(pi, pairs[pi]); pi += 1
            for e in range(nexp):
                if e >= 2:
                    g.wait_ge(s_dmm, (e - 1) * NT + NT)
                g.dma_start(out=scr["dwW"][:, 8192 + (e % 2) * 4096: 8192 + (e % 2 + 1) * 4096],
                            in_=aps["eo"][e * 128:(e + 1) * 128, :]).then_inc(s_dw, 16)
                for f in range(4):
                    load_pair(pi, pairs[pi]); pi += 1

        @b.tensor
        def _(t):
            pcnt = 0
            ccnt = 0
            dcnt = 0
            for f in range(8):
                t.wait_ge(s_gu, (pcnt + 1) * 16)
                for c in range(4):
                    if ccnt >= 2:
                        t.wait_ge(s_h, ccnt - 1)
                    gub = (pcnt % 2) * 2048
                    for kt in range(ND):
                        t.matmul(pgu[c % 2][:, 0:512],
                                 scr["guW"][:, gub + kt * 256: gub + kt * 256 + 128],
                                 xhatT[:, kt * S + c * 512: kt * S + (c + 1) * 512],
                                 start=(kt == 0), stop=(kt == ND - 1))
                    for kt in range(ND):
                        ins = t.matmul(pgu[c % 2][:, 512:1024],
                                       scr["guW"][:, gub + kt * 256 + 128: gub + kt * 256 + 256],
                                       xhatT[:, kt * S + c * 512: kt * S + (c + 1) * 512],
                                       start=(kt == 0), stop=(kt == ND - 1))
                    ins.then_inc(s_gmm, 1)
                    ccnt += 1
                pcnt += 1
            t.wait_ge(s_h, ccnt)
            t.wait_ge(s_dw, 16)
            for tt in range(NT):
                if tt >= 2:
                    t.wait_ge(s_acc, dcnt + tt - 1)
                for ft in range(8):
                    for nn in range(2):
                        ins = t.matmul(pd[tt % 2][:, nn * 512:(nn + 1) * 512],
                                       scr["h"][:, ft * 2048 + tt * 128: ft * 2048 + (tt + 1) * 128],
                                       scr["dwW"][:, ft * 1024 + nn * 512: ft * 1024 + (nn + 1) * 512],
                                       start=(ft == 0), stop=(ft == 7))
                ins.then_inc(s_dmm, 1)
            dcnt += NT
            for e in range(nexp):
                t.wait_ge(s_dw, 32 + e * 16)
                for f in range(4):
                    t.wait_ge(s_gu, (pcnt + 1) * 16)
                    for c in range(4):
                        if ccnt >= 2:
                            t.wait_ge(s_h, ccnt - 1)
                        gub = (pcnt % 2) * 2048
                        for kt in range(ND):
                            t.matmul(pgu[c % 2][:, 0:512],
                                     scr["guW"][:, gub + kt * 256: gub + kt * 256 + 128],
                                     xhatT[:, kt * S + c * 512: kt * S + (c + 1) * 512],
                                     start=(kt == 0), stop=(kt == ND - 1))
                        for kt in range(ND):
                            ins = t.matmul(pgu[c % 2][:, 512:1024],
                                           scr["guW"][:, gub + kt * 256 + 128: gub + kt * 256 + 256],
                                           xhatT[:, kt * S + c * 512: kt * S + (c + 1) * 512],
                                           start=(kt == 0), stop=(kt == ND - 1))
                        ins.then_inc(s_gmm, 1)
                        ccnt += 1
                    pcnt += 1
                t.wait_ge(s_h, ccnt)
                for tt in range(NT):
                    if dcnt + tt >= 2:
                        t.wait_ge(s_acc, dcnt + tt - 1)
                    ob = 8192 + (e % 2) * 4096
                    for ft in range(4):
                        for nn in range(2):
                            ins = t.matmul(pd[tt % 2][:, nn * 512:(nn + 1) * 512],
                                           scr["h"][:, ft * 2048 + tt * 128: ft * 2048 + (tt + 1) * 128],
                                           scr["dwW"][:, ob + ft * 1024 + nn * 512: ob + ft * 1024 + (nn + 1) * 512],
                                           start=(ft == 0), stop=(ft == 3))
                    ins.then_inc(s_dmm, 1)
                dcnt += NT

        @b.scalar
        def _(sc):
            ccnt = 0
            for blk in range(1 + nexp):
                npair = 8 if blk == 0 else 4
                for f in range(npair):
                    for c in range(4):
                        sc.wait_ge(s_gmm, ccnt + 1)
                        if ccnt >= 2:
                            sc.wait_ge(s_h, ccnt - 1)
                        sc.activation(out=scr["sg"][:, (c % 2) * 512:(c % 2 + 1) * 512],
                                      in_=pgu[c % 2][:, 0:512],
                                      func=AF.Silu).then_inc(s_sg, 1)
                        ccnt += 1

        @b.vector
        def _(v):
            ccnt = 0
            dcnt = 0
            for blk in range(1 + nexp):
                npair = 8 if blk == 0 else 4
                for f in range(npair):
                    for c in range(4):
                        v.wait_ge(s_sg, ccnt + 1)
                        v.tensor_tensor(
                            out=scr["h"][:, f * 2048 + c * 512: f * 2048 + (c + 1) * 512],
                            in0=pgu[c % 2][:, 512:1024],
                            in1=scr["sg"][:, (c % 2) * 512:(c % 2 + 1) * 512],
                            op=ALU.mult).then_inc(s_h, 1)
                        ccnt += 1
                for tt in range(NT):
                    v.wait_ge(s_dmm, dcnt + tt + 1)
                    v.drain()
                    if blk == 0:
                        v.tensor_tensor(out=x_res[:, tt * D:(tt + 1) * D],
                                        in0=pd[tt % 2][:, :],
                                        in1=x_res[:, tt * D:(tt + 1) * D],
                                        op=ALU.add).then_inc(s_acc, 1)
                    else:
                        e = blk - 1
                        v.scalar_tensor_tensor(
                            out=x_res[:, tt * D:(tt + 1) * D],
                            in0=pd[tt % 2][:, :],
                            scalar=comb[:, tt * 16 + e: tt * 16 + e + 1],
                            in1=x_res[:, tt * D:(tt + 1) * D],
                            op0=ALU.mult, op1=ALU.add).then_inc(s_acc, 1)
                dcnt += NT


def prep_l1ffn(inp):
    gw = np.ascontiguousarray(
        inp["L1_ffn_gate_w"].T.reshape(8, 128, 16).transpose(1, 0, 2).reshape(128, 128))
    wsi, vsi = inp["L1_ffn_ws_in"], inp["L1_ffn_vs_in"]
    shp = np.empty((8, 128, 2048), np.float32)
    for f in range(8):
        blk = np.concatenate([wsi[:, f * 128:(f + 1) * 128],
                              vsi[:, f * 128:(f + 1) * 128]], axis=1)
        shp[f] = blk.reshape(8, 128, 256).transpose(1, 0, 2).reshape(128, 2048)
    sho = inp["L1_ffn_ws_out"].reshape(8, 128, 1024).transpose(1, 0, 2).reshape(128, 8192)
    ewin, evin = inp["L1_ffn_e_win"], inp["L1_ffn_e_vin"]
    ep = np.empty((16, 4, 128, 2048), np.float32)
    for e in range(16):
        for f in range(4):
            blk = np.concatenate([ewin[e][:, f * 128:(f + 1) * 128],
                                  evin[e][:, f * 128:(f + 1) * 128]], axis=1)
            ep[e, f] = blk.reshape(8, 128, 256).transpose(1, 0, 2).reshape(128, 2048)
    eo = inp["L1_ffn_e_wout"].reshape(16, 4, 128, 1024).transpose(0, 2, 1, 3).reshape(16 * 128, 4096)
    return {"gw": gw, "shp": np.ascontiguousarray(shp.reshape(8 * 128, 2048)),
            "sho": np.ascontiguousarray(sho),
            "ep": np.ascontiguousarray(ep.reshape(16 * 4 * 128, 2048)),
            "eo": np.ascontiguousarray(eo)}



def emit_l1attn(ctx, x_res, xhatT, aps, scr):
    nc = ctx.nc
    st = scr["st"]
    u1 = scr["u1"]
    I128 = scr["cst"][:, 0:128]
    maskB = scr["cst"][:, 128:256]
    permB = scr["cst"][:, 256:384]
    ones = scr["ones"]
    s1 = scr["ps_s1"]
    s2 = scr["ps_s2"]
    G = 2
    NG = 16 // G

    s_ld = ctx.sem("a1_ld")
    s_vmm = ctx.sem("a1_vmm")
    s_vev = ctx.sem("a1_vev")
    s_lam = ctx.sem("a1_lam")

    with nc.Block() as b:
        @b.gpsimd
        def _(g):
            for key, dst in [("C", scr["C"][:]), ("Sg", scr["Sg"][:]),
                             ("cst", scr["cst"][:]), ("ones", scr["ones"][:]),
                             ("bq", scr["bqB"][:]), ("bk", scr["bkB"][:]),
                             ("bv", scr["bvB"][:]), ("bo", scr["boB"][:]),
                             ("lqk", scr["lqk"][0:16, :]), ("hn", scr["hnB"][:]),
                             ("wv", u1[:])]:
                g.dma_start(out=dst, in_=aps[key][:, :]).then_inc(s_ld, 16)

        @b.tensor
        def _(t):
            t.wait_ge(s_ld, ctx.cnt["a1_ld"] + 11 * 16)
            pb = [s1[:, 0:512], s1[:, 512:1024], s2[:, 0:512], s2[:, 512:1024]]
            for tt in range(NT):
                for nn in range(2):
                    idx = tt * 2 + nn
                    if idx >= 4:
                        t.wait_ge(s_vev, ctx.cnt["a1_vev"] + idx - 3)
                    out = pb[idx % 4]
                    for kt in range(ND):
                        t.matmul(out, xhatT[:, kt * S + tt * 128:kt * S + (tt + 1) * 128],
                                 u1[:, kt * 1024 + nn * 512: kt * 1024 + (nn + 1) * 512],
                                 start=(kt == 0), stop=False)
                    ins = t.matmul(out, ones[0:1, 0:128],
                                   scr["bvB"][0:1, nn * 512:(nn + 1) * 512],
                                   start=False, stop=True)
                    ins.then_inc(s_vmm, 1)

        @b.vector
        def _(v):
            pb = [s1[:, 0:512], s1[:, 512:1024], s2[:, 0:512], s2[:, 512:1024]]
            for tt in range(NT):
                for nn in range(2):
                    idx = tt * 2 + nn
                    v.wait_ge(s_vmm, ctx.cnt["a1_vmm"] + idx + 1)
                    v.tensor_copy(out=scr["v_tok"][:, tt * 1024 + nn * 512: tt * 1024 + (nn + 1) * 512],
                                  in_=pb[idx % 4]).then_inc(s_vev, 1)
            lqk = scr["lqk"]
            v.tensor_tensor(out=st[0:16, 32:96], in0=lqk[0:16, 0:64],
                            in1=lqk[0:16, 64:128], op=ALU.mult)
            v.drain()
            v.tensor_reduce(out=st[0:16, 0:1], in_=st[0:16, 32:96], axis=AX.X, op=ALU.add)
            v.drain()
            v.tensor_tensor(out=st[0:16, 32:96], in0=lqk[0:16, 128:192],
                            in1=lqk[0:16, 192:256], op=ALU.mult)
            v.drain()
            v.tensor_reduce(out=st[0:16, 1:2], in_=st[0:16, 32:96], axis=AX.X, op=ALU.add)
            v.drain()
            v.nop().then_inc(s_lam, 1)

    ctx.cnt["a1_ld"] += 11 * 16
    ctx.cnt["a1_vmm"] += 32
    ctx.cnt["a1_vev"] += 32

    with nc.Block() as b:
        @b.scalar
        def _(sc):
            sc.wait_ge(s_lam, 1)
            sc.activation(out=st[0:16, 2:3], in_=st[0:16, 0:1], func=AF.Exp)
            sc.activation(out=st[0:16, 3:4], in_=st[0:16, 1:2], func=AF.Exp)

    with nc.Block() as b:
        @b.vector
        def _(v):
            v.tensor_tensor(out=st[0:16, 4:5], in0=st[0:16, 2:3], in1=st[0:16, 3:4],
                            op=ALU.subtract)
            v.drain()
            v.tensor_scalar(out=st[0:16, 4:5], in0=st[0:16, 4:5], scalar1=0.2,
                            scalar2=None, op0=ALU.add)

    lam_d = nc.dram_tensor("lam_scratch", [16], F32, kind="Internal").ap()
    with nc.Block() as b:
        @b.sync
        def _(sy):
            sy.dma_start(out=lam_d[:], in_=st[0:16, 4:5]).then_inc(s_ld, 16)
            sy.wait_ge(s_ld, ctx.cnt["a1_ld"] + 16)

    with nc.Block() as b:
        @b.gpsimd
        def _(g):
            g.dma_start(out=scr["lamB"][:],
                        in_=lam_d[:].unsqueeze(0).broadcast_to([128, 16])).then_inc(s_ld, 16)
            g.wait_ge(s_ld, ctx.cnt["a1_ld"] + 32)
    ctx.cnt["a1_ld"] += 32

    s_pmm = ctx.sem("a1_pmm")
    s_pev = ctx.sem("a1_pev")
    s_rmm = ctx.sem("a1_rmm")
    s_rdv = ctx.sem("a1_rdv")
    s_qkc = ctx.sem("a1_qkc")
    s_smm = ctx.sem("a1_smm")
    s_mx = ctx.sem("a1_mx")
    s_pexp = ctx.sem("a1_pexp")
    s_att = ctx.sem("a1_att")
    s_trD = ctx.sem("a1_trD")
    s_av = ctx.sem("a1_av")
    s_oev = ctx.sem("a1_oev")
    s_wld = ctx.sem("a1_wld")
    stg = None

    for g_i in range(NG):
        stg = [scr["e1"], scr["e2"], scr["aS"], scr["aT"]]
        with nc.Block() as b:
            @b.gpsimd
            def _(g):
                g.dma_start(out=u1[:, 0:2048],
                            in_=aps["wq"][g_i * 128:(g_i + 1) * 128, :]).then_inc(s_wld, 16)
                g.dma_start(out=u1[:, 2048:4096],
                            in_=aps["wk"][g_i * 128:(g_i + 1) * 128, :]).then_inc(s_wld, 16)

            @b.tensor
            def _(t):
                t.wait_ge(s_wld, ctx.cnt["a1_wld"] + 32)
                pb = [s1[:, 0:512], s1[:, 512:1024], s1[:, 1024:1536], s1[:, 1536:2048],
                      s2[:, 0:512], s2[:, 512:1024], s2[:, 1024:1536], scr["ps_o"][:, 0:512]]
                cnt = 0
                for qk in range(2):
                    wsl = u1[:, 0:2048] if qk == 0 else u1[:, 2048:4096]
                    for j in range(G):
                        for c in range(4):
                            if cnt >= 8:
                                t.wait_ge(s_pev, ctx.cnt["a1_pev"] + cnt - 7)
                            out = pb[cnt % 8]
                            for kt in range(ND):
                                ins = t.matmul(
                                    out,
                                    wsl[:, kt * 256 + j * 128: kt * 256 + (j + 1) * 128],
                                    xhatT[:, kt * S + c * 512: kt * S + (c + 1) * 512],
                                    start=(kt == 0), stop=(kt == ND - 1))
                            ins.then_inc(s_pmm, 1)
                            cnt += 1

            @b.scalar
            def _(sc):
                pb = [s1[:, 0:512], s1[:, 512:1024], s1[:, 1024:1536], s1[:, 1536:2048],
                      s2[:, 0:512], s2[:, 512:1024], s2[:, 1024:1536], scr["ps_o"][:, 0:512]]
                cnt = 0
                for qk in range(2):
                    bias = scr["bqB"] if qk == 0 else scr["bkB"]
                    for j in range(G):
                        dst = stg[qk * G + j]
                        for c in range(4):
                            sc.wait_ge(s_pmm, ctx.cnt["a1_pmm"] + cnt + 1)
                            sc.activation(out=dst[:, c * 512:(c + 1) * 512],
                                          in_=pb[cnt % 8], func=AF.Identity,
                                          bias=bias[:, g_i * G + j: g_i * G + j + 1]
                                          ).then_inc(s_pev, 1)
                            cnt += 1

        ctx.cnt["a1_wld"] += 32
        ctx.cnt["a1_pmm"] += 16
        ctx.cnt["a1_pev"] += 16

        with nc.Block() as b:
            @b.tensor
            def _(t):
                pb = [s1[:, 0:512], s1[:, 512:1024], s1[:, 1024:1536], s1[:, 1536:2048],
                      s2[:, 0:512], s2[:, 512:1024], s2[:, 1024:1536], scr["ps_o"][:, 0:512]]
                cnt = 0
                for sidx in range(4):
                    src = stg[sidx]
                    for c in range(4):
                        if cnt >= 8:
                            t.wait_ge(s_rdv, ctx.cnt["a1_rdv"] + cnt - 7)
                        ins = t.matmul(pb[cnt % 8], permB, src[:, c * 512:(c + 1) * 512],
                                       start=True, stop=True)
                        ins.then_inc(s_rmm, 1)
                        cnt += 1

            @b.vector
            def _(v):
                pb = [s1[:, 0:512], s1[:, 512:1024], s1[:, 1024:1536], s1[:, 1536:2048],
                      s2[:, 0:512], s2[:, 512:1024], s2[:, 1024:1536], scr["ps_o"][:, 0:512]]
                cnt = 0
                for sidx in range(4):
                    src = stg[sidx]
                    for c in range(4):
                        sl = slice(c * 512, (c + 1) * 512)
                        v.wait_ge(s_rmm, ctx.cnt["a1_rmm"] + cnt + 1)
                        v.tensor_tensor(out=src[:, sl], in0=src[:, sl],
                                        in1=scr["C"][:, sl], op=ALU.mult)
                        v.drain()
                        v.tensor_tensor(out=scr["rs"][:, 0:512], in0=pb[cnt % 8],
                                        in1=scr["Sg"][:, sl], op=ALU.mult)
                        v.drain()
                        v.tensor_tensor(out=src[:, sl], in0=src[:, sl],
                                        in1=scr["rs"][:, 0:512],
                                        op=ALU.add).then_inc(s_rdv, 1)
                        v.drain()
                        cnt += 1
                for sidx in range(4):
                    v.tensor_copy(out=u1[:, sidx * 2048:(sidx + 1) * 2048],
                                  in_=stg[sidx][:]).then_inc(s_qkc, 1)
                v.drain()

        ctx.cnt["a1_rmm"] += 16
        ctx.cnt["a1_rdv"] += 16
        ctx.cnt["a1_qkc"] += 4

        prs = [(r, j) for r in range(NT) for j in range(G)]
        with nc.Block() as b:
            @b.tensor
            def _(t):
                for idx, (r, j) in enumerate(prs):
                    h = g_i * G + j
                    q = u1[:, j * 2048:(j + 1) * 2048]
                    k = u1[:, (2 + j) * 2048:(3 + j) * 2048]
                    vc = (r + 1) * 128
                    nch = (vc + 511) // 512
                    if idx >= 1:
                        t.wait_ge(s_pexp, ctx.cnt["a1_pexp"] + 2 * idx - 1)
                    for c in range(nch):
                        c0, c1 = c * 512, min((c + 1) * 512, vc)
                        last = (c == nch - 1)
                        t.matmul(s1[:, c0:c1], q[0:64, r * 128:(r + 1) * 128],
                                 k[0:64, c0:c1], start=True, stop=not last)
                        if last:
                            ins = t.matmul(s1[:, vc - 128:vc], I128, maskB,
                                           start=False, stop=True)
                    ins.then_inc(s_smm, 1)
                    if idx >= 1:
                        t.wait_ge(s_pexp, ctx.cnt["a1_pexp"] + 2 * idx)
                    if vc > 1536:
                        t.wait_ge(s_pexp, ctx.cnt["a1_pexp"] + 2 * idx + 1)
                    def s2sl(c0, c1):
                        if c0 >= 1536:
                            return s1[:, c0:c1]
                        return s2[:, c0:c1]
                    for c in range(nch):
                        c0, c1 = c * 512, min((c + 1) * 512, vc)
                        last = (c == nch - 1)
                        t.matmul(s2sl(c0, c1), q[64:128, r * 128:(r + 1) * 128],
                                 k[64:128, c0:c1], start=True, stop=not last)
                        if last:
                            ins = t.matmul(s2sl(vc - 128, vc), I128, maskB,
                                           start=False, stop=True)
                    ins.then_inc(s_smm, 1)
                    t.wait_ge(s_trD, ctx.cnt["a1_trD"] + (idx + 1) * 16)
                    t.wait_ge(s_oev, ctx.cnt["a1_oev"] + idx)
                    tb = scr["aT"] if idx % 2 == 0 else scr["aS"]
                    for c in range(r + 1):
                        ins = t.matmul(scr["ps_o"][:, 0:64],
                                       tb[:, c * 128:(c + 1) * 128],
                                       scr["v_tok"][:, c * 1024 + h * 64: c * 1024 + h * 64 + 64],
                                       start=(c == 0), stop=(c == r))
                    ins.then_inc(s_av, 1)

            @b.vector
            def _(v):
                def oev(pidx):
                    pr, pj = prs[pidx]
                    ph = g_i * G + pj
                    v.wait_ge(s_av, ctx.cnt["a1_av"] + pidx + 1)
                    osl = scr["o_tok"][:, pr * 1024 + ph * 64: pr * 1024 + ph * 64 + 64]
                    v.tensor_copy(out=osl, in_=scr["ps_o"][:, 0:64])
                    v.drain()
                    v.nop().then_inc(s_oev, 1)
                    v.tensor_tensor(out=scr["rs"][:, 0:64], in0=osl, in1=osl,
                                    op=ALU.mult)
                    v.drain()
                    v.tensor_reduce(out=scr["ssqB"][:, pr * 16 + ph: pr * 16 + ph + 1],
                                    in_=scr["rs"][:, 0:64], axis=AX.X, op=ALU.add)
                    v.drain()

                for idx, (r, j) in enumerate(prs):
                    h = g_i * G + j
                    vc = (r + 1) * 128
                    v.wait_ge(s_smm, ctx.cnt["a1_smm"] + 2 * idx + 1)
                    v.tensor_reduce(out=st[:, 8:9], in_=s1[:, 0:vc], axis=AX.X,
                                    op=ALU.max)
                    v.drain()
                    v.tensor_scalar(out=st[:, 9:10], in0=st[:, 8:9], scalar1=-0.125,
                                    scalar2=None, op0=ALU.mult)
                    v.drain()
                    v.nop().then_inc(s_mx, 1)
                    if idx >= 1:
                        oev(idx - 1)
                    v.wait_ge(s_smm, ctx.cnt["a1_smm"] + 2 * idx + 2)
                    v.tensor_reduce(out=st[:, 10:11], in_=s2[:, 0:min(vc, 1536)],
                                    axis=AX.X, op=ALU.max)
                    v.drain()
                    if vc > 1536:
                        v.tensor_reduce(out=st[:, 20:21], in_=s1[:, 1536:vc],
                                        axis=AX.X, op=ALU.max)
                        v.drain()
                        v.tensor_tensor(out=st[:, 10:11], in0=st[:, 10:11],
                                        in1=st[:, 20:21], op=ALU.max)
                        v.drain()
                    v.tensor_scalar(out=st[:, 11:12], in0=st[:, 10:11], scalar1=-0.125,
                                    scalar2=None, op0=ALU.mult)
                    v.drain()
                    v.nop().then_inc(s_mx, 1)
                    v.wait_ge(s_pexp, ctx.cnt["a1_pexp"] + 2 * idx + 2)
                    if vc > 1536:
                        v.tensor_tensor(out=st[:, 13:14], in0=st[:, 13:14],
                                        in1=st[:, 21:22], op=ALU.add)
                        v.drain()
                    v.reciprocal(out=st[:, 14:15], in_=st[:, 12:13])
                    v.reciprocal(out=st[:, 15:16], in_=st[:, 13:14])
                    v.drain()
                    v.tensor_tensor(out=st[:, 16:17], in0=st[:, 15:16],
                                    in1=scr["lamB"][:, h:h + 1], op=ALU.mult)
                    v.drain()
                    cb = scr["e1"] if idx % 2 == 0 else scr["e2"]
                    v.tensor_scalar(out=scr["e2"][:, 0:vc],
                                    in0=scr["e2"][:, 0:vc], scalar1=st[:, 16:17],
                                    scalar2=None, op0=ALU.mult)
                    v.drain()
                    v.scalar_tensor_tensor(out=cb[:, 0:vc],
                                           in0=scr["e1"][:, 0:vc],
                                           scalar=st[:, 14:15],
                                           in1=scr["e2"][:, 0:vc],
                                           op0=ALU.mult,
                                           op1=ALU.subtract).then_inc(s_att, 1)
                    v.drain()
                oev(len(prs) - 1)

            @b.scalar
            def _(sc):
                for idx, (r, j) in enumerate(prs):
                    vc = (r + 1) * 128
                    sc.wait_ge(s_mx, ctx.cnt["a1_mx"] + 2 * idx + 1)
                    if idx >= 1:
                        sc.wait_ge(s_trD, ctx.cnt["a1_trD"] + idx * 16)
                    sc.activation(out=scr["e1"][:, 0:vc], in_=s1[:, 0:vc],
                                  func=AF.Exp, bias=st[:, 9:10], scale=0.125,
                                  accum_out=st[:, 12:13]).then_inc(s_pexp, 1)
                    sc.wait_ge(s_mx, ctx.cnt["a1_mx"] + 2 * idx + 2)
                    if vc <= 1536:
                        sc.activation(out=scr["e2"][:, 0:vc], in_=s2[:, 0:vc],
                                      func=AF.Exp, bias=st[:, 11:12], scale=0.125,
                                      accum_out=st[:, 13:14]).then_inc(s_pexp, 1)
                    else:
                        sc.activation(out=scr["e2"][:, 0:1536], in_=s2[:, 0:1536],
                                      func=AF.Exp, bias=st[:, 11:12], scale=0.125,
                                      accum_out=st[:, 13:14])
                        sc.activation(out=scr["e2"][:, 1536:vc], in_=s1[:, 1536:vc],
                                      func=AF.Exp, bias=st[:, 11:12], scale=0.125,
                                      accum_out=st[:, 21:22]).then_inc(s_pexp, 1)

            @b.sync
            def _(sy):
                for idx, (r, j) in enumerate(prs):
                    vc = (r + 1) * 128
                    sy.wait_ge(s_att, ctx.cnt["a1_att"] + idx + 1)
                    if idx >= 2:
                        sy.wait_ge(s_av, ctx.cnt["a1_av"] + idx - 1)
                    cb = scr["e1"] if idx % 2 == 0 else scr["e2"]
                    tb = scr["aT"] if idx % 2 == 0 else scr["aS"]
                    sy.dma_start(
                        out=tb[:].rearrange("p (c j) -> p c j", c=NT)[:, 0:r + 1, :],
                        in_=cb[:, 0:vc],
                        transpose=True).then_inc(s_trD, 16)

        ctx.cnt["a1_smm"] += 2 * G * NT
        ctx.cnt["a1_mx"] += 2 * G * NT
        ctx.cnt["a1_pexp"] += 2 * G * NT
        ctx.cnt["a1_att"] += G * NT
        ctx.cnt["a1_trD"] += G * NT * 16
        ctx.cnt["a1_av"] += G * NT
        ctx.cnt["a1_oev"] += G * NT

    with nc.Block() as b:
        @b.vector
        def _(v):
            v.tensor_scalar(out=scr["ssqB"][:], in0=scr["ssqB"][:], scalar1=1.0 / 64,
                            scalar2=1e-6, op0=ALU.mult, op1=ALU.add)

    with nc.Block() as b:
        @b.scalar
        def _(sc):
            sc.activation(out=scr["ssqB"][:], in_=scr["ssqB"][:], func=AF.Sqrt)

    with nc.Block() as b:
        @b.vector
        def _(v):
            v.reciprocal(out=scr["ssqB"][:], in_=scr["ssqB"][:])
            v.drain()
            for r in range(NT):
                for h in range(16):
                    osl = scr["o_tok"][:, r * 1024 + h * 64: r * 1024 + h * 64 + 64]
                    v.tensor_scalar(out=osl, in0=osl,
                                    scalar1=scr["ssqB"][:, r * 16 + h: r * 16 + h + 1],
                                    scalar2=None, op0=ALU.mult)
            v.drain()

    s_otr = ctx.sem("a1_otr")
    oT = scr["oT"]
    with nc.Block() as b:
        @b.sync
        def _(sy):
            for tt in range(NT):
                sy.dma_start(
                    out=oT[:].rearrange("p (k s) -> p k s", k=ND)[:, :, tt * 128:(tt + 1) * 128],
                    in_=scr["o_tok"][:, tt * 1024:(tt + 1) * 1024],
                    transpose=True).then_inc(s_otr, 16)
            sy.wait_ge(s_otr, ctx.cnt["a1_otr"] + NT * 16)

        @b.gpsimd
        def _(g):
            g.dma_start(out=u1[:], in_=aps["wo"][:, :]).then_inc(s_wld, 16)
            g.wait_ge(s_wld, ctx.cnt["a1_wld"] + 16)
    ctx.cnt["a1_otr"] += NT * 16
    ctx.cnt["a1_wld"] += 16

    with nc.Block() as b:
        @b.vector
        def _(v):
            for kt in range(ND):
                v.tensor_scalar(out=u1[:, kt * 1024:(kt + 1) * 1024],
                                in0=u1[:, kt * 1024:(kt + 1) * 1024],
                                scalar1=scr["hnB"][:, kt:kt + 1],
                                scalar2=0.8, op0=ALU.mult, op1=ALU.mult)
            v.drain()

    s_omm = ctx.sem("a1_omm")
    s_oac = ctx.sem("a1_oac")
    with nc.Block() as b:
        @b.tensor
        def _(t):
            pb = [s1[:, 0:512], s1[:, 512:1024], s2[:, 0:512], s2[:, 512:1024]]
            for tt in range(NT):
                for nn in range(2):
                    idx = tt * 2 + nn
                    if idx >= 4:
                        t.wait_ge(s_oac, ctx.cnt["a1_oac"] + idx - 3)
                    out = pb[idx % 4]
                    for kt in range(ND):
                        t.matmul(out, oT[:, kt * S + tt * 128: kt * S + (tt + 1) * 128],
                                 u1[:, kt * 1024 + nn * 512: kt * 1024 + (nn + 1) * 512],
                                 start=(kt == 0), stop=False)
                    ins = t.matmul(out, ones[0:1, 0:128],
                                   scr["boB"][0:1, nn * 512:(nn + 1) * 512],
                                   start=False, stop=True)
                    ins.then_inc(s_omm, 1)

        @b.vector
        def _(v):
            pb = [s1[:, 0:512], s1[:, 512:1024], s2[:, 0:512], s2[:, 512:1024]]
            for tt in range(NT):
                for nn in range(2):
                    idx = tt * 2 + nn
                    v.wait_ge(s_omm, ctx.cnt["a1_omm"] + idx + 1)
                    v.drain()
                    v.tensor_tensor(out=x_res[:, tt * D + nn * 512: tt * D + (nn + 1) * 512],
                                    in0=pb[idx % 4],
                                    in1=x_res[:, tt * D + nn * 512: tt * D + (nn + 1) * 512],
                                    op=ALU.add).then_inc(s_oac, 1)
                    v.drain()
    ctx.cnt["a1_omm"] += 32
    ctx.cnt["a1_oac"] += 32


def prep_l1attn(inp):
    wqT = inp["L1_attn_Wq"].T
    wkT = inp["L1_attn_Wk"].T
    wq = np.empty((8, 128, 2048), np.float32)
    wk = np.empty((8, 128, 2048), np.float32)
    for g in range(8):
        blk = wqT[:, g * 256:(g + 1) * 256]
        wq[g] = blk.reshape(8, 128, 256).transpose(1, 0, 2).reshape(128, 2048)
        blk = wkT[:, g * 256:(g + 1) * 256]
        wk[g] = blk.reshape(8, 128, 256).transpose(1, 0, 2).reshape(128, 2048)
    wv = inp["L1_attn_Wv"].T.reshape(8, 128, 1024).transpose(1, 0, 2).reshape(128, 8192)
    wo = inp["L1_attn_Wo"].T.reshape(8, 128, 1024).transpose(1, 0, 2).reshape(128, 8192)
    bq = np.ascontiguousarray(inp["L1_attn_bq"].reshape(16, 128).T)
    bk = np.ascontiguousarray(inp["L1_attn_bk"].reshape(16, 128).T)
    bv = inp["L1_attn_bv"].reshape(1, 1024)
    bo = inp["L1_attn_bo"].reshape(1, 1024)
    half = 32
    inv = 1.0 / 10000.0 ** (np.arange(half, dtype=np.float64) / half)
    ang = np.arange(S, dtype=np.float64)[:, None] * inv[None, :]
    cos = np.cos(ang).astype(np.float32).T
    sin = np.sin(ang).astype(np.float32).T
    C = np.empty((128, S), np.float32)
    Sg = np.empty((128, S), np.float32)
    for j in range(128):
        C[j] = cos[j % 32]
        Sg[j] = (-sin[j % 32]) if (j % 64) < 32 else sin[(j % 64) - 32]
    cst = np.zeros((128, 384), np.float32)
    cst[:, 0:128] = np.eye(128)
    m = np.zeros((128, 128), np.float32)
    iu = np.triu_indices(128, 1)
    m[iu] = -1e30
    cst[:, 128:256] = m
    pm = np.zeros((128, 128), np.float32)
    for i in range(128):
        j = i + 32 if (i % 64) < 32 else i - 32
        pm[i, j] = 1.0
    cst[:, 256:384] = pm
    ones = np.ones((1, 128), np.float32)
    lqk = np.concatenate([inp["L1_attn_lq1"], inp["L1_attn_lk1"],
                          inp["L1_attn_lq2"], inp["L1_attn_lk2"]], axis=1)
    hn = np.ascontiguousarray(inp["L1_attn_head_norm_w"].reshape(1024).reshape(8, 128).T)
    return {"wq": np.ascontiguousarray(wq.reshape(8 * 128, 2048)),
            "wk": np.ascontiguousarray(wk.reshape(8 * 128, 2048)),
            "wv": np.ascontiguousarray(wv), "wo": np.ascontiguousarray(wo),
            "bq": bq, "bk": bk, "bv": np.ascontiguousarray(bv),
            "bo": np.ascontiguousarray(bo),
            "C": C, "Sg": Sg, "cst": cst, "ones": ones,
            "lqk": np.ascontiguousarray(lqk), "hn": hn}



TC = 512
NTC = S // TC
H0, DK0 = 4, 256


def emit_ln_stream(ctx, xin, xhatT, scr, tag):
    nc = ctx.nc
    st = scr["st"]
    ld = ctx.sem("lns_ld_" + tag)
    hat = ctx.sem("lns_hat_" + tag)
    tr = ctx.sem("lns_tr_" + tag)
    sv = ctx.sem("lns_sv_" + tag)
    ss = ctx.sem("lns_ss_" + tag)
    xrot = scr["xrot"]
    b_ld = ctx.cnt["lns_ld_" + tag]
    b_hat = ctx.cnt["lns_hat_" + tag]
    b_tr = ctx.cnt["lns_tr_" + tag]
    b_sv = ctx.cnt["lns_sv_" + tag]
    b_ss = ctx.cnt["lns_ss_" + tag]

    with nc.Block() as b:
        @b.sync
        def _(sy):
            for t in range(NT):
                if t >= 2:
                    sy.wait_ge(sv, b_sv + t - 1)
                    sy.wait_ge(ss, b_ss + t - 1)
                sy.dma_start(out=xrot[:, (t % 2) * D:(t % 2 + 1) * D],
                             in_=xin[t * 128:(t + 1) * 128, :]).then_inc(ld, 16)

        @b.vector
        def _(v):
            for t in range(NT):
                v.wait_ge(ld, b_ld + (t + 1) * 16)
                v.tensor_reduce(out=st[:, t:t + 1], in_=xrot[:, (t % 2) * D:(t % 2 + 1) * D],
                                axis=AX.X, op=ALU.add)
                v.drain()
                v.nop().then_inc(sv, 1)

        @b.scalar
        def _(sc):
            for t in range(NT):
                sc.wait_ge(ld, b_ld + (t + 1) * 16)
                sc.activation(out=scr["sq"][:], in_=xrot[:, (t % 2) * D:(t % 2 + 1) * D],
                              func=AF.Square, accum_out=st[:, 16 + t:17 + t])
                sc.drain()
                sc.nop().then_inc(ss, 1)
    ctx.cnt["lns_ld_" + tag] += NT * 16
    ctx.cnt["lns_sv_" + tag] += NT
    ctx.cnt["lns_ss_" + tag] += NT
    b_ld = ctx.cnt["lns_ld_" + tag]

    with nc.Block() as b:
        @b.vector
        def _(v):
            v.tensor_scalar(out=st[:, 32:48], in0=st[:, 0:16], scalar1=1.0 / D,
                            scalar2=None, op0=ALU.mult)
            v.tensor_scalar(out=st[:, 48:64], in0=st[:, 16:32], scalar1=1.0 / D,
                            scalar2=None, op0=ALU.mult)
            v.drain()
            v.tensor_tensor(out=st[:, 64:80], in0=st[:, 32:48], in1=st[:, 32:48], op=ALU.mult)
            v.drain()
            v.tensor_tensor(out=st[:, 80:96], in0=st[:, 48:64], in1=st[:, 64:80], op=ALU.subtract)
            v.drain()
            v.tensor_scalar(out=st[:, 80:96], in0=st[:, 80:96], scalar1=1e-5,
                            scalar2=None, op0=ALU.add)
            v.tensor_scalar(out=st[:, 0:16], in0=st[:, 32:48], scalar1=-1.0,
                            scalar2=None, op0=ALU.mult)

    with nc.Block() as b:
        @b.scalar
        def _(sc):
            sc.activation(out=st[:, 48:64], in_=st[:, 80:96], func=AF.Sqrt)

    with nc.Block() as b:
        @b.sync
        def _(sy):
            sy.dma_start(out=xrot[:, 0:D], in_=xin[0:128, :]).then_inc(ld, 16)
            sy.dma_start(out=xrot[:, D:2 * D], in_=xin[128:256, :]).then_inc(ld, 16)
            for t in range(NT):
                sy.wait_ge(hat, b_hat + t + 1)
                sy.dma_start(
                    out=xhatT[:].rearrange("p (k s) -> p k s", k=ND)[:, :, t * 128:(t + 1) * 128],
                    in_=scr["xhat2"][:, (t % 2) * D:(t % 2 + 1) * D],
                    transpose=True).then_inc(tr, 16)
                if t + 2 < NT:
                    sy.dma_start(out=xrot[:, ((t + 2) % 2) * D:((t + 2) % 2 + 1) * D],
                                 in_=xin[(t + 2) * 128:(t + 3) * 128, :]).then_inc(ld, 16)
            sy.wait_ge(tr, b_tr + NT * 16)

        @b.vector
        def _(v):
            v.reciprocal(out=st[:, 16:32], in_=st[:, 48:64])
            v.drain()
            for t in range(NT):
                v.wait_ge(ld, b_ld + (t + 1) * 16)
                if t >= 2:
                    v.wait_ge(tr, b_tr + (t - 1) * 16)
                v.tensor_scalar(out=scr["xhat2"][:, (t % 2) * D:(t % 2 + 1) * D],
                                in0=xrot[:, (t % 2) * D:(t % 2 + 1) * D],
                                scalar1=st[:, t:t + 1], scalar2=st[:, 16 + t:17 + t],
                                op0=ALU.add, op1=ALU.mult)
                v.drain()
                v.nop().then_inc(hat, 1)
    ctx.cnt["lns_ld_" + tag] += NT * 16
    ctx.cnt["lns_hat_" + tag] += NT
    ctx.cnt["lns_tr_" + tag] += NT * 16


def emit_l0attn(ctx, xin, xout, xhatT, aps, scr):
    nc = ctx.nc
    st = scr["st"]
    ones = scr["ones"]
    s_ld = ctx.sem("a0_ld")
    s_d1 = ctx.sem("a0_d1")
    ps_d = scr["ps_d"]

    with nc.Block() as b:
        @b.gpsimd
        def _(g):
            for key, dst in [("cst0", scr["cst0"][:]), ("ones", scr["ones"][:]),
                             ("qcw", scr["qcw"][:]), ("kcw", scr["kcw"][:]),
                             ("vcw", scr["vcw"][:]), ("firw", scr["firw"][:]),
                             ("w2", scr["w2B"][:]), ("b1", scr["b1B"][:]),
                             ("b2", scr["b2B"][:]), ("bpw", scr["bpW"][:]),
                             ("idb", scr["idbB"][:]), ("ltau", scr["ltau"][:]),
                             ("onw", scr["onwB"][:])]:
                g.dma_start(out=dst, in_=aps[key][:, :]).then_inc(s_ld, 16)
            g.memset(scr["S32"][:], 0)
            g.memset(scr["S_bf"][:], 0)
            g.memset(scr["histq"][:], 0)
            g.memset(scr["histk"][:], 0)
            g.memset(scr["histv"][:], 0)
            g.memset(scr["vc"][:], 0)
            g.wait_ge(s_ld, ctx.cnt["a0_ld"] + 13 * 16)
    ctx.cnt["a0_ld"] += 13 * 16

    with nc.Block() as b:
        @b.scalar
        def _(sc):
            sc.activation(out=st[0:1, 0:4], in_=scr["ltau"][0:1, 0:4], func=AF.Exp)

    with nc.Block() as b:
        @b.vector
        def _(v):
            v.tensor_scalar(out=st[0:1, 0:4], in0=st[0:1, 0:4], scalar1=0.5,
                            scalar2=None, op0=ALU.max)
            v.drain()
            v.reciprocal(out=st[0:1, 4:8], in_=st[0:1, 0:4])
            v.drain()
            v.tensor_copy(out=scr["rowb"][0:1, 0:4], in_=st[0:1, 4:8])
            v.drain()

    with nc.Block() as b:
        @b.tensor
        def _(t):
            t.matmul(ps_d[:, 0:4], ones[0:1, 0:128], scr["rowb"][0:1, 0:4],
                     start=True, stop=True).then_inc(s_d1, 1)

        @b.vector
        def _(v):
            v.wait_ge(s_d1, ctx.cnt["a0_d1"] + 1)
            v.tensor_copy(out=scr["invtauB"][:], in_=ps_d[:, 0:4])
    ctx.cnt["a0_d1"] += 1

    for tcid in range(NTC):
        emit_l0attn_tc(ctx, xin, xout, xhatT, aps, scr, tcid)


def emit_l0attn_tc(ctx, xin, xout, xhatT, aps, scr, tcid):
    nc = ctx.nc
    st = scr["st"]
    c0 = tcid * TC
    I128 = scr["cst0"][:, 0:128]
    maskLn = scr["cst0"][:, 128:256]
    maskUn = scr["cst0"][:, 256:384]
    maskUD = scr["cst0"][:, 384:512]
    ones = scr["ones"]
    pb0, pb1 = scr["ps_b0"], scr["ps_b1"]
    ps_a, ps_d, ps_f, ps_su = scr["ps_a"], scr["ps_d"], scr["ps_f"], scr["ps_su"]
    wbuf = scr["wbuf"]
    s_w = ctx.sem("a0_w"); s_pm = ctx.sem("a0_pm"); s_pe = ctx.sem("a0_pe")
    s_cv = ctx.sem("a0_cv"); s_tr = ctx.sem("a0_tr")
    s_d1 = ctx.sem("a0_d1"); s_d2 = ctx.sem("a0_d2"); s_d3 = ctx.sem("a0_d3")
    s_r1 = ctx.sem("a0_r1"); s_r2 = ctx.sem("a0_r2"); s_r3 = ctx.sem("a0_r3")
    s_ox = ctx.sem("a0_ox")

    tok_bufs = [scr["q_tok"], scr["k_tok"], scr["v_tok0"]]
    for ti in range(3):
        wkey = ["wqa", "wka", "wva"][ti]
        cw = [scr["qcw"], scr["kcw"], scr["vcw"]][ti]
        hist = [scr["histq"], scr["histk"], scr["histv"]][ti]
        raw = scr["rawfull"]
        conv = scr["convfull"]
        with nc.Block() as b:
            @b.gpsimd
            def _(g):
                g.dma_start(out=wbuf[:, 0:4096],
                            in_=aps[wkey][:, 0:4096]).then_inc(s_w, 16)
                g.dma_start(out=wbuf[:, 4096:8192],
                            in_=aps[wkey][:, 4096:8192]).then_inc(s_w, 16)

            @b.tensor
            def _(t):
                for m in range(8):
                    t.wait_ge(s_w, ctx.cnt["a0_w"] + (16 if m < 4 else 32))
                    if m >= 2:
                        t.wait_ge(s_pe, ctx.cnt["a0_pe"] + m - 1)
                    ps = pb0 if m % 2 == 0 else pb1
                    for kt in range(ND):
                        ins = t.matmul(ps[:, 0:512],
                                       wbuf[:, m * 1024 + kt * 128: m * 1024 + (kt + 1) * 128],
                                       xhatT[:, kt * S + c0: kt * S + c0 + TC],
                                       start=(kt == 0), stop=(kt == ND - 1))
                    ins.then_inc(s_pm, 1)

            @b.scalar
            def _(sc):
                for m in range(8):
                    sc.wait_ge(s_pm, ctx.cnt["a0_pm"] + m + 1)
                    sc.activation(out=raw[:, m * 544 + 32: m * 544 + 544],
                                  in_=(pb0 if m % 2 == 0 else pb1)[:, 0:512],
                                  func=AF.Identity).then_inc(s_pe, 1)

            @b.vector
            def _(v):
                for m in range(8):
                    v.tensor_copy(out=raw[:, m * 544: m * 544 + 32],
                                  in_=hist[:, m * 32:(m + 1) * 32])
                if ti == 2:
                    for m in range(8):
                        v.tensor_copy(out=scr["rs0"][:, 0:32],
                                      in_=scr["vc"][:, m * 544 + 512: m * 544 + 544])
                        v.drain()
                        v.tensor_copy(out=scr["vc"][:, m * 544: m * 544 + 32],
                                      in_=scr["rs0"][:, 0:32])
                        v.drain()
                v.drain()
                for m in range(8):
                    v.wait_ge(s_pe, ctx.cnt["a0_pe"] + m + 1)
                    v.tensor_copy(out=hist[:, m * 32:(m + 1) * 32],
                                  in_=raw[:, m * 544 + 512: m * 544 + 544])
                    v.drain()
                    dst = conv[:, m * 512:(m + 1) * 512]
                    for tau in range(4):
                        src = raw[:, m * 544 + 32 - 3 + tau: m * 544 + 32 - 3 + tau + 512]
                        if tau == 0:
                            v.tensor_scalar(out=dst, in0=src,
                                            scalar1=cw[:, m * 4:m * 4 + 1],
                                            scalar2=None, op0=ALU.mult)
                        else:
                            v.scalar_tensor_tensor(out=dst, in0=src,
                                                   scalar=cw[:, m * 4 + tau:m * 4 + tau + 1],
                                                   in1=dst, op0=ALU.mult, op1=ALU.add)
                        v.drain()
                    v.nop().then_inc(s_cv, 1)
        ctx.cnt["a0_w"] += 32
        ctx.cnt["a0_pm"] += 8
        ctx.cnt["a0_pe"] += 8
        ctx.cnt["a0_cv"] += 8

        with nc.Block() as b:
            @b.scalar
            def _(sc):
                for m in range(8):
                    if ti < 2:
                        sc.activation(out=conv[:, m * 512:(m + 1) * 512],
                                      in_=conv[:, m * 512:(m + 1) * 512],
                                      func=AF.Silu)
                    else:
                        sc.activation(out=scr["vc"][:, m * 544 + 32: m * 544 + 544],
                                      in_=conv[:, m * 512:(m + 1) * 512],
                                      func=AF.Silu)
                    sc.drain()
                    sc.nop().then_inc(s_cv, 1)

        ctx.cnt["a0_cv"] += 8
        with nc.Block() as b:
            @b.sync
            def _(sy):
                for m in range(8):
                    src = (conv[:, m * 512:(m + 1) * 512] if ti < 2
                           else scr["vc"][:, m * 544 + 32: m * 544 + 544])
                    sy.dma_start(
                        out=tok_bufs[ti][:].rearrange("p (t c) -> p t c", t=4)[:, :, m * 128:(m + 1) * 128],
                        in_=src, transpose=True).then_inc(s_tr, 16)
                sy.wait_ge(s_tr, ctx.cnt["a0_tr"] + 8 * 16)
        ctx.cnt["a0_tr"] += 8 * 16

    with nc.Block() as b:
        @b.tensor
        def _(t):
            for tt in range(4):
                for kt in range(ND):
                    t.matmul(ps_d[:, tt * 8:(tt + 1) * 8],
                             xhatT[:, kt * S + c0 + tt * 128: kt * S + c0 + (tt + 1) * 128],
                             scr["bpW"][:, kt * 8:(kt + 1) * 8],
                             start=(kt == 0), stop=False)
                ins = t.matmul(ps_d[:, tt * 8:(tt + 1) * 8], ones[0:1, 0:128],
                               scr["idbB"][0:1, 0:8], start=False, stop=True)
            ins.then_inc(s_pm, 1)

        @b.scalar
        def _(sc):
            sc.wait_ge(s_pm, ctx.cnt["a0_pm"] + 1)
            sc.activation(out=scr["bp"][:, 0:32], in_=ps_d[:, 0:32], func=AF.Sigmoid)
    ctx.cnt["a0_pm"] += 1

    with nc.Block() as b:
        @b.vector
        def _(v):
            for tt in range(4):
                for qk in range(2):
                    src = tok_bufs[qk][:, tt * 1024:(tt + 1) * 1024]
                    v.tensor_tensor(out=scr["scrot"][:, 0:1024], in0=src, in1=src,
                                    op=ALU.mult)
                    v.drain()
                    dstc = scr["ecol"] if qk == 0 else scr["dcol"]
                    v.tensor_reduce(out=dstc[:, tt * 4:(tt + 1) * 4],
                                    in_=scr["scrot"][:, 0:1024].rearrange("p (h c) -> p h c", h=4),
                                    axis=AX.X, op=ALU.add)
                    v.drain()
            v.tensor_scalar(out=scr["ecol"][:], in0=scr["ecol"][:], scalar1=1e-6,
                            scalar2=None, op0=ALU.add)
            v.tensor_scalar(out=scr["dcol"][:], in0=scr["dcol"][:], scalar1=1e-6,
                            scalar2=None, op0=ALU.add)

    with nc.Block() as b:
        @b.scalar
        def _(sc):
            sc.activation(out=scr["ecol"][:], in_=scr["ecol"][:], func=AF.Sqrt)
            sc.activation(out=scr["dcol"][:], in_=scr["dcol"][:], func=AF.Sqrt)

    with nc.Block() as b:
        @b.vector
        def _(v):
            v.reciprocal(out=scr["ecol"][:], in_=scr["ecol"][:])
            v.reciprocal(out=scr["dcol"][:], in_=scr["dcol"][:])
            v.drain()
            for tt in range(4):
                for h in range(4):
                    sl = slice(tt * 1024 + h * 256, tt * 1024 + (h + 1) * 256)
                    v.tensor_scalar(out=scr["scrot"][:, h * 256:(h + 1) * 256],
                                    in0=scr["q_tok"][:, sl.start:sl.stop],
                                    scalar1=scr["ecol"][:, tt * 4 + h:tt * 4 + h + 1],
                                    scalar2=None, op0=ALU.mult)
                    v.tensor_scalar(out=scr["kn_tok"][:, sl.start:sl.stop],
                                    in0=scr["k_tok"][:, sl.start:sl.stop],
                                    scalar1=scr["dcol"][:, tt * 4 + h:tt * 4 + h + 1],
                                    scalar2=None, op0=ALU.mult)
                v.drain()
                for h in range(4):
                    v.tensor_scalar(out=scr["scrot"][:, 1024 + h * 256: 1024 + (h + 1) * 256],
                                    in0=scr["kn_tok"][:, tt * 1024 + h * 256: tt * 1024 + (h + 1) * 256],
                                    scalar1=scr["bp"][:, tt * 8 + h: tt * 8 + h + 1],
                                    scalar2=None, op0=ALU.mult)
                v.drain()
                v.nop().then_inc(s_cv, 1)

        @b.sync
        def _(sy):
            for tt in range(4):
                sy.wait_ge(s_cv, ctx.cnt["a0_cv"] + tt + 1)
                sy.dma_start(
                    out=scr["qnT"][:].rearrange("p (f c) -> p f c", f=8)[:, :, tt * 128:(tt + 1) * 128],
                    in_=scr["scrot"][:, 0:1024], transpose=True).then_inc(s_tr, 16)
                sy.dma_start(
                    out=scr["knT"][:].rearrange("p (f c) -> p f c", f=8)[:, :, tt * 128:(tt + 1) * 128],
                    in_=scr["kn_tok"][:, tt * 1024:(tt + 1) * 1024],
                    transpose=True).then_inc(s_tr, 16)
                sy.dma_start(
                    out=scr["kbnT"][:].rearrange("p (f c) -> p f c", f=8)[:, :, tt * 128:(tt + 1) * 128],
                    in_=scr["scrot"][:, 1024:2048], transpose=True).then_inc(s_tr, 16)
            sy.wait_ge(s_tr, ctx.cnt["a0_tr"] + 12 * 16)
    ctx.cnt["a0_cv"] += 4
    ctx.cnt["a0_tr"] += 12 * 16

    KTAPS = [(0, 3, 0), (1, 7, 3), (2, 15, 10), (3, 31, 25)]
    fir_ops = []
    for (p, K, po) in KTAPS:
        for m in range(8):
            for tau in range(K):
                fir_ops.append((p, m, K, po, tau))

    def emit_fir_op(v, op):
        p, m, K, po, tau = op
        dst = scr["firb"][p][:, m * 512:(m + 1) * 512]
        src = scr["vc"][:, m * 544 + 32 - (K - 1) + tau: m * 544 + 32 - (K - 1) + tau + 512]
        wcol = scr["firw"][:, m * 56 + po + tau: m * 56 + po + tau + 1]
        if tau == 0:
            v.tensor_scalar(out=dst, in0=src, scalar1=wcol,
                            scalar2=None, op0=ALU.mult)
        else:
            v.scalar_tensor_tensor(out=dst, in0=src, scalar=wcol,
                                   in1=dst, op0=ALU.mult, op1=ALU.add)
        v.drain()

    cum1 = ctx.cnt["a0_d1"]
    cum2 = ctx.cnt["a0_d2"]
    cum3 = ctx.cnt["a0_d3"]
    cumT = ctx.cnt["a0_tr"]
    if "a0_smark" not in ctx.__dict__:
        ctx.a0_smark = {h: 0 for h in range(4)}
        ctx.a0_y0mark = 0
        ctx.a0_wTmark = 0
    pairs = [(i, h) for i in range(4) for h in range(4)]
    sched = []
    for (i, h) in pairs:
        sc_ = {"i": i, "h": h,
               "b1": cum1, "b2": cum2, "b3": cum3, "bT": cumT,
               "smark": ctx.a0_smark[h], "y0mark": ctx.a0_y0mark,
               "wTmark": ctx.a0_wTmark}
        sched.append(sc_)
        ctx.a0_y0mark = cum1 + 2
        ctx.a0_wTmark = cum1 + 15
        ctx.a0_smark[h] = cum2 + 10
        cum1 += 17
        cum2 += 10
        cum3 += 9
        cumT += 16
    ctx.cnt["a0_d1"] = cum1
    ctx.cnt["a0_d2"] = cum2
    ctx.cnt["a0_d3"] = cum3
    ctx.cnt["a0_tr"] = cumT

    s_d1h, s_d2h, s_d3h, s_trh = s_d1, s_d2, s_d3, s_tr
    delta_tok = scr["convfull"]
    with nc.Block() as b:
        @b.tensor
        def _(t):
            for sc_ in sched:
                i, h = sc_["i"], sc_["h"]
                b1, b2, b3, bT = sc_["b1"], sc_["b2"], sc_["b3"], sc_["bT"]
                f0, f1 = 2 * h, 2 * h + 1
                isl = slice(i * 128, (i + 1) * 128)

                def fmT(buf, f):
                    return buf[:, f * 512 + i * 128: f * 512 + (i + 1) * 128]
                t.wait_ge(s_d2h, b2)
                t.matmul(ps_a[:, 0:128], fmT(scr["kbnT"], f0), fmT(scr["knT"], f0),
                         start=True, stop=False)
                t.matmul(ps_a[:, 0:128], fmT(scr["kbnT"], f1), fmT(scr["knT"], f1),
                         start=False, stop=True)
                t.matmul(ps_a[:, 128:256], fmT(scr["knT"], f0), fmT(scr["kbnT"], f0),
                         start=True, stop=False)
                t.matmul(ps_a[:, 128:256], fmT(scr["knT"], f1), fmT(scr["kbnT"], f1),
                         start=False, stop=True)
                t.matmul(ps_a[:, 256:384], fmT(scr["knT"], f0), fmT(scr["qnT"], f0),
                         start=True, stop=False)
                ins = t.matmul(ps_a[:, 256:384], fmT(scr["knT"], f1), fmT(scr["qnT"], f1),
                               start=False, stop=True)
                ins.then_inc(s_d1h, 1)
                Pslot = [(scr["Pp"][:, 0:128], scr["Pp"][:, 128:256]),
                         (scr["Pp"][:, 256:384], scr["Pp"][:, 384:512])]
                tslot = [scr["tA"], scr["tB"]]
                for j in range(7):
                    rhs = scr["y0"][:] if j == 0 else tslot[(j - 1) % 2][:]
                    PT = Pslot[j % 2][1]
                    if j == 0:
                        t.wait_ge(s_d2h, b2 + 2)
                    else:
                        t.wait_ge(s_d3h, b3 + j)
                        t.wait_ge(s_d2h, b2 + 2 + j)
                    t.matmul(ps_f[:, 0:512] if j % 2 == 0 else ps_su[:, 0:512],
                             PT, rhs, start=True, stop=False)
                    ins = t.matmul(ps_f[:, 0:512] if j % 2 == 0 else ps_su[:, 0:512],
                                   I128, rhs, start=False, stop=True)
                    ins.then_inc(s_d1h, 1)
                    if j < 6:
                        jd = j + 1
                        P_prev, PT_prev = Pslot[j % 2]
                        dsl = ps_d[:, 0:256] if jd % 2 == 1 else pb0[:, 0:256]
                        t.matmul(dsl[:, 0:128], PT_prev, P_prev, start=True, stop=True)
                        ins = t.matmul(dsl[:, 128:256], P_prev, PT_prev, start=True, stop=True)
                        ins.then_inc(s_d1h, 1)
                t.wait_ge(s_trh, bT + 16)
                t.wait_ge(s_d2h, sc_["smark"])
                t.wait_ge(s_d3h, b3 + 7)
                t.matmul(ps_su[:, 0:256], scr["wnegT"][:, 0:128],
                         scr["S_bf"][:, h * 512:h * 512 + 256], start=True, stop=False)
                t.matmul(ps_su[:, 0:256], scr["wnegT"][:, 128:256],
                         scr["S_bf"][:, h * 512 + 256:h * 512 + 512], start=False, stop=False)
                ins = t.matmul(ps_su[:, 0:256], I128, scr["u_sb"][:],
                               start=False, stop=True)
                ins.then_inc(s_d1h, 1)
                t.wait_ge(s_d3h, b3 + 8)
                t.matmul(ps_f[:, 0:256], fmT(scr["qnT"], f0),
                         scr["S_bf"][:, h * 512:h * 512 + 256], start=True, stop=False)
                t.matmul(ps_f[:, 0:256], fmT(scr["qnT"], f1),
                         scr["S_bf"][:, h * 512 + 256:h * 512 + 512], start=False, stop=False)
                ins = t.matmul(ps_f[:, 0:256], scr["attT"][:], scr["uadj"][:],
                               start=False, stop=True)
                ins.then_inc(s_d1h, 1)
                t.wait_ge(s_d2h, b2 + 9)
                ksl = scr["kn_tok"][:, i * 1024 + h * 256: i * 1024 + (h + 1) * 256]
                t.matmul(ps_a[:, 0:256], ksl[:, 0:128], scr["uadj2"][:],
                         start=True, stop=True)
                ins = t.matmul(ps_a[:, 256:512], ksl[:, 128:256], scr["uadj2"][:],
                               start=True, stop=True)
                ins.then_inc(s_d1h, 1)

        @b.vector
        def _(v):
            fir_iter = iter(fir_ops)
            per_pair = (len(fir_ops) + len(sched) - 1) // len(sched)
            for pi, sc_ in enumerate(sched):
                for _ in range(per_pair):
                    op = next(fir_iter, None)
                    if op is not None:
                        emit_fir_op(v, op)
                i, h = sc_["i"], sc_["h"]
                b1, b2, b3 = sc_["b1"], sc_["b2"], sc_["b3"]
                v.wait_ge(s_d1h, sc_["y0mark"])
                vsl = scr["v_tok0"][:, i * 1024 + h * 256: i * 1024 + (h + 1) * 256]
                ksl = scr["kn_tok"][:, i * 1024 + h * 256: i * 1024 + (h + 1) * 256]
                bcol = scr["bp"][:, i * 8 + h: i * 8 + h + 1]
                v.tensor_scalar(out=scr["y0"][:, 0:256], in0=vsl, scalar1=bcol,
                                scalar2=None, op0=ALU.mult)
                v.tensor_scalar(out=scr["y0"][:, 256:512], in0=ksl, scalar1=bcol,
                                scalar2=None, op0=ALU.mult)
                v.drain()
                v.nop().then_inc(s_d2h, 1)
                v.wait_ge(s_d1h, b1 + 1)
                v.tensor_tensor(out=scr["Pp"][:, 0:128], in0=ps_a[:, 0:128],
                                in1=maskLn, op=ALU.mult)
                v.tensor_tensor(out=scr["Pp"][:, 128:256], in0=ps_a[:, 128:256],
                                in1=maskUn, op=ALU.mult)
                v.tensor_tensor(out=scr["attT"][:], in0=ps_a[:, 256:384],
                                in1=maskUD, op=ALU.mult)
                v.drain()
                v.nop().then_inc(s_d2h, 1)
                for jd in range(1, 7):
                    v.wait_ge(s_d1h, b1 + 2 * jd + 1)
                    if jd >= 2:
                        v.wait_ge(s_d1h, b1 + 2 * (jd - 1) + 1)
                    dsl = ps_d[:, 0:256] if jd % 2 == 1 else pb0[:, 0:256]
                    slot = jd % 2
                    v.tensor_copy(out=scr["Pp"][:, slot * 256: slot * 256 + 128],
                                  in_=dsl[:, 0:128])
                    v.tensor_copy(out=scr["Pp"][:, slot * 256 + 128: slot * 256 + 256],
                                  in_=dsl[:, 128:256])
                    v.drain()
                    v.nop().then_inc(s_d2h, 1)
                v.wait_ge(s_d3h, b3 + 8)
                v.tensor_scalar(out=scr["uadj2"][:], in0=scr["uadj"][:],
                                scalar1=scr["dcol"][:, i * 4 + h: i * 4 + h + 1],
                                scalar2=None, op0=ALU.mult)
                v.drain()
                v.nop().then_inc(s_d2h, 1)
                v.wait_ge(s_d1h, b1 + 17)
                v.tensor_tensor(out=scr["S32"][:, h * 512:(h + 1) * 512],
                                in0=ps_a[:, 0:512],
                                in1=scr["S32"][:, h * 512:(h + 1) * 512],
                                op=ALU.add)
                v.drain()
                v.tensor_copy(out=scr["S_bf"][:, h * 512:(h + 1) * 512],
                              in_=scr["S32"][:, h * 512:(h + 1) * 512])
                v.drain()
                v.nop().then_inc(s_d2h, 1)

        @b.scalar
        def _(sc):
            for sc_ in sched:
                i, h = sc_["i"], sc_["h"]
                b1, b3 = sc_["b1"], sc_["b3"]
                tslot = [scr["tA"], scr["tB"]]
                for j in range(7):
                    sc.wait_ge(s_d1h, b1 + (2 if j == 0 else 2 * j + 2))
                    src = ps_f[:, 0:512] if j % 2 == 0 else ps_su[:, 0:512]
                    if j < 6:
                        sc.activation(out=tslot[j % 2][:], in_=src,
                                      func=AF.Identity).then_inc(s_d3h, 1)
                    else:
                        sc.activation(out=scr["u_sb"][:], in_=src[:, 0:256],
                                      func=AF.Identity)
                        sc.activation(out=scr["w_sb"][:], in_=src[:, 256:512],
                                      func=AF.Identity, scale=-1.0).then_inc(s_d3h, 1)
                sc.wait_ge(s_d1h, b1 + 15)
                sc.activation(out=scr["uadj"][:], in_=ps_su[:, 0:256],
                              func=AF.Identity).then_inc(s_d3h, 1)
                sc.wait_ge(s_d1h, b1 + 16)
                sc.activation(out=delta_tok[:, i * 1024 + h * 256: i * 1024 + (h + 1) * 256],
                              in_=ps_f[:, 0:256], func=AF.Identity).then_inc(s_d3h, 1)

        @b.sync
        def _(sy):
            for sc_ in sched:
                sy.wait_ge(s_d3h, sc_["b3"] + 7)
                if sc_["wTmark"] > 0:
                    sy.wait_ge(s_d1h, sc_["wTmark"])
                sy.dma_start(out=scr["wnegT"][:].rearrange("p (f c) -> p f c", f=2),
                             in_=scr["w_sb"][:], transpose=True).then_inc(s_trh, 16)

    rin = []
    for kt in range(8):
        rin.append(xhatT[:, kt * S + c0: kt * S + c0 + TC])
    for p in range(4):
        for kt in range(8):
            rin.append(scr["firb"][p][:, kt * 512:(kt + 1) * 512])
    deltaT = scr["rawfull"]
    for kt in range(8):
        rin.append(deltaT[:, kt * 512:(kt + 1) * 512])

    with nc.Block() as b:
        @b.sync
        def _(sy):
            for tt in range(4):
                sy.dma_start(
                    out=deltaT[:].rearrange("p (f c) -> p f c", f=8)[:, :, tt * 128:(tt + 1) * 128],
                    in_=delta_tok[:, tt * 1024:(tt + 1) * 1024],
                    transpose=True).then_inc(s_tr, 16)
            sy.wait_ge(s_tr, ctx.cnt["a0_tr"] + 4 * 16)
    ctx.cnt["a0_tr"] += 4 * 16

    b_w = ctx.cnt["a0_w"]
    b_r1 = ctx.cnt["a0_r1"]
    b_r2 = ctx.cnt["a0_r2"]
    b_r3 = ctx.cnt["a0_r3"]
    with nc.Block() as b:
        @b.gpsimd
        def _(g):
            for m in range(16):
                for hf in range(2):
                    idx = m * 2 + hf
                    if idx >= 2:
                        g.wait_ge(s_r1, b_r1 + idx - 1)
                    g.dma_start(out=wbuf[:, hf * 3072:(hf + 1) * 3072],
                                in_=aps["w1"][(idx) * 128:(idx + 1) * 128, :]
                                ).then_inc(s_w, 16)

        @b.tensor
        def _(t):
            for m in range(16):
                ps = pb0 if m % 2 == 0 else pb1
                if m >= 2:
                    t.wait_ge(s_r2, b_r2 + m - 1)
                for hf in range(2):
                    idx = m * 2 + hf
                    t.wait_ge(s_w, b_w + (idx + 1) * 16)
                    for ktl in range(24):
                        ins = t.matmul(ps[:, 0:512],
                                       wbuf[:, hf * 3072 + ktl * 128: hf * 3072 + (ktl + 1) * 128],
                                       rin[hf * 24 + ktl],
                                       start=(hf == 0 and ktl == 0),
                                       stop=(hf == 1 and ktl == 23))
                    ins.then_inc(s_r1, 1)
                if m >= 1:
                    mm = m - 1
                    t.wait_ge(s_r2, b_r2 + mm + 1)
                    lgb = [ps_d, ps_f, ps_su, ps_a]
                    for tsub in range(4):
                        ins = t.matmul(lgb[tsub][:, 0:20],
                                       scr["hmid"][:, (mm % 2) * 512 + tsub * 128: (mm % 2) * 512 + (tsub + 1) * 128],
                                       scr["w2B"][:, mm * 20:(mm + 1) * 20],
                                       start=(mm == 0), stop=False)
                    ins.then_inc(s_r3, 1)
            t.wait_ge(s_r2, b_r2 + 16)
            lgb = [ps_d, ps_f, ps_su, ps_a]
            for tsub in range(4):
                t.matmul(lgb[tsub][:, 0:20],
                         scr["hmid"][:, (15 % 2) * 512 + tsub * 128: (15 % 2) * 512 + (tsub + 1) * 128],
                         scr["w2B"][:, 15 * 20:16 * 20],
                         start=False, stop=False)
                ins = t.matmul(lgb[tsub][:, 0:20], ones[0:1, 0:128],
                               scr["b2B"][0:1, 0:20], start=False, stop=True)
            ins.then_inc(s_r3, 1)

        @b.scalar
        def _(sc):
            for m in range(16):
                sc.wait_ge(s_r1, b_r1 + 2 * m + 2)
                if m >= 2:
                    sc.wait_ge(s_r3, b_r3 + m - 1)
                ps = pb0 if m % 2 == 0 else pb1
                sc.activation(out=scr["hmid"][:, (m % 2) * 512:(m % 2 + 1) * 512],
                              in_=ps[:, 0:512], func=AF.Gelu,
                              bias=scr["b1B"][:, m:m + 1]).then_inc(s_r2, 1)
    ctx.cnt["a0_w"] += 32 * 16
    ctx.cnt["a0_r1"] += 32
    ctx.cnt["a0_r2"] += 16
    ctx.cnt["a0_r3"] += 16

    lgb = [ps_d, ps_f, ps_su, ps_a]
    with nc.Block() as b:
        @b.vector
        def _(v):
            for tsub in range(4):
                v.tensor_copy(out=scr["probs"][:, tsub * 20:(tsub + 1) * 20],
                              in_=lgb[tsub][:, 0:20])
            v.drain()
            for tsub in range(4):
                for h in range(4):
                    sl = scr["probs"][:, tsub * 20 + h * 5: tsub * 20 + (h + 1) * 5]
                    v.tensor_scalar(out=sl, in0=sl, scalar1=scr["invtauB"][:, h:h + 1],
                                    scalar2=None, op0=ALU.mult)
            v.drain()
            for tsub in range(4):
                v.tensor_reduce(out=st[:, tsub * 4:(tsub + 1) * 4],
                                in_=scr["probs"][:, tsub * 20:(tsub + 1) * 20].rearrange("p (h f) -> p h f", h=4),
                                axis=AX.X, op=ALU.max)
            v.drain()
            v.tensor_scalar(out=st[:, 0:16], in0=st[:, 0:16], scalar1=-1.0,
                            scalar2=None, op0=ALU.mult)
            v.drain()
            for tsub in range(4):
                for h in range(4):
                    sl = scr["probs"][:, tsub * 20 + h * 5: tsub * 20 + (h + 1) * 5]
                    v.tensor_scalar(out=sl, in0=sl, scalar1=st[:, tsub * 4 + h:tsub * 4 + h + 1],
                                    scalar2=None, op0=ALU.add)
            v.drain()

    with nc.Block() as b:
        @b.scalar
        def _(sc):
            sc.activation(out=scr["probs"][:, 0:80], in_=scr["probs"][:, 0:80], func=AF.Exp)

    with nc.Block() as b:
        @b.vector
        def _(v):
            for tsub in range(4):
                v.tensor_reduce(out=st[:, tsub * 4:(tsub + 1) * 4],
                                in_=scr["probs"][:, tsub * 20:(tsub + 1) * 20].rearrange("p (h f) -> p h f", h=4),
                                axis=AX.X, op=ALU.add)
            v.drain()
            v.reciprocal(out=st[:, 0:16], in_=st[:, 0:16])
            v.drain()
            for tsub in range(4):
                for h in range(4):
                    sl = scr["probs"][:, tsub * 20 + h * 5: tsub * 20 + (h + 1) * 5]
                    v.tensor_scalar(out=sl, in0=sl, scalar1=st[:, tsub * 4 + h:tsub * 4 + h + 1],
                                    scalar2=None, op0=ALU.mult)
            v.drain()
            bp3 = scr["bp"][:, 0:32].rearrange("p (t g) -> p t g", t=4)[:, :, 4:8]
            st3 = st[:, 16:32].rearrange("p (t g) -> p t g", t=4)
            v.tensor_scalar(out=st3, in0=bp3,
                            scalar1=0.09, scalar2=0.9, op0=ALU.mult, op1=ALU.add)
            v.drain()
            v.tensor_tensor(out=st3, in0=st3, in1=bp3, op=ALU.mult)
            v.drain()
            v.tensor_scalar(out=st[:, 32:48], in0=st[:, 16:32], scalar1=-1.0,
                            scalar2=1.0, op0=ALU.mult, op1=ALU.add)
            v.drain()

    tokfir = [scr["firb"][1], scr["firb"][2], scr["firb"][3], scr["rawfull"]]
    with nc.Block() as b:
        @b.sync
        def _(sy):
            order = [3, 2, 1, 0]
            for oi, p in enumerate(order):
                if oi >= 1:
                    sy.wait_ge(s_tr, ctx.cnt["a0_tr"] + oi * 8 * 16)
                dstb = tokfir[p]
                for m in range(8):
                    sy.dma_start(
                        out=dstb[:].rearrange("p (t c) -> p t c", t=4)[:, :, m * 128:(m + 1) * 128],
                        in_=scr["firb"][p][:, m * 512:(m + 1) * 512],
                        transpose=True).then_inc(s_tr, 16)
            sy.wait_ge(s_tr, ctx.cnt["a0_tr"] + 32 * 16)

        @b.gpsimd
        def _(g):
            g.dma_start(out=wbuf[:], in_=aps["woa"][:, :]).then_inc(s_w, 16)
            g.wait_ge(s_w, ctx.cnt["a0_w"] + 16)
    ctx.cnt["a0_tr"] += 32 * 16
    ctx.cnt["a0_w"] += 16

    o_tok = scr["k_tok"]
    with nc.Block() as b:
        @b.vector
        def _(v):
            for kt in range(ND):
                v.tensor_scalar(out=wbuf[:, kt * 1024:(kt + 1) * 1024],
                                in0=wbuf[:, kt * 1024:(kt + 1) * 1024],
                                scalar1=scr["onwB"][:, kt:kt + 1],
                                scalar2=None, op0=ALU.mult)
            v.drain()
            for tsub in range(4):
                for h in range(4):
                    osl = o_tok[:, tsub * 1024 + h * 256: tsub * 1024 + (h + 1) * 256]
                    for p in range(4):
                        src = tokfir[p][:, tsub * 1024 + h * 256: tsub * 1024 + (h + 1) * 256]
                        pcol = scr["probs"][:, tsub * 20 + h * 5 + p: tsub * 20 + h * 5 + p + 1]
                        if p == 0:
                            v.tensor_scalar(out=osl, in0=src, scalar1=pcol,
                                            scalar2=None, op0=ALU.mult)
                        else:
                            v.scalar_tensor_tensor(out=osl, in0=src, scalar=pcol,
                                                   in1=osl, op0=ALU.mult, op1=ALU.add)
                        v.drain()
                    dsl = delta_tok[:, tsub * 1024 + h * 256: tsub * 1024 + (h + 1) * 256]
                    pcol = scr["probs"][:, tsub * 20 + h * 5 + 4: tsub * 20 + h * 5 + 5]
                    v.scalar_tensor_tensor(out=osl, in0=dsl, scalar=pcol,
                                           in1=osl, op0=ALU.mult, op1=ALU.add)
                    v.drain()
                    vsl = scr["v_tok0"][:, tsub * 1024 + h * 256: tsub * 1024 + (h + 1) * 256]
                    v.tensor_scalar(out=scr["y0"][:, 0:256], in0=vsl,
                                    scalar1=st[:, 16 + tsub * 4 + h: 16 + tsub * 4 + h + 1],
                                    scalar2=None, op0=ALU.mult)
                    v.drain()
                    v.scalar_tensor_tensor(out=osl, in0=osl,
                                           scalar=st[:, 32 + tsub * 4 + h: 32 + tsub * 4 + h + 1],
                                           in1=scr["y0"][:, 0:256],
                                           op0=ALU.mult, op1=ALU.add)
                    v.drain()
            for tsub in range(4):
                v.tensor_tensor(out=scr["scrot"][:, 0:1024],
                                in0=o_tok[:, tsub * 1024:(tsub + 1) * 1024],
                                in1=o_tok[:, tsub * 1024:(tsub + 1) * 1024], op=ALU.mult)
                v.drain()
                v.tensor_reduce(out=st[:, 48 + tsub * 4: 48 + (tsub + 1) * 4],
                                in_=scr["scrot"][:, 0:1024].rearrange("p (h c) -> p h c", h=4),
                                axis=AX.X, op=ALU.add)
                v.drain()
            v.tensor_scalar(out=st[:, 48:64], in0=st[:, 48:64], scalar1=1.0 / 256,
                            scalar2=1e-5, op0=ALU.mult, op1=ALU.add)
            v.drain()

    with nc.Block() as b:
        @b.scalar
        def _(sc):
            sc.activation(out=st[:, 48:64], in_=st[:, 48:64], func=AF.Sqrt)

    s_ox = ctx.sem("a0_ox")
    b_ox = ctx.cnt["a0_ox"]
    b_om = ctx.cnt["a0_pm"]
    b_ol = ctx.cnt["a0_ld"]
    with nc.Block() as b:
        @b.vector
        def _(v):
            v.reciprocal(out=st[:, 48:64], in_=st[:, 48:64])
            v.drain()
            for tsub in range(4):
                for h in range(4):
                    osl = o_tok[:, tsub * 1024 + h * 256: tsub * 1024 + (h + 1) * 256]
                    v.tensor_scalar(out=osl, in0=osl,
                                    scalar1=st[:, 48 + tsub * 4 + h: 48 + tsub * 4 + h + 1],
                                    scalar2=None, op0=ALU.mult)
            v.drain()
            v.nop().then_inc(s_ox, 1)

        @b.sync
        def _(sy):
            sy.wait_ge(s_ox, b_ox + 1)
            for tsub in range(4):
                sy.dma_start(
                    out=scr["q_tok"][:].rearrange("p (f c) -> p f c", f=8)[:, :, tsub * 128:(tsub + 1) * 128],
                    in_=o_tok[:, tsub * 1024:(tsub + 1) * 1024],
                    transpose=True).then_inc(s_tr, 16)
            sy.wait_ge(s_tr, ctx.cnt["a0_tr"] + 4 * 16)
    ctx.cnt["a0_ox"] += 1
    ctx.cnt["a0_tr"] += 4 * 16

    oT = scr["q_tok"]
    s_st = ctx.sem("a0_st")
    b_st = ctx.cnt["a0_st"]
    b_xw = ctx.cnt["a0_w"]
    b_oxd = ctx.cnt["a0_ox"]
    with nc.Block() as b:
        @b.gpsimd
        def _(g):
            for tsub in range(4):
                if tsub >= 2:
                    g.wait_ge(s_ox, b_oxd + (tsub - 2) * 2 + 2)
                g.dma_start(out=scr["xrot"][:, (tsub % 2) * D:(tsub % 2 + 1) * D],
                            in_=xin[c0 + tsub * 128: c0 + (tsub + 1) * 128, :]
                            ).then_inc(s_w, 16)

        @b.tensor
        def _(t):
            for tsub in range(4):
                for nn in range(2):
                    idx = tsub * 2 + nn
                    ps = pb0 if idx % 2 == 0 else pb1
                    if idx >= 2:
                        t.wait_ge(s_ox, b_oxd + idx - 1)
                    for kt in range(ND):
                        ins = t.matmul(ps[:, 0:512],
                                       oT[:, kt * 512 + tsub * 128: kt * 512 + (tsub + 1) * 128],
                                       wbuf[:, kt * 1024 + nn * 512: kt * 1024 + (nn + 1) * 512],
                                       start=(kt == 0), stop=(kt == ND - 1))
                    ins.then_inc(s_pm, 1)

        @b.vector
        def _(v):
            for tsub in range(4):
                for nn in range(2):
                    idx = tsub * 2 + nn
                    ps = pb0 if idx % 2 == 0 else pb1
                    v.wait_ge(s_pm, b_om + idx + 1)
                    v.wait_ge(s_w, b_xw + (tsub + 1) * 16)
                    if idx >= 4:
                        v.wait_ge(s_st, b_st + (idx - 3) * 16)
                    v.drain()
                    v.tensor_tensor(out=scr["xo"][:, (idx % 4) * 512:(idx % 4 + 1) * 512],
                                    in0=ps[:, 0:512],
                                    in1=scr["xrot"][:, (tsub % 2) * D + nn * 512:(tsub % 2) * D + (nn + 1) * 512],
                                    op=ALU.add).then_inc(s_ox, 1)
                    v.drain()

        @b.sync
        def _(sy):
            for tsub in range(4):
                for nn in range(2):
                    idx = tsub * 2 + nn
                    sy.wait_ge(s_ox, b_oxd + idx + 1)
                    sy.dma_start(out=xout[c0 + tsub * 128: c0 + (tsub + 1) * 128,
                                          nn * 512:(nn + 1) * 512],
                                 in_=scr["xo"][:, (idx % 4) * 512:(idx % 4 + 1) * 512]
                                 ).then_inc(s_st, 16)
            sy.wait_ge(s_st, b_st + 8 * 16)
    ctx.cnt["a0_pm"] += 8
    ctx.cnt["a0_w"] += 64
    ctx.cnt["a0_ox"] += 8
    ctx.cnt["a0_st"] += 8 * 16


def prep_l0attn(inp):
    def prepW(w):
        return np.ascontiguousarray(w.reshape(8, 128, 1024).transpose(1, 0, 2).reshape(128, 8192))

    def prepWm(w):
        t = w.reshape(8, 128, 8, 128)
        return np.ascontiguousarray(t.transpose(1, 2, 0, 3).reshape(128, 8192))
    wqa = prepWm(inp["L0_attn_Wq"]); wka = prepWm(inp["L0_attn_Wk"])
    wva = prepWm(inp["L0_attn_Wv"]); woa = prepW(inp["L0_attn_Wo"])
    W1 = inp["L0_attn_router_W1"]
    w1p = np.empty((16, 2, 128, 3072), np.float32)
    W1r = W1.reshape(48, 128, 2048)
    for m in range(16):
        for hf in range(2):
            blk = W1r[hf * 24:(hf + 1) * 24, :, m * 128:(m + 1) * 128]
            w1p[m, hf] = blk.transpose(1, 0, 2).reshape(128, 3072)
    w1 = np.ascontiguousarray(w1p.reshape(32 * 128, 3072))
    w2 = np.ascontiguousarray(
        inp["L0_attn_router_W2"].reshape(16, 128, 20).transpose(1, 0, 2).reshape(128, 320))
    b1 = np.ascontiguousarray(inp["L0_attn_router_b1"].reshape(16, 128).T)
    b2 = inp["L0_attn_router_b2"].reshape(1, 20)
    bpw = np.concatenate([inp["L0_attn_Wb"], inp["L0_attn_id_gate_W"]], axis=1)
    bpw = np.ascontiguousarray(bpw.reshape(8, 128, 8).transpose(1, 0, 2).reshape(128, 64))
    idb = np.zeros((1, 8), np.float32)
    idb[0, 4:8] = inp["L0_attn_id_gate_b"]
    ltau = inp["L0_attn_log_tau"].reshape(1, 4)
    onw = np.ascontiguousarray(np.tile(inp["L0_attn_o_norm_w"], 4).reshape(8, 128).T)

    def prepC(w):
        return np.ascontiguousarray(w.reshape(8, 128, 4).transpose(1, 0, 2).reshape(128, 32))
    qcw = prepC(inp["L0_attn_q_conv_w"]); kcw = prepC(inp["L0_attn_k_conv_w"])
    vcw = prepC(inp["L0_attn_v_conv_w"])
    firw = np.zeros((8, 128, 56), np.float32)
    offs = {3: 0, 7: 3, 15: 10, 31: 25}
    for key, K in [("L0_attn_fir_w3", 3), ("L0_attn_fir_w7", 7),
                   ("L0_attn_fir_w15", 15), ("L0_attn_fir_w31", 31)]:
        f = inp[key].reshape(1024, K)
        fr = f.reshape(8, 128, K)
        firw[:, :, offs[K]:offs[K] + K] += fr
    firw = np.ascontiguousarray(firw.transpose(1, 0, 2).reshape(128, 448))
    cst0 = np.zeros((128, 512), np.float32)
    cst0[:, 0:128] = np.eye(128)
    cst0[:, 128:256] = -np.tril(np.ones((128, 128)), -1)
    cst0[:, 256:384] = -np.triu(np.ones((128, 128)), 1)
    cst0[:, 384:512] = np.triu(np.ones((128, 128)), 0)
    ones = np.ones((1, 128), np.float32)
    return {"wqa": wqa, "wka": wka, "wva": wva, "woa": woa, "w1": w1, "w2": w2,
            "b1": b1, "b2": np.ascontiguousarray(b2), "bpw": bpw,
            "idb": idb, "ltau": np.ascontiguousarray(ltau), "onw": onw,
            "qcw": qcw, "kcw": kcw, "vcw": vcw, "firw": firw,
            "cst0": cst0, "ones": ones}


def alloc_l0attn_scr(ctx):
    scr = {}
    convfull = ctx.sbuf("convfull", [128, 4096], BF16)
    scr["convfull"] = convfull
    scr["xhat2"] = convfull[:, 0:2048]
    scr["sq"] = convfull[:, 2048:3072]
    for name, shape, dt in [
            ("wbuf", [128, 8192], BF16), ("rawfull", [128, 4352], BF16),
            ("histq", [128, 256], BF16), ("histk", [128, 256], BF16),
            ("histv", [128, 256], BF16), ("vc", [128, 4352], BF16),
            ("q_tok", [128, 4096], BF16), ("k_tok", [128, 4096], BF16),
            ("v_tok0", [128, 4096], BF16), ("kn_tok", [128, 4096], BF16),
            ("scrot", [128, 2048], BF16),
            ("qnT", [128, 4096], BF16), ("knT", [128, 4096], BF16),
            ("kbnT", [128, 4096], BF16),
            ("y0", [128, 512], BF16), ("tA", [128, 512], BF16),
            ("tB", [128, 512], BF16), ("Pp", [128, 512], BF16),
            ("attT", [128, 128], BF16), ("u_sb", [128, 256], BF16),
            ("w_sb", [128, 256], BF16), ("wnegT", [128, 256], BF16),
            ("uadj", [128, 256], BF16), ("uadj2", [128, 256], BF16),
            ("S32", [128, 2048], F32), ("S_bf", [128, 2048], BF16),
            ("dcol", [128, 16], F32), ("ecol", [128, 16], F32),
            ("bp", [128, 32], F32), ("probs", [128, 80], F32),
            ("invtauB", [128, 4], F32), ("rowb", [1, 64], BF16),
            ("rs0", [128, 64], BF16),
            ("xrot", [128, 2048], F32), ("xo", [128, 2048], F32),
            ("st", [128, 96], F32),
            ("cst0", [128, 512], BF16), ("ones", [1, 128], BF16),
            ("qcw", [128, 32], F32), ("kcw", [128, 32], F32),
            ("vcw", [128, 32], F32), ("firw", [128, 448], F32),
            ("w2B", [128, 320], BF16), ("b1B", [128, 16], F32),
            ("b2B", [1, 20], BF16), ("bpW", [128, 64], BF16),
            ("idbB", [1, 8], BF16), ("ltau", [1, 4], F32),
            ("onwB", [128, 8], F32)]:
        scr[name] = ctx.sbuf(name, shape, dt)
    scr["hmid"] = scr["scrot"][:, 0:1024]
    scr["firb"] = [ctx.sbuf("fir%d" % p, [128, 4096], BF16) for p in range(4)]
    for name in ["ps_b0", "ps_b1", "ps_a", "ps_d", "ps_f", "ps_su"]:
        scr[name] = ctx.psum(name, [128, 512])
    return scr

_L0A_SHAPES = {"wqa": [128, 8192], "wka": [128, 8192], "wva": [128, 8192],
               "woa": [128, 8192], "w1": [4096, 3072], "w2": [128, 320],
               "b1": [128, 16], "b2": [1, 20], "bpw": [128, 64], "idb": [1, 8],
               "ltau": [1, 4], "onw": [128, 8], "qcw": [128, 32], "kcw": [128, 32],
               "vcw": [128, 32], "firw": [128, 448], "cst0": [128, 512],
               "ones": [1, 128]}

_L1A_SHAPES = {"wq": [8 * 128, 2048], "wk": [8 * 128, 2048], "wv": [128, 8192],
               "wo": [128, 8192], "bq": [128, 16], "bk": [128, 16],
               "bv": [1, 1024], "bo": [1, 1024], "C": [128, 2048],
               "Sg": [128, 2048], "cst": [128, 384], "ones": [1, 128],
               "lqk": [16, 256], "hn": [128, 8]}


def build_l0attn_prog():
    ctx = Ctx()
    nc = ctx.nc
    xin = nc.dram_tensor("xin", [S, D], F32, kind="ExternalInput").ap()
    dr = {k: nc.dram_tensor("i_" + k, shp, F32, kind="ExternalInput").ap()
          for k, shp in _L0A_SHAPES.items()}
    xout = nc.dram_tensor("xout", [S, D], F32, kind="ExternalOutput").ap()
    xhatT = ctx.sbuf("xhatT", [128, ND * S], BF16)
    scr = alloc_l0attn_scr(ctx)
    emit_ln_stream(ctx, xin, xhatT, scr, "ln0")
    emit_l0attn(ctx, xin, xout, xhatT, dr, scr)
    ctx.close()
    return ctx


def build_l0ffn_prog():
    ctx = Ctx()
    nc = ctx.nc
    xin = nc.dram_tensor("xin", [S, D], F32, kind="ExternalInput").ap()
    gw = nc.dram_tensor("i_gw", [128, 64], F32, kind="ExternalInput").ap()
    gu = nc.dram_tensor("i_gu", [8 * 16 * 128, 2048], F32, kind="ExternalInput").ap()
    dw = nc.dram_tensor("i_dw", [8 * 128, 16384], F32, kind="ExternalInput").ap()
    xout = nc.dram_tensor("xout", [S, D], F32, kind="ExternalOutput").ap()

    x_res = ctx.sbuf("x_res", [128, NT * D], F32)
    xhatT = ctx.sbuf("xhatT", [128, ND * S], BF16)
    h = ctx.sbuf("h", [128, 16 * 2048], BF16)
    scr = {
        "xhat2": h[:, 0:2 * D],
        "sq": h[:, 2 * D:3 * D],
        "st": ctx.sbuf("st", [128, 96], F32),
        "h": h,
        "guW": ctx.sbuf("guW", [128, 2 * 2048], BF16),
        "dwW": ctx.sbuf("dwW", [128, 16384], BF16),
        "sg": ctx.sbuf("sg", [128, 2 * 512], BF16),
        "comb": ctx.sbuf("comb", [128, 128], F32),
        "pex": ctx.sbuf("pex", [128, 16], F32),
        "gwB": ctx.sbuf("gwB", [128, 128], BF16),
        "pgu0": ctx.psum("pgu0", [128, 1024]),
        "pgu1": ctx.psum("pgu1", [128, 1024]),
        "pd0": ctx.psum("pd0", [128, 1024]),
        "pd1": ctx.psum("pd1", [128, 1024]),
    }
    s_io = ctx.sem("io")
    with ctx.nc.Block() as b:
        @b.sync
        def _(sy):
            sy.dma_start(out=x_res[:].rearrange("p (t d) -> p t d", t=NT),
                         in_=xin[:].rearrange("(t p) d -> p t d", p=128)).then_inc(s_io, 16)
            sy.wait_ge(s_io, 16)
    emit_ln(ctx, x_res, xhatT, scr, "ln1")
    emit_l0ffn(ctx, x_res, xhatT, {"gw": gw, "gu": gu, "dw": dw}, scr)
    with ctx.nc.Block() as b:
        @b.sync
        def _(sy):
            sy.dma_start(out=xout[:].rearrange("(t p) d -> p t d", p=128),
                         in_=x_res[:].rearrange("p (t d) -> p t d", t=NT)).then_inc(s_io, 16)
            sy.wait_ge(s_io, 32)
    ctx.close()
    return ctx


def build_l1attn_prog():
    ctx = Ctx()
    nc = ctx.nc
    xin = nc.dram_tensor("xin", [S, D], F32, kind="ExternalInput").ap()
    dr = {k: nc.dram_tensor("i_" + k, shp, F32, kind="ExternalInput").ap()
          for k, shp in _L1A_SHAPES.items()}
    xout = nc.dram_tensor("xout", [S, D], F32, kind="ExternalOutput").ap()

    x_res = ctx.sbuf("x_res", [128, NT * D], F32)
    xhatT = ctx.sbuf("xhatT", [128, ND * S], BF16)
    scr = {
        "u1": ctx.sbuf("u1", [128, 8192], BF16),
        "v_tok": ctx.sbuf("v_tok", [128, NT * 1024], BF16),
        "o_tok": ctx.sbuf("o_tok", [128, NT * 1024], BF16),
        "e1": ctx.sbuf("e1", [128, 2048], BF16),
        "e2": ctx.sbuf("e2", [128, 2048], BF16),
        "aS": ctx.sbuf("aS", [128, 2048], BF16),
        "aT": ctx.sbuf("aT", [128, 2048], BF16),
        "rs": ctx.sbuf("rs", [128, 512], BF16),
        "C": ctx.sbuf("C", [128, 2048], BF16),
        "Sg": ctx.sbuf("Sg", [128, 2048], BF16),
        "cst": ctx.sbuf("cst", [128, 384], BF16),
        "ones": ctx.sbuf("onesb", [1, 128], BF16),
        "bqB": ctx.sbuf("bqB", [128, 16], F32),
        "bkB": ctx.sbuf("bkB", [128, 16], F32),
        "bvB": ctx.sbuf("bvB", [1, 1024], BF16),
        "boB": ctx.sbuf("boB", [1, 1024], BF16),
        "hnB": ctx.sbuf("hnB", [128, 8], F32),
        "lamB": ctx.sbuf("lamB", [128, 16], F32),
        "ssqB": ctx.sbuf("ssqB", [128, 256], F32),
        "st": ctx.sbuf("st", [128, 96], F32),
        "ps_s1": ctx.psum("ps_s1", [128, 2048]),
        "ps_s2": ctx.psum("ps_s2", [128, 1536]),
        "ps_o": ctx.psum("ps_o", [128, 512]),
    }
    scr["lqk"] = scr["e1"][:, 0:512].bitcast(F32)
    scr["xhat2"] = scr["v_tok"][:, 0:2 * D]
    scr["sq"] = scr["v_tok"][:, 2 * D:3 * D]
    scr["oT"] = xhatT
    s_io = ctx.sem("io")
    with ctx.nc.Block() as b:
        @b.sync
        def _(sy):
            sy.dma_start(out=x_res[:].rearrange("p (t d) -> p t d", t=NT),
                         in_=xin[:].rearrange("(t p) d -> p t d", p=128)).then_inc(s_io, 16)
            sy.wait_ge(s_io, 16)
    emit_ln(ctx, x_res, xhatT, scr, "ln1")
    emit_l1attn(ctx, x_res, xhatT, dr, scr)
    with ctx.nc.Block() as b:
        @b.sync
        def _(sy):
            sy.dma_start(out=xout[:].rearrange("(t p) d -> p t d", p=128),
                         in_=x_res[:].rearrange("p (t d) -> p t d", t=NT)).then_inc(s_io, 16)
            sy.wait_ge(s_io, 32)
    ctx.close()
    return ctx


def build_l1ffn_prog():
    ctx = Ctx()
    nc = ctx.nc
    xin = nc.dram_tensor("xin", [S, D], F32, kind="ExternalInput").ap()
    gw = nc.dram_tensor("i_gw", [128, 128], F32, kind="ExternalInput").ap()
    shp = nc.dram_tensor("i_shp", [8 * 128, 2048], F32, kind="ExternalInput").ap()
    sho = nc.dram_tensor("i_sho", [128, 8192], F32, kind="ExternalInput").ap()
    ep = nc.dram_tensor("i_ep", [16 * 4 * 128, 2048], F32, kind="ExternalInput").ap()
    eo = nc.dram_tensor("i_eo", [16 * 128, 4096], F32, kind="ExternalInput").ap()
    xout = nc.dram_tensor("xout", [S, D], F32, kind="ExternalOutput").ap()

    x_res = ctx.sbuf("x_res", [128, NT * D], F32)
    xhatT = ctx.sbuf("xhatT", [128, ND * S], BF16)
    h = ctx.sbuf("h", [128, 16 * 2048], BF16)
    scr = {
        "xhat2": h[:, 0:2 * D],
        "sq": h[:, 2 * D:3 * D],
        "st": ctx.sbuf("st", [128, 96], F32),
        "h": h,
        "guW": ctx.sbuf("guW", [128, 2 * 2048], BF16),
        "dwW": ctx.sbuf("dwW", [128, 16384], BF16),
        "sg": ctx.sbuf("sg", [128, 2 * 512], BF16),
        "comb": ctx.sbuf("comb", [128, 256], F32),
        "pex": ctx.sbuf("pex", [128, 16], F32),
        "gwB": ctx.sbuf("gwB", [128, 128], BF16),
        "pgu0": ctx.psum("pgu0", [128, 1024]),
        "pgu1": ctx.psum("pgu1", [128, 1024]),
        "pd0": ctx.psum("pd0", [128, 1024]),
        "pd1": ctx.psum("pd1", [128, 1024]),
    }
    s_io = ctx.sem("io")
    with ctx.nc.Block() as b:
        @b.sync
        def _(sy):
            sy.dma_start(out=x_res[:].rearrange("p (t d) -> p t d", t=NT),
                         in_=xin[:].rearrange("(t p) d -> p t d", p=128)).then_inc(s_io, 16)
            sy.wait_ge(s_io, 16)
    emit_ln(ctx, x_res, xhatT, scr, "ln1")
    emit_l1ffn(ctx, x_res, xhatT,
               {"gw": gw, "shp": shp, "sho": sho, "ep": ep, "eo": eo}, scr)
    with ctx.nc.Block() as b:
        @b.sync
        def _(sy):
            sy.dma_start(out=xout[:].rearrange("(t p) d -> p t d", p=128),
                         in_=x_res[:].rearrange("p (t d) -> p t d", t=NT)).then_inc(s_io, 16)
            sy.wait_ge(s_io, 32)
    ctx.close()
    return ctx


_CACHE = {}


def kernel_unfused(**inputs):
    inputs = {k: np.ascontiguousarray(np.asarray(v, dtype=np.float32))
              for k, v in inputs.items()}
    if "progs" not in _CACHE:
        _CACHE["progs"] = (build_l0attn_prog(), build_l0ffn_prog(),
                           build_l1attn_prog(), build_l1ffn_prog())
    p0a, p0f, p1a, p1f = _CACHE["progs"]
    w0a = prep_l0attn(inputs)
    w0f = prep_l0ffn(inputs)
    w1a = prep_l1attn(inputs)
    w1f = prep_l1ffn(inputs)

    x = inputs["L0_attn_hidden_states"]
    ncore = 8

    def launch(ctx, wmap, xcur):
        maps = []
        for c in range(ncore):
            m = {"xin": np.ascontiguousarray(xcur[c])}
            m.update({"i_" + k: v for k, v in wmap.items()})
            maps.append(m)
        res = run_bass_kernel_spmd(ctx.nc, maps, core_ids=list(range(ncore)))
        return np.stack([res.results[c]["xout"] for c in range(ncore)], axis=0)

    x = launch(p0a, w0a, x)
    x = launch(p0f, w0f, x)
    x = launch(p1a, w1a, x)
    x = launch(p1f, w1f, x)
    return x



def _alloc_ffn_scr(ctx, comb_cols, suf):
    h = ctx.sbuf("h" + suf, [128, 16 * 2048], BF16)
    scr = {
        "xhat2": h[:, 0:2 * D],
        "sq": h[:, 2 * D:3 * D],
        "st": ctx.sbuf("st" + suf, [128, 96], F32),
        "h": h,
        "guW": ctx.sbuf("guW" + suf, [128, 2 * 2048], BF16),
        "dwW": ctx.sbuf("dwW" + suf, [128, 16384], BF16),
        "sg": ctx.sbuf("sg" + suf, [128, 2 * 512], BF16),
        "comb": ctx.sbuf("comb" + suf, [128, comb_cols], F32),
        "pex": ctx.sbuf("pex" + suf, [128, 16], F32),
        "gwB": ctx.sbuf("gwB" + suf, [128, 128], BF16),
        "pgu0": ctx.psum("pgu0" + suf, [128, 1024]),
        "pgu1": ctx.psum("pgu1" + suf, [128, 1024]),
        "pd0": ctx.psum("pd0" + suf, [128, 1024]),
        "pd1": ctx.psum("pd1" + suf, [128, 1024]),
    }
    return scr


def _alloc_l1attn_scr(ctx, xhatT):
    scr = {
        "u1": ctx.sbuf("u1", [128, 8192], BF16),
        "v_tok": ctx.sbuf("v_tok1", [128, NT * 1024], BF16),
        "o_tok": ctx.sbuf("o_tok", [128, NT * 1024], BF16),
        "e1": ctx.sbuf("e1", [128, 2048], BF16),
        "e2": ctx.sbuf("e2", [128, 2048], BF16),
        "aS": ctx.sbuf("aS", [128, 2048], BF16),
        "aT": ctx.sbuf("aT", [128, 2048], BF16),
        "rs": ctx.sbuf("rs", [128, 512], BF16),
        "C": ctx.sbuf("C", [128, 2048], BF16),
        "Sg": ctx.sbuf("Sg", [128, 2048], BF16),
        "cst": ctx.sbuf("cst", [128, 384], BF16),
        "ones": ctx.sbuf("onesb", [1, 128], BF16),
        "bqB": ctx.sbuf("bqB", [128, 16], F32),
        "bkB": ctx.sbuf("bkB", [128, 16], F32),
        "bvB": ctx.sbuf("bvB", [1, 1024], BF16),
        "boB": ctx.sbuf("boB", [1, 1024], BF16),
        "hnB": ctx.sbuf("hnB", [128, 8], F32),
        "lamB": ctx.sbuf("lamB", [128, 16], F32),
        "ssqB": ctx.sbuf("ssqB", [128, 256], F32),
        "st": ctx.sbuf("st1a", [128, 96], F32),
        "ps_s1": ctx.psum("ps_s1", [128, 2048]),
        "ps_s2": ctx.psum("ps_s2", [128, 1536]),
        "ps_o": ctx.psum("ps_o", [128, 512]),
    }
    scr["lqk"] = scr["e1"][:, 0:512].bitcast(F32)
    scr["xhat2"] = scr["v_tok"][:, 0:2 * D]
    scr["sq"] = scr["v_tok"][:, 2 * D:3 * D]
    scr["oT"] = xhatT
    return scr


def build_fused_prog():
    ctx = Ctx()
    nc = ctx.nc
    xin = nc.dram_tensor("xin", [S, D], F32, kind="ExternalInput").ap()
    xout = nc.dram_tensor("xout", [S, D], F32, kind="ExternalOutput").ap()
    xmid = nc.dram_tensor("xmid", [S, D], F32, kind="Internal").ap()
    dr = {}
    for k, shp in _L0A_SHAPES.items():
        dr["a0_" + k] = nc.dram_tensor("a0_" + k, shp, F32, kind="ExternalInput").ap()
    for k, shp in [("gw", [128, 64]), ("gu", [8 * 16 * 128, 2048]),
                   ("dw", [8 * 128, 16384])]:
        dr["f0_" + k] = nc.dram_tensor("f0_" + k, shp, F32, kind="ExternalInput").ap()
    for k, shp in _L1A_SHAPES.items():
        dr["a1_" + k] = nc.dram_tensor("a1_" + k, shp, F32, kind="ExternalInput").ap()
    for k, shp in [("gw", [128, 128]), ("shp", [8 * 128, 2048]),
                   ("sho", [128, 8192]), ("ep", [16 * 4 * 128, 2048]),
                   ("eo", [16 * 128, 4096])]:
        dr["f1_" + k] = nc.dram_tensor("f1_" + k, shp, F32, kind="ExternalInput").ap()

    xhatT = ctx.sbuf("xhatT", [128, ND * S], BF16)

    ctx.push_scope()
    scr = alloc_l0attn_scr(ctx)
    emit_ln_stream(ctx, xin, xhatT, scr, "ln0")
    emit_l0attn(ctx, xin, xmid, xhatT,
                {k: dr["a0_" + k] for k in _L0A_SHAPES}, scr)
    ctx.pop_scope()

    x_res = ctx.sbuf("x_res", [128, NT * D], F32)
    s_io = ctx.sem("io")
    ctx.push_scope()
    scr = _alloc_ffn_scr(ctx, 128, "f0")
    with nc.Block() as b:
        @b.sync
        def _(sy):
            sy.dma_start(out=x_res[:].rearrange("p (t d) -> p t d", t=NT),
                         in_=xmid[:].rearrange("(t p) d -> p t d", p=128)).then_inc(s_io, 16)
            sy.wait_ge(s_io, 16)
    emit_ln(ctx, x_res, xhatT, scr, "ln1")
    emit_l0ffn(ctx, x_res, xhatT,
               {"gw": dr["f0_gw"], "gu": dr["f0_gu"], "dw": dr["f0_dw"]}, scr)
    ctx.pop_scope()

    ctx.push_scope()
    scr = _alloc_l1attn_scr(ctx, xhatT)
    emit_ln(ctx, x_res, xhatT, scr, "ln2")
    emit_l1attn(ctx, x_res, xhatT, {k: dr["a1_" + k] for k in _L1A_SHAPES}, scr)
    ctx.pop_scope()

    ctx.push_scope()
    scr = _alloc_ffn_scr(ctx, 256, "f1")
    emit_ln(ctx, x_res, xhatT, scr, "ln3")
    emit_l1ffn(ctx, x_res, xhatT,
               {"gw": dr["f1_gw"], "shp": dr["f1_shp"], "sho": dr["f1_sho"],
                "ep": dr["f1_ep"], "eo": dr["f1_eo"]}, scr)
    ctx.pop_scope()

    with nc.Block() as b:
        @b.sync
        def _(sy):
            sy.dma_start(out=xout[:].rearrange("(t p) d -> p t d", p=128),
                         in_=x_res[:].rearrange("p (t d) -> p t d", t=NT)).then_inc(s_io, 16)
            sy.wait_ge(s_io, 32)

    ctx.close()
    return ctx


def kernel_fused(**inputs):
    inputs = {k: np.ascontiguousarray(np.asarray(v, dtype=np.float32))
              for k, v in inputs.items()}
    if "fprog" not in _CACHE:
        _CACHE["fprog"] = build_fused_prog()
    ctx = _CACHE["fprog"]
    w0a = prep_l0attn(inputs)
    w0f = prep_l0ffn(inputs)
    w1a = prep_l1attn(inputs)
    w1f = prep_l1ffn(inputs)
    wall = {}
    wall.update({"a0_" + k: v for k, v in w0a.items()})
    wall.update({"f0_" + k: v for k, v in w0f.items()})
    wall.update({"a1_" + k: v for k, v in w1a.items()})
    wall.update({"f1_" + k: v for k, v in w1f.items()})
    x = inputs["L0_attn_hidden_states"]
    maps = []
    for c in range(8):
        m = {"xin": np.ascontiguousarray(x[c])}
        m.update(wall)
        maps.append(m)
    res = run_bass_kernel_spmd(ctx.nc, maps, core_ids=list(range(8)))
    return np.stack([res.results[c]["xout"] for c in range(8)], axis=0)


def kernel(**inputs):
    return kernel_fused(**inputs)


# version 08
# speedup vs baseline: 1.0441x; 1.0441x over previous
import contextlib
import numpy as np
import concourse.bass as bass
import concourse.mybir as mybir
from concourse.bass_utils import run_bass_kernel_spmd

F32 = mybir.dt.float32
BF16 = mybir.dt.bfloat16
AF = mybir.ActivationFunctionType
ALU = mybir.AluOpType
AX = mybir.AxisListType

S = 2048
D = 1024
NT = S // 128
ND = D // 128


class Ctx:
    def __init__(self):
        self.nc = bass.Bass("TRN2", target_bir_lowering=False)
        self.es = contextlib.ExitStack()
        self.scopes = [self.es]
        self.sems = {}
        self.cnt = {}

    def push_scope(self):
        es = contextlib.ExitStack()
        self.scopes.append(es)
        return es

    def pop_scope(self):
        self.scopes.pop().close()

    def sem(self, name):
        if name not in self.sems:
            self.sems[name] = self.es.enter_context(self.nc.semaphore(name))
            self.cnt[name] = 0
        return self.sems[name]

    def sbuf(self, name, shape, dtype):
        return self.scopes[-1].enter_context(self.nc.sbuf_tensor(name, shape, dtype))

    def psum(self, name, shape, dtype=F32):
        return self.scopes[-1].enter_context(self.nc.psum_tensor(name, shape, dtype))

    def close(self):
        while len(self.scopes) > 1:
            self.pop_scope()
        self.es.close()



def emit_ln(ctx, x_res, xhatT, scr, tag):
    nc = ctx.nc
    st = scr["st"]
    hat = ctx.sem("hat_" + tag)
    tr = ctx.sem("tr_" + tag)
    aq = ctx.sem("aq_" + tag)

    with nc.Block() as b:
        @b.vector
        def _(v):
            for t in range(NT):
                v.tensor_reduce(out=st[:, t:t + 1], in_=x_res[:, t * D:(t + 1) * D],
                                axis=AX.X, op=ALU.add)

        @b.scalar
        def _(sc):
            for t in range(NT):
                sc.activation(out=scr["sq"][:], in_=x_res[:, t * D:(t + 1) * D],
                              func=AF.Square, accum_out=st[:, 16 + t:17 + t])

    with nc.Block() as b:
        @b.vector
        def _(v):
            v.tensor_scalar(out=st[:, 32:48], in0=st[:, 0:16], scalar1=1.0 / D,
                            scalar2=None, op0=ALU.mult)
            v.tensor_scalar(out=st[:, 48:64], in0=st[:, 16:32], scalar1=1.0 / D,
                            scalar2=None, op0=ALU.mult)
            v.drain()
            v.tensor_tensor(out=st[:, 64:80], in0=st[:, 32:48], in1=st[:, 32:48],
                            op=ALU.mult)
            v.drain()
            v.tensor_tensor(out=st[:, 80:96], in0=st[:, 48:64], in1=st[:, 64:80],
                            op=ALU.subtract)
            v.drain()
            v.tensor_scalar(out=st[:, 80:96], in0=st[:, 80:96], scalar1=1e-5,
                            scalar2=None, op0=ALU.add)
            v.tensor_scalar(out=st[:, 0:16], in0=st[:, 32:48], scalar1=-1.0,
                            scalar2=None, op0=ALU.mult)

    with nc.Block() as b:
        @b.scalar
        def _(sc):
            sc.activation(out=st[:, 48:64], in_=st[:, 80:96], func=AF.Sqrt)

    with nc.Block() as b:
        @b.vector
        def _(v):
            v.reciprocal(out=st[:, 16:32], in_=st[:, 48:64])
            v.drain()
            for t in range(NT):
                if t >= 2:
                    v.wait_ge(tr, ctx.cnt["tr_" + tag] + (t - 1) * 16)
                v.tensor_scalar(out=scr["xhat2"][:, (t % 2) * D:(t % 2 + 1) * D],
                                in0=x_res[:, t * D:(t + 1) * D],
                                scalar1=st[:, t:t + 1], scalar2=st[:, 16 + t:17 + t],
                                op0=ALU.add, op1=ALU.mult).then_inc(hat, 1)

        @b.sync
        def _(sy):
            for t in range(NT):
                sy.wait_ge(hat, ctx.cnt["hat_" + tag] + t + 1)
                sy.dma_start(
                    out=xhatT[:].rearrange("p (k s) -> p k s", k=ND)[:, :, t * 128:(t + 1) * 128],
                    in_=scr["xhat2"][:, (t % 2) * D:(t % 2 + 1) * D],
                    transpose=True).then_inc(tr, 16)
            sy.wait_ge(tr, ctx.cnt["tr_" + tag] + NT * 16).then_inc(aq, 1)

    ctx.cnt["hat_" + tag] += NT
    ctx.cnt["tr_" + tag] += NT * 16
    ctx.cnt["aq_" + tag] += 1



def emit_l0ffn(ctx, x_res, xhatT, aps, scr, experts=True):
    import os
    rpart = int(os.environ.get("RPART", "3"))
    """aps: dict with dram APs: gw [128,64], gu [8*16*128, 2048], dw [8*128, 16384].
    scr: h [128, 16*2048] bf16, guW [128, 2*2048] bf16, dwW [128, 16384] bf16,
         sg [128, 2*512] bf16, comb [128, 128] f32, st [128,96] f32 (reuse),
         pex [128, 8] f32 (exp p scratch), gwB [128, 64] bf16.
    psum: pgu 2x[128,1024], pd 2x[128,1024] (use scr['pgu0'],...)
    """
    nc = ctx.nc
    st = scr["st"]
    comb = scr["comb"]
    s_ld = ctx.sem("ffn_ld")
    s_gmm = ctx.sem("ffn_gmm")
    s_sg = ctx.sem("ffn_sg")
    s_h = ctx.sem("ffn_h")
    s_dmm = ctx.sem("ffn_dmm")
    s_acc = ctx.sem("ffn_acc")
    s_gu = ctx.sem("ffn_gu")
    s_dw = ctx.sem("ffn_dw")
    pgu = [scr["pgu0"], scr["pgu1"]]
    pd = [scr["pd0"], scr["pd1"]]

    s_rlog = ctx.sem("ffn_rlog")
    s_rmax = ctx.sem("ffn_rmax")
    s_rexp = ctx.sem("ffn_rexp")
    s_rdone = ctx.sem("ffn_rdone")
    with nc.Block() as b:
        @b.gpsimd
        def _(g):
            g.dma_start(out=scr["gwB"][:, 0:64], in_=aps["gw"][:, :]).then_inc(s_ld, 16)

        @b.tensor
        def _(t):
            t.wait_ge(s_ld, ctx.cnt["ffn_ld"] + 16)
            for ti in range(NT):
                if ti >= 2:
                    t.wait_ge(s_rdone, ti - 1)
                for kt in range(ND):
                    ins = t.matmul(pgu[ti % 2][:, 0:8],
                             xhatT[:, kt * S + ti * 128: kt * S + (ti + 1) * 128],
                             scr["gwB"][:, kt * 8:(kt + 1) * 8],
                             start=(kt == 0), stop=(kt == ND - 1))
                ins.then_inc(s_rlog, 1)

        @b.vector
        def _(v):
            for ti in range(NT):
                lg = pgu[ti % 2][:, 0:8]
                v.wait_ge(s_rlog, ti + 1)
                if rpart < 2:
                    v.tensor_copy(out=comb[:, ti * 8:(ti + 1) * 8], in_=lg).then_inc(s_rmax, 1)
                    v.nop().then_inc(s_rdone, 1)
                    continue
                v.tensor_reduce(out=st[:, 0:1], in_=lg, axis=AX.X, op=ALU.max)
                v.drain()
                v.tensor_scalar(out=st[:, 1:2], in0=st[:, 0:1], scalar1=-1.0,
                                scalar2=None, op0=ALU.mult).then_inc(s_rmax, 1)
                v.wait_ge(s_rexp, ti + 1)
                pex = scr["pex"][:, 0:8]
                if rpart < 3:
                    v.tensor_copy(out=comb[:, ti * 8:(ti + 1) * 8], in_=pex).then_inc(s_rdone, 1)
                    continue
                v.tensor_reduce(out=st[:, 2:3], in_=pex, axis=AX.X, op=ALU.max)
                v.drain()
                cslice = comb[:, ti * 8:(ti + 1) * 8]
                v.tensor_scalar(out=cslice, in0=pex, scalar1=st[:, 2:3],
                                scalar2=None, op0=ALU.is_ge)
                v.drain()
                v.scalar_tensor_tensor(out=cslice, in0=cslice, scalar=-1e30,
                                       in1=pex, op0=ALU.mult, op1=ALU.add)
                v.drain()
                v.tensor_reduce(out=st[:, 4:5], in_=cslice, axis=AX.X, op=ALU.max)
                v.drain()
                v.tensor_tensor(out=st[:, 5:6], in0=st[:, 2:3], in1=st[:, 4:5], op=ALU.add)
                v.drain()
                v.reciprocal(out=st[:, 6:7], in_=st[:, 5:6])
                v.drain()
                v.tensor_scalar(out=cslice, in0=pex, scalar1=st[:, 4:5],
                                scalar2=None, op0=ALU.is_ge)
                v.drain()
                v.tensor_tensor(out=cslice, in0=cslice, in1=pex, op=ALU.mult)
                v.drain()
                v.tensor_scalar(out=cslice, in0=cslice, scalar1=st[:, 6:7],
                                scalar2=None, op0=ALU.mult).then_inc(s_rdone, 1)

        @b.scalar
        def _(sc):
            for ti in range(NT):
                sc.wait_ge(s_rmax, ti + 1)
                if rpart < 2:
                    sc.nop().then_inc(s_rexp, 1)
                    continue
                sc.activation(out=scr["pex"][:, 0:8], in_=pgu[ti % 2][:, 0:8], func=AF.Exp,
                              bias=st[:, 1:2]).then_inc(s_rexp, 1)

    ctx.cnt["ffn_ld"] += 16

    if not experts:
        return

    E = 8
    with nc.Block() as b:
        @b.gpsimd
        def _(g):
            def load_pair(e, f):
                p = e * 16 + f
                if p >= 2:
                    g.wait_ge(s_gmm, ctx.cnt["ffn_gmm"] + (p - 1) * 4)
                g.dma_start(out=scr["guW"][:, (f % 2) * 2048:(f % 2 + 1) * 2048],
                            in_=aps["gu"][p * 128:(p + 1) * 128, :]).then_inc(s_gu, 16)

            for e in range(E):
                load_pair(e, 0)
                load_pair(e, 1)
                if e >= 1:
                    g.wait_ge(s_dmm, ctx.cnt["ffn_dmm"] + e * 16)
                g.dma_start(out=scr["dwW"][:],
                            in_=aps["dw"][e * 128:(e + 1) * 128, :]).then_inc(s_dw, 16)
                for f in range(2, 16):
                    load_pair(e, f)

        @b.tensor
        def _(t):
            for e in range(E):
                for f in range(16):
                    t.wait_ge(s_gu, ctx.cnt["ffn_gu"] + (e * 16 + f + 1) * 16)
                    for c in range(4):
                        cc = e * 64 + f * 4 + c
                        if cc >= 2:
                            t.wait_ge(s_h, ctx.cnt["ffn_h"] + cc - 1)
                        gub = (f % 2) * 2048
                        for kt in range(ND):
                            t.matmul(pgu[c % 2][:, 0:512],
                                     scr["guW"][:, gub + kt * 256: gub + kt * 256 + 128],
                                     xhatT[:, kt * S + c * 512: kt * S + (c + 1) * 512],
                                     start=(kt == 0), stop=(kt == ND - 1))
                        for kt in range(ND):
                            ins = t.matmul(pgu[c % 2][:, 512:1024],
                                     scr["guW"][:, gub + kt * 256 + 128: gub + kt * 256 + 256],
                                     xhatT[:, kt * S + c * 512: kt * S + (c + 1) * 512],
                                     start=(kt == 0), stop=(kt == ND - 1))
                        ins.then_inc(s_gmm, 1)
                t.wait_ge(s_h, ctx.cnt["ffn_h"] + (e + 1) * 64)
                t.wait_ge(s_dw, ctx.cnt["ffn_dw"] + (e + 1) * 16)
                for tt in range(16):
                    if e * 16 + tt >= 2:
                        t.wait_ge(s_acc, ctx.cnt["ffn_acc"] + e * 16 + tt - 1)
                    for ft in range(16):
                        for nn in range(2):
                            ins = t.matmul(pd[tt % 2][:, nn * 512:(nn + 1) * 512],
                                     scr["h"][:, ft * 2048 + tt * 128: ft * 2048 + (tt + 1) * 128],
                                     scr["dwW"][:, ft * 1024 + nn * 512: ft * 1024 + (nn + 1) * 512],
                                     start=(ft == 0), stop=(ft == 15))
                    ins.then_inc(s_dmm, 1)

        @b.scalar
        def _(sc):
            for e in range(E):
                for f in range(16):
                    for c in range(4):
                        cc = e * 64 + f * 4 + c
                        sc.wait_ge(s_gmm, ctx.cnt["ffn_gmm"] + cc + 1)
                        if cc >= 2:
                            sc.wait_ge(s_h, ctx.cnt["ffn_h"] + cc - 1)
                        sc.activation(out=scr["sg"][:, (c % 2) * 512:(c % 2 + 1) * 512],
                                      in_=pgu[c % 2][:, 0:512],
                                      func=AF.Silu).then_inc(s_sg, 1)

        @b.vector
        def _(v):
            for e in range(E):
                for f in range(16):
                    for c in range(4):
                        cc = e * 64 + f * 4 + c
                        v.wait_ge(s_sg, ctx.cnt["ffn_sg"] + cc + 1)
                        v.tensor_tensor(
                            out=scr["h"][:, f * 2048 + c * 512: f * 2048 + (c + 1) * 512],
                            in0=pgu[c % 2][:, 512:1024],
                            in1=scr["sg"][:, (c % 2) * 512:(c % 2 + 1) * 512],
                            op=ALU.mult).then_inc(s_h, 1)
                for tt in range(16):
                    v.wait_ge(s_dmm, ctx.cnt["ffn_dmm"] + e * 16 + tt + 1)
                    v.scalar_tensor_tensor(
                        out=x_res[:, tt * D:(tt + 1) * D],
                        in0=pd[tt % 2][:, :],
                        scalar=comb[:, tt * 8 + e: tt * 8 + e + 1],
                        in1=x_res[:, tt * D:(tt + 1) * D],
                        op0=ALU.mult, op1=ALU.add).then_inc(s_acc, 1)

    ctx.cnt["ffn_gu"] += E * 16 * 16
    ctx.cnt["ffn_dw"] += E * 16
    ctx.cnt["ffn_gmm"] += E * 64
    ctx.cnt["ffn_sg"] += E * 64
    ctx.cnt["ffn_h"] += E * 64
    ctx.cnt["ffn_dmm"] += E * 16
    ctx.cnt["ffn_acc"] += E * 16



def prep_l0ffn(inp):
    gw = np.ascontiguousarray(
        inp["L0_ffn_gate_w"].reshape(8, 128, 8).transpose(1, 0, 2).reshape(128, 64))
    guw = inp["L0_ffn_gate_up_w"]
    pairs = np.empty((8, 16, 128, 2048), dtype=np.float32)
    for f in range(16):
        blk = np.concatenate([guw[:, :, f * 128:(f + 1) * 128],
                              guw[:, :, 2048 + f * 128: 2048 + (f + 1) * 128]],
                             axis=2)
        pairs[:, f] = blk.reshape(8, 8, 128, 256).transpose(0, 2, 1, 3).reshape(8, 128, 2048)
    gu = pairs.reshape(8 * 16 * 128, 2048)
    dw = inp["L0_ffn_down_w"].reshape(8, 16, 128, 1024).transpose(0, 2, 1, 3).reshape(8 * 128, 16384)
    return {"gw": gw, "gu": np.ascontiguousarray(gu), "dw": np.ascontiguousarray(dw)}



def emit_l1ffn(ctx, x_res, xhatT, aps, scr, nexp=16):
    nc = ctx.nc
    st = scr["st"]
    comb = scr["comb"]
    pgu = [scr["pgu0"], scr["pgu1"]]
    pd = [scr["pd0"], scr["pd1"]]
    s_gu = ctx.sem("f1_gu")
    s_dw = ctx.sem("f1_dw")
    s_gmm = ctx.sem("f1_gmm")
    s_sg = ctx.sem("f1_sg")
    s_h = ctx.sem("f1_h")
    s_dmm = ctx.sem("f1_dmm")
    s_acc = ctx.sem("f1_acc")
    s_ld = ctx.sem("f1_ld")
    s_glog = ctx.sem("f1_glog")
    s_gsig = ctx.sem("f1_gsig")
    s_gdone = ctx.sem("f1_gdone")

    with nc.Block() as b:
        @b.gpsimd
        def _(g):
            g.dma_start(out=scr["gwB"][:, 0:128], in_=aps["gw"][:, :]).then_inc(s_ld, 16)

        @b.tensor
        def _(t):
            t.wait_ge(s_ld, 16)
            for ti in range(NT):
                if ti >= 2:
                    t.wait_ge(s_gdone, ti - 1)
                for kt in range(ND):
                    ins = t.matmul(pgu[ti % 2][:, 0:16],
                                   xhatT[:, kt * S + ti * 128: kt * S + (ti + 1) * 128],
                                   scr["gwB"][:, kt * 16:(kt + 1) * 16],
                                   start=(kt == 0), stop=(kt == ND - 1))
                ins.then_inc(s_glog, 1)

        @b.scalar
        def _(sc):
            for ti in range(NT):
                sc.wait_ge(s_glog, ti + 1)
                if ti >= 1:
                    sc.wait_ge(s_gdone, ti)
                sc.activation(out=scr["pex"][:, 0:16], in_=pgu[ti % 2][:, 0:16],
                              func=AF.Sigmoid).then_inc(s_gsig, 1)

        @b.vector
        def _(v):
            for ti in range(NT):
                v.wait_ge(s_gsig, ti + 1)
                gs = scr["pex"][:, 0:16]
                cs = comb[:, ti * 16:(ti + 1) * 16]
                v.tensor_copy(out=cs, in_=gs)
                v.drain()
                for k in range(4):
                    v.tensor_reduce(out=st[:, 9:10], in_=cs, axis=AX.X, op=ALU.max)
                    v.drain()
                    if k == 0:
                        v.tensor_copy(out=st[:, 8:9], in_=st[:, 9:10])
                    else:
                        v.tensor_tensor(out=st[:, 8:9], in0=st[:, 8:9], in1=st[:, 9:10],
                                        op=ALU.add)
                    v.drain()
                    if k < 3:
                        v.tensor_scalar(out=st[:, 10:26], in0=cs, scalar1=st[:, 9:10],
                                        scalar2=None, op0=ALU.is_ge)
                        v.drain()
                        v.scalar_tensor_tensor(out=cs, in0=st[:, 10:26], scalar=-1e30,
                                               in1=cs, op0=ALU.mult, op1=ALU.add)
                        v.drain()
                v.reciprocal(out=st[:, 8:9], in_=st[:, 8:9])
                v.drain()
                v.tensor_scalar(out=cs, in0=gs, scalar1=st[:, 9:10],
                                scalar2=None, op0=ALU.is_ge)
                v.drain()
                v.tensor_tensor(out=cs, in0=cs, in1=gs, op=ALU.mult)
                v.drain()
                v.tensor_scalar(out=cs, in0=cs, scalar1=st[:, 8:9],
                                scalar2=None, op0=ALU.mult).then_inc(s_gdone, 1)

    with nc.Block() as b:
        @b.gpsimd
        def _(g):
            def load_pair(p, src_row):
                if p >= 2:
                    g.wait_ge(s_gmm, (p - 1) * 4)
                g.dma_start(out=scr["guW"][:, (p % 2) * 2048:(p % 2 + 1) * 2048],
                            in_=aps[src_row[0]][src_row[1] * 128:(src_row[1] + 1) * 128, :]
                            ).then_inc(s_gu, 16)

            pairs = [("shp", f) for f in range(8)]
            for e in range(nexp):
                pairs += [("ep", e * 4 + f) for f in range(4)]
            pi = 0
            g.dma_start(out=scr["dwW"][:, 0:8192],
                        in_=aps["sho"][:, :]).then_inc(s_dw, 16)
            for f in range(8):
                load_pair(pi, pairs[pi]); pi += 1
            for e in range(nexp):
                if e >= 2:
                    g.wait_ge(s_dmm, (e - 1) * NT + NT)
                g.dma_start(out=scr["dwW"][:, 8192 + (e % 2) * 4096: 8192 + (e % 2 + 1) * 4096],
                            in_=aps["eo"][e * 128:(e + 1) * 128, :]).then_inc(s_dw, 16)
                for f in range(4):
                    load_pair(pi, pairs[pi]); pi += 1

        @b.tensor
        def _(t):
            pcnt = 0
            ccnt = 0
            dcnt = 0
            for f in range(8):
                t.wait_ge(s_gu, (pcnt + 1) * 16)
                for c in range(4):
                    if ccnt >= 2:
                        t.wait_ge(s_h, ccnt - 1)
                    gub = (pcnt % 2) * 2048
                    for kt in range(ND):
                        t.matmul(pgu[c % 2][:, 0:512],
                                 scr["guW"][:, gub + kt * 256: gub + kt * 256 + 128],
                                 xhatT[:, kt * S + c * 512: kt * S + (c + 1) * 512],
                                 start=(kt == 0), stop=(kt == ND - 1))
                    for kt in range(ND):
                        ins = t.matmul(pgu[c % 2][:, 512:1024],
                                       scr["guW"][:, gub + kt * 256 + 128: gub + kt * 256 + 256],
                                       xhatT[:, kt * S + c * 512: kt * S + (c + 1) * 512],
                                       start=(kt == 0), stop=(kt == ND - 1))
                    ins.then_inc(s_gmm, 1)
                    ccnt += 1
                pcnt += 1
            t.wait_ge(s_h, ccnt)
            t.wait_ge(s_dw, 16)
            for tt in range(NT):
                if tt >= 2:
                    t.wait_ge(s_acc, dcnt + tt - 1)
                for ft in range(8):
                    for nn in range(2):
                        ins = t.matmul(pd[tt % 2][:, nn * 512:(nn + 1) * 512],
                                       scr["h"][:, ft * 2048 + tt * 128: ft * 2048 + (tt + 1) * 128],
                                       scr["dwW"][:, ft * 1024 + nn * 512: ft * 1024 + (nn + 1) * 512],
                                       start=(ft == 0), stop=(ft == 7))
                ins.then_inc(s_dmm, 1)
            dcnt += NT
            for e in range(nexp):
                t.wait_ge(s_dw, 32 + e * 16)
                for f in range(4):
                    t.wait_ge(s_gu, (pcnt + 1) * 16)
                    for c in range(4):
                        if ccnt >= 2:
                            t.wait_ge(s_h, ccnt - 1)
                        gub = (pcnt % 2) * 2048
                        for kt in range(ND):
                            t.matmul(pgu[c % 2][:, 0:512],
                                     scr["guW"][:, gub + kt * 256: gub + kt * 256 + 128],
                                     xhatT[:, kt * S + c * 512: kt * S + (c + 1) * 512],
                                     start=(kt == 0), stop=(kt == ND - 1))
                        for kt in range(ND):
                            ins = t.matmul(pgu[c % 2][:, 512:1024],
                                           scr["guW"][:, gub + kt * 256 + 128: gub + kt * 256 + 256],
                                           xhatT[:, kt * S + c * 512: kt * S + (c + 1) * 512],
                                           start=(kt == 0), stop=(kt == ND - 1))
                        ins.then_inc(s_gmm, 1)
                        ccnt += 1
                    pcnt += 1
                t.wait_ge(s_h, ccnt)
                for tt in range(NT):
                    if dcnt + tt >= 2:
                        t.wait_ge(s_acc, dcnt + tt - 1)
                    ob = 8192 + (e % 2) * 4096
                    for ft in range(4):
                        for nn in range(2):
                            ins = t.matmul(pd[tt % 2][:, nn * 512:(nn + 1) * 512],
                                           scr["h"][:, ft * 2048 + tt * 128: ft * 2048 + (tt + 1) * 128],
                                           scr["dwW"][:, ob + ft * 1024 + nn * 512: ob + ft * 1024 + (nn + 1) * 512],
                                           start=(ft == 0), stop=(ft == 3))
                    ins.then_inc(s_dmm, 1)
                dcnt += NT

        @b.scalar
        def _(sc):
            ccnt = 0
            for blk in range(1 + nexp):
                npair = 8 if blk == 0 else 4
                for f in range(npair):
                    for c in range(4):
                        sc.wait_ge(s_gmm, ccnt + 1)
                        if ccnt >= 2:
                            sc.wait_ge(s_h, ccnt - 1)
                        sc.activation(out=scr["sg"][:, (c % 2) * 512:(c % 2 + 1) * 512],
                                      in_=pgu[c % 2][:, 0:512],
                                      func=AF.Silu).then_inc(s_sg, 1)
                        ccnt += 1

        @b.vector
        def _(v):
            ccnt = 0
            dcnt = 0
            for blk in range(1 + nexp):
                npair = 8 if blk == 0 else 4
                for f in range(npair):
                    for c in range(4):
                        v.wait_ge(s_sg, ccnt + 1)
                        v.tensor_tensor(
                            out=scr["h"][:, f * 2048 + c * 512: f * 2048 + (c + 1) * 512],
                            in0=pgu[c % 2][:, 512:1024],
                            in1=scr["sg"][:, (c % 2) * 512:(c % 2 + 1) * 512],
                            op=ALU.mult).then_inc(s_h, 1)
                        ccnt += 1
                for tt in range(NT):
                    v.wait_ge(s_dmm, dcnt + tt + 1)
                    if blk == 0:
                        v.tensor_tensor(out=x_res[:, tt * D:(tt + 1) * D],
                                        in0=pd[tt % 2][:, :],
                                        in1=x_res[:, tt * D:(tt + 1) * D],
                                        op=ALU.add).then_inc(s_acc, 1)
                    else:
                        e = blk - 1
                        v.scalar_tensor_tensor(
                            out=x_res[:, tt * D:(tt + 1) * D],
                            in0=pd[tt % 2][:, :],
                            scalar=comb[:, tt * 16 + e: tt * 16 + e + 1],
                            in1=x_res[:, tt * D:(tt + 1) * D],
                            op0=ALU.mult, op1=ALU.add).then_inc(s_acc, 1)
                dcnt += NT


def prep_l1ffn(inp):
    gw = np.ascontiguousarray(
        inp["L1_ffn_gate_w"].T.reshape(8, 128, 16).transpose(1, 0, 2).reshape(128, 128))
    wsi, vsi = inp["L1_ffn_ws_in"], inp["L1_ffn_vs_in"]
    shp = np.empty((8, 128, 2048), np.float32)
    for f in range(8):
        blk = np.concatenate([wsi[:, f * 128:(f + 1) * 128],
                              vsi[:, f * 128:(f + 1) * 128]], axis=1)
        shp[f] = blk.reshape(8, 128, 256).transpose(1, 0, 2).reshape(128, 2048)
    sho = inp["L1_ffn_ws_out"].reshape(8, 128, 1024).transpose(1, 0, 2).reshape(128, 8192)
    ewin, evin = inp["L1_ffn_e_win"], inp["L1_ffn_e_vin"]
    ep = np.empty((16, 4, 128, 2048), np.float32)
    for e in range(16):
        for f in range(4):
            blk = np.concatenate([ewin[e][:, f * 128:(f + 1) * 128],
                                  evin[e][:, f * 128:(f + 1) * 128]], axis=1)
            ep[e, f] = blk.reshape(8, 128, 256).transpose(1, 0, 2).reshape(128, 2048)
    eo = inp["L1_ffn_e_wout"].reshape(16, 4, 128, 1024).transpose(0, 2, 1, 3).reshape(16 * 128, 4096)
    return {"gw": gw, "shp": np.ascontiguousarray(shp.reshape(8 * 128, 2048)),
            "sho": np.ascontiguousarray(sho),
            "ep": np.ascontiguousarray(ep.reshape(16 * 4 * 128, 2048)),
            "eo": np.ascontiguousarray(eo)}



def emit_l1attn(ctx, x_res, xhatT, aps, scr):
    nc = ctx.nc
    st = scr["st"]
    u1 = scr["u1"]
    I128 = scr["cst"][:, 0:128]
    maskB = scr["cst"][:, 128:256]
    permB = scr["cst"][:, 256:384]
    ones = scr["ones"]
    s1 = scr["ps_s1"]
    s2 = scr["ps_s2"]
    G = 2
    NG = 16 // G

    s_ld = ctx.sem("a1_ld")
    s_vmm = ctx.sem("a1_vmm")
    s_vev = ctx.sem("a1_vev")
    s_lam = ctx.sem("a1_lam")

    with nc.Block() as b:
        @b.gpsimd
        def _(g):
            for key, dst in [("C", scr["C"][:]), ("Sg", scr["Sg"][:]),
                             ("cst", scr["cst"][:]), ("ones", scr["ones"][:]),
                             ("bq", scr["bqB"][:]), ("bk", scr["bkB"][:]),
                             ("bv", scr["bvB"][:]), ("bo", scr["boB"][:]),
                             ("lqk", scr["lqk"][0:16, :]), ("hn", scr["hnB"][:]),
                             ("wv", u1[:])]:
                g.dma_start(out=dst, in_=aps[key][:, :]).then_inc(s_ld, 16)

        @b.tensor
        def _(t):
            t.wait_ge(s_ld, ctx.cnt["a1_ld"] + 11 * 16)
            pb = [s1[:, 0:512], s1[:, 512:1024], s2[:, 0:512], s2[:, 512:1024]]
            for tt in range(NT):
                for nn in range(2):
                    idx = tt * 2 + nn
                    if idx >= 4:
                        t.wait_ge(s_vev, ctx.cnt["a1_vev"] + idx - 3)
                    out = pb[idx % 4]
                    for kt in range(ND):
                        t.matmul(out, xhatT[:, kt * S + tt * 128:kt * S + (tt + 1) * 128],
                                 u1[:, kt * 1024 + nn * 512: kt * 1024 + (nn + 1) * 512],
                                 start=(kt == 0), stop=False)
                    ins = t.matmul(out, ones[0:1, 0:128],
                                   scr["bvB"][0:1, nn * 512:(nn + 1) * 512],
                                   start=False, stop=True)
                    ins.then_inc(s_vmm, 1)

        @b.vector
        def _(v):
            pb = [s1[:, 0:512], s1[:, 512:1024], s2[:, 0:512], s2[:, 512:1024]]
            for tt in range(NT):
                for nn in range(2):
                    idx = tt * 2 + nn
                    v.wait_ge(s_vmm, ctx.cnt["a1_vmm"] + idx + 1)
                    v.tensor_copy(out=scr["v_tok"][:, tt * 1024 + nn * 512: tt * 1024 + (nn + 1) * 512],
                                  in_=pb[idx % 4]).then_inc(s_vev, 1)
            lqk = scr["lqk"]
            v.tensor_tensor(out=st[0:16, 32:96], in0=lqk[0:16, 0:64],
                            in1=lqk[0:16, 64:128], op=ALU.mult)
            v.tensor_reduce(out=st[0:16, 0:1], in_=st[0:16, 32:96], axis=AX.X, op=ALU.add)
            v.drain()
            v.tensor_tensor(out=st[0:16, 32:96], in0=lqk[0:16, 128:192],
                            in1=lqk[0:16, 192:256], op=ALU.mult)
            v.tensor_reduce(out=st[0:16, 1:2], in_=st[0:16, 32:96], axis=AX.X, op=ALU.add)
            v.drain()
            v.nop().then_inc(s_lam, 1)

    ctx.cnt["a1_ld"] += 11 * 16
    ctx.cnt["a1_vmm"] += 32
    ctx.cnt["a1_vev"] += 32

    with nc.Block() as b:
        @b.scalar
        def _(sc):
            sc.wait_ge(s_lam, 1)
            sc.activation(out=st[0:16, 2:3], in_=st[0:16, 0:1], func=AF.Exp)
            sc.activation(out=st[0:16, 3:4], in_=st[0:16, 1:2], func=AF.Exp)

    with nc.Block() as b:
        @b.vector
        def _(v):
            v.tensor_tensor(out=st[0:16, 4:5], in0=st[0:16, 2:3], in1=st[0:16, 3:4],
                            op=ALU.subtract)
            v.drain()
            v.tensor_scalar(out=st[0:16, 4:5], in0=st[0:16, 4:5], scalar1=0.2,
                            scalar2=None, op0=ALU.add)

    lam_d = nc.dram_tensor("lam_scratch", [16], F32, kind="Internal").ap()
    with nc.Block() as b:
        @b.sync
        def _(sy):
            sy.dma_start(out=lam_d[:], in_=st[0:16, 4:5]).then_inc(s_ld, 16)
            sy.wait_ge(s_ld, ctx.cnt["a1_ld"] + 16)

    with nc.Block() as b:
        @b.gpsimd
        def _(g):
            g.dma_start(out=scr["lamB"][:],
                        in_=lam_d[:].unsqueeze(0).broadcast_to([128, 16])).then_inc(s_ld, 16)
            g.wait_ge(s_ld, ctx.cnt["a1_ld"] + 32)
    ctx.cnt["a1_ld"] += 32

    s_pmm = ctx.sem("a1_pmm")
    s_pev = ctx.sem("a1_pev")
    s_rmm = ctx.sem("a1_rmm")
    s_rdv = ctx.sem("a1_rdv")
    s_qkc = ctx.sem("a1_qkc")
    s_smm = ctx.sem("a1_smm")
    s_mx = ctx.sem("a1_mx")
    s_pexp = ctx.sem("a1_pexp")
    s_att = ctx.sem("a1_att")
    s_trD = ctx.sem("a1_trD")
    s_av = ctx.sem("a1_av")
    s_oev = ctx.sem("a1_oev")
    s_wld = ctx.sem("a1_wld")
    stg = None

    for g_i in range(NG):
        stg = [scr["e1"], scr["e2"], scr["aS"], scr["aT"]]
        with nc.Block() as b:
            @b.gpsimd
            def _(g):
                g.dma_start(out=u1[:, 0:2048],
                            in_=aps["wq"][g_i * 128:(g_i + 1) * 128, :]).then_inc(s_wld, 16)
                g.dma_start(out=u1[:, 2048:4096],
                            in_=aps["wk"][g_i * 128:(g_i + 1) * 128, :]).then_inc(s_wld, 16)

            @b.tensor
            def _(t):
                t.wait_ge(s_wld, ctx.cnt["a1_wld"] + 32)
                pb = [s1[:, 0:512], s1[:, 512:1024], s1[:, 1024:1536], s1[:, 1536:2048],
                      s2[:, 0:512], s2[:, 512:1024], s2[:, 1024:1536], scr["ps_o"][:, 0:512]]
                cnt = 0
                for qk in range(2):
                    wsl = u1[:, 0:2048] if qk == 0 else u1[:, 2048:4096]
                    for j in range(G):
                        for c in range(4):
                            if cnt >= 8:
                                t.wait_ge(s_pev, ctx.cnt["a1_pev"] + cnt - 7)
                            out = pb[cnt % 8]
                            for kt in range(ND):
                                ins = t.matmul(
                                    out,
                                    wsl[:, kt * 256 + j * 128: kt * 256 + (j + 1) * 128],
                                    xhatT[:, kt * S + c * 512: kt * S + (c + 1) * 512],
                                    start=(kt == 0), stop=(kt == ND - 1))
                            ins.then_inc(s_pmm, 1)
                            cnt += 1

            @b.scalar
            def _(sc):
                pb = [s1[:, 0:512], s1[:, 512:1024], s1[:, 1024:1536], s1[:, 1536:2048],
                      s2[:, 0:512], s2[:, 512:1024], s2[:, 1024:1536], scr["ps_o"][:, 0:512]]
                cnt = 0
                for qk in range(2):
                    bias = scr["bqB"] if qk == 0 else scr["bkB"]
                    for j in range(G):
                        dst = stg[qk * G + j]
                        for c in range(4):
                            sc.wait_ge(s_pmm, ctx.cnt["a1_pmm"] + cnt + 1)
                            sc.activation(out=dst[:, c * 512:(c + 1) * 512],
                                          in_=pb[cnt % 8], func=AF.Identity,
                                          bias=bias[:, g_i * G + j: g_i * G + j + 1]
                                          ).then_inc(s_pev, 1)
                            cnt += 1

        ctx.cnt["a1_wld"] += 32
        ctx.cnt["a1_pmm"] += 16
        ctx.cnt["a1_pev"] += 16

        with nc.Block() as b:
            @b.tensor
            def _(t):
                pb = [s1[:, 0:512], s1[:, 512:1024], s1[:, 1024:1536], s1[:, 1536:2048],
                      s2[:, 0:512], s2[:, 512:1024], s2[:, 1024:1536], scr["ps_o"][:, 0:512]]
                cnt = 0
                for sidx in range(4):
                    src = stg[sidx]
                    for c in range(4):
                        if cnt >= 8:
                            t.wait_ge(s_rdv, ctx.cnt["a1_rdv"] + cnt - 7)
                        ins = t.matmul(pb[cnt % 8], permB, src[:, c * 512:(c + 1) * 512],
                                       start=True, stop=True)
                        ins.then_inc(s_rmm, 1)
                        cnt += 1

            @b.vector
            def _(v):
                pb = [s1[:, 0:512], s1[:, 512:1024], s1[:, 1024:1536], s1[:, 1536:2048],
                      s2[:, 0:512], s2[:, 512:1024], s2[:, 1024:1536], scr["ps_o"][:, 0:512]]
                cnt = 0
                for sidx in range(4):
                    src = stg[sidx]
                    for c in range(4):
                        sl = slice(c * 512, (c + 1) * 512)
                        v.wait_ge(s_rmm, ctx.cnt["a1_rmm"] + cnt + 1)
                        v.tensor_tensor(out=src[:, sl], in0=src[:, sl],
                                        in1=scr["C"][:, sl], op=ALU.mult)
                        v.tensor_tensor(out=scr["rs"][:, 0:512], in0=pb[cnt % 8],
                                        in1=scr["Sg"][:, sl], op=ALU.mult)
                        v.tensor_tensor(out=src[:, sl], in0=src[:, sl],
                                        in1=scr["rs"][:, 0:512],
                                        op=ALU.add).then_inc(s_rdv, 1)
                        cnt += 1
                for sidx in range(4):
                    v.tensor_copy(out=u1[:, sidx * 2048:(sidx + 1) * 2048],
                                  in_=stg[sidx][:]).then_inc(s_qkc, 1)

        ctx.cnt["a1_rmm"] += 16
        ctx.cnt["a1_rdv"] += 16
        ctx.cnt["a1_qkc"] += 4

        prs = [(r, j) for r in range(NT) for j in range(G)]
        with nc.Block() as b:
            @b.tensor
            def _(t):
                for idx, (r, j) in enumerate(prs):
                    h = g_i * G + j
                    q = u1[:, j * 2048:(j + 1) * 2048]
                    k = u1[:, (2 + j) * 2048:(3 + j) * 2048]
                    vc = (r + 1) * 128
                    nch = (vc + 511) // 512
                    if idx >= 1:
                        t.wait_ge(s_pexp, ctx.cnt["a1_pexp"] + 2 * idx - 1)
                    for c in range(nch):
                        c0, c1 = c * 512, min((c + 1) * 512, vc)
                        last = (c == nch - 1)
                        t.matmul(s1[:, c0:c1], q[0:64, r * 128:(r + 1) * 128],
                                 k[0:64, c0:c1], start=True, stop=not last)
                        if last:
                            ins = t.matmul(s1[:, vc - 128:vc], I128, maskB,
                                           start=False, stop=True)
                    ins.then_inc(s_smm, 1)
                    if idx >= 1:
                        t.wait_ge(s_pexp, ctx.cnt["a1_pexp"] + 2 * idx)
                    if vc > 1536:
                        t.wait_ge(s_pexp, ctx.cnt["a1_pexp"] + 2 * idx + 1)
                    def s2sl(c0, c1):
                        if c0 >= 1536:
                            return s1[:, c0:c1]
                        return s2[:, c0:c1]
                    for c in range(nch):
                        c0, c1 = c * 512, min((c + 1) * 512, vc)
                        last = (c == nch - 1)
                        t.matmul(s2sl(c0, c1), q[64:128, r * 128:(r + 1) * 128],
                                 k[64:128, c0:c1], start=True, stop=not last)
                        if last:
                            ins = t.matmul(s2sl(vc - 128, vc), I128, maskB,
                                           start=False, stop=True)
                    ins.then_inc(s_smm, 1)
                    t.wait_ge(s_trD, ctx.cnt["a1_trD"] + (idx + 1) * 16)
                    t.wait_ge(s_oev, ctx.cnt["a1_oev"] + idx)
                    tb = scr["aT"] if idx % 2 == 0 else scr["aS"]
                    for c in range(r + 1):
                        ins = t.matmul(scr["ps_o"][:, 0:64],
                                       tb[:, c * 128:(c + 1) * 128],
                                       scr["v_tok"][:, c * 1024 + h * 64: c * 1024 + h * 64 + 64],
                                       start=(c == 0), stop=(c == r))
                    ins.then_inc(s_av, 1)

            @b.vector
            def _(v):
                def oev(pidx):
                    pr, pj = prs[pidx]
                    ph = g_i * G + pj
                    v.wait_ge(s_av, ctx.cnt["a1_av"] + pidx + 1)
                    osl = scr["o_tok"][:, pr * 1024 + ph * 64: pr * 1024 + ph * 64 + 64]
                    v.tensor_copy(out=osl, in_=scr["ps_o"][:, 0:64]).then_inc(s_oev, 1)
                    v.drain()
                    v.tensor_tensor(out=scr["rs"][:, 0:64], in0=osl, in1=osl,
                                    op=ALU.mult)
                    v.drain()
                    v.tensor_reduce(out=scr["ssqB"][:, pr * 16 + ph: pr * 16 + ph + 1],
                                    in_=scr["rs"][:, 0:64], axis=AX.X, op=ALU.add)
                    v.drain()

                for idx, (r, j) in enumerate(prs):
                    h = g_i * G + j
                    vc = (r + 1) * 128
                    v.wait_ge(s_smm, ctx.cnt["a1_smm"] + 2 * idx + 1)
                    v.tensor_reduce(out=st[:, 8:9], in_=s1[:, 0:vc], axis=AX.X,
                                    op=ALU.max)
                    v.drain()
                    v.tensor_scalar(out=st[:, 9:10], in0=st[:, 8:9], scalar1=-0.125,
                                    scalar2=None, op0=ALU.mult).then_inc(s_mx, 1)
                    if idx >= 1:
                        oev(idx - 1)
                    v.wait_ge(s_smm, ctx.cnt["a1_smm"] + 2 * idx + 2)
                    v.tensor_reduce(out=st[:, 10:11], in_=s2[:, 0:min(vc, 1536)],
                                    axis=AX.X, op=ALU.max)
                    v.drain()
                    if vc > 1536:
                        v.tensor_reduce(out=st[:, 20:21], in_=s1[:, 1536:vc],
                                        axis=AX.X, op=ALU.max)
                        v.drain()
                        v.tensor_tensor(out=st[:, 10:11], in0=st[:, 10:11],
                                        in1=st[:, 20:21], op=ALU.max)
                    v.tensor_scalar(out=st[:, 11:12], in0=st[:, 10:11], scalar1=-0.125,
                                    scalar2=None, op0=ALU.mult).then_inc(s_mx, 1)
                    v.wait_ge(s_pexp, ctx.cnt["a1_pexp"] + 2 * idx + 2)
                    if vc > 1536:
                        v.tensor_tensor(out=st[:, 13:14], in0=st[:, 13:14],
                                        in1=st[:, 21:22], op=ALU.add)
                        v.drain()
                    v.reciprocal(out=st[:, 14:15], in_=st[:, 12:13])
                    v.reciprocal(out=st[:, 15:16], in_=st[:, 13:14])
                    v.drain()
                    v.tensor_tensor(out=st[:, 16:17], in0=st[:, 15:16],
                                    in1=scr["lamB"][:, h:h + 1], op=ALU.mult)
                    v.drain()
                    cb = scr["e1"] if idx % 2 == 0 else scr["e2"]
                    v.tensor_scalar(out=scr["e2"][:, 0:vc],
                                    in0=scr["e2"][:, 0:vc], scalar1=st[:, 16:17],
                                    scalar2=None, op0=ALU.mult)
                    v.scalar_tensor_tensor(out=cb[:, 0:vc],
                                           in0=scr["e1"][:, 0:vc],
                                           scalar=st[:, 14:15],
                                           in1=scr["e2"][:, 0:vc],
                                           op0=ALU.mult,
                                           op1=ALU.subtract).then_inc(s_att, 1)
                oev(len(prs) - 1)

            @b.scalar
            def _(sc):
                for idx, (r, j) in enumerate(prs):
                    vc = (r + 1) * 128
                    sc.wait_ge(s_mx, ctx.cnt["a1_mx"] + 2 * idx + 1)
                    if idx >= 1:
                        sc.wait_ge(s_trD, ctx.cnt["a1_trD"] + idx * 16)
                    sc.activation(out=scr["e1"][:, 0:vc], in_=s1[:, 0:vc],
                                  func=AF.Exp, bias=st[:, 9:10], scale=0.125,
                                  accum_out=st[:, 12:13]).then_inc(s_pexp, 1)
                    sc.wait_ge(s_mx, ctx.cnt["a1_mx"] + 2 * idx + 2)
                    if vc <= 1536:
                        sc.activation(out=scr["e2"][:, 0:vc], in_=s2[:, 0:vc],
                                      func=AF.Exp, bias=st[:, 11:12], scale=0.125,
                                      accum_out=st[:, 13:14]).then_inc(s_pexp, 1)
                    else:
                        sc.activation(out=scr["e2"][:, 0:1536], in_=s2[:, 0:1536],
                                      func=AF.Exp, bias=st[:, 11:12], scale=0.125,
                                      accum_out=st[:, 13:14])
                        sc.activation(out=scr["e2"][:, 1536:vc], in_=s1[:, 1536:vc],
                                      func=AF.Exp, bias=st[:, 11:12], scale=0.125,
                                      accum_out=st[:, 21:22]).then_inc(s_pexp, 1)

            @b.sync
            def _(sy):
                for idx, (r, j) in enumerate(prs):
                    vc = (r + 1) * 128
                    sy.wait_ge(s_att, ctx.cnt["a1_att"] + idx + 1)
                    if idx >= 2:
                        sy.wait_ge(s_av, ctx.cnt["a1_av"] + idx - 1)
                    cb = scr["e1"] if idx % 2 == 0 else scr["e2"]
                    tb = scr["aT"] if idx % 2 == 0 else scr["aS"]
                    sy.dma_start(
                        out=tb[:].rearrange("p (c j) -> p c j", c=NT)[:, 0:r + 1, :],
                        in_=cb[:, 0:vc],
                        transpose=True).then_inc(s_trD, 16)

        ctx.cnt["a1_smm"] += 2 * G * NT
        ctx.cnt["a1_mx"] += 2 * G * NT
        ctx.cnt["a1_pexp"] += 2 * G * NT
        ctx.cnt["a1_att"] += G * NT
        ctx.cnt["a1_trD"] += G * NT * 16
        ctx.cnt["a1_av"] += G * NT
        ctx.cnt["a1_oev"] += G * NT

    with nc.Block() as b:
        @b.vector
        def _(v):
            v.tensor_scalar(out=scr["ssqB"][:], in0=scr["ssqB"][:], scalar1=1.0 / 64,
                            scalar2=1e-6, op0=ALU.mult, op1=ALU.add)

    with nc.Block() as b:
        @b.scalar
        def _(sc):
            sc.activation(out=scr["ssqB"][:], in_=scr["ssqB"][:], func=AF.Sqrt)

    with nc.Block() as b:
        @b.vector
        def _(v):
            v.reciprocal(out=scr["ssqB"][:], in_=scr["ssqB"][:])
            v.drain()
            for r in range(NT):
                for h in range(16):
                    osl = scr["o_tok"][:, r * 1024 + h * 64: r * 1024 + h * 64 + 64]
                    v.tensor_scalar(out=osl, in0=osl,
                                    scalar1=scr["ssqB"][:, r * 16 + h: r * 16 + h + 1],
                                    scalar2=None, op0=ALU.mult)

    s_otr = ctx.sem("a1_otr")
    oT = scr["oT"]
    with nc.Block() as b:
        @b.sync
        def _(sy):
            for tt in range(NT):
                sy.dma_start(
                    out=oT[:].rearrange("p (k s) -> p k s", k=ND)[:, :, tt * 128:(tt + 1) * 128],
                    in_=scr["o_tok"][:, tt * 1024:(tt + 1) * 1024],
                    transpose=True).then_inc(s_otr, 16)
            sy.wait_ge(s_otr, ctx.cnt["a1_otr"] + NT * 16)

        @b.gpsimd
        def _(g):
            g.dma_start(out=u1[:], in_=aps["wo"][:, :]).then_inc(s_wld, 16)
            g.wait_ge(s_wld, ctx.cnt["a1_wld"] + 16)
    ctx.cnt["a1_otr"] += NT * 16
    ctx.cnt["a1_wld"] += 16

    with nc.Block() as b:
        @b.vector
        def _(v):
            for kt in range(ND):
                v.tensor_scalar(out=u1[:, kt * 1024:(kt + 1) * 1024],
                                in0=u1[:, kt * 1024:(kt + 1) * 1024],
                                scalar1=scr["hnB"][:, kt:kt + 1],
                                scalar2=0.8, op0=ALU.mult, op1=ALU.mult)

    s_omm = ctx.sem("a1_omm")
    s_oac = ctx.sem("a1_oac")
    with nc.Block() as b:
        @b.tensor
        def _(t):
            pb = [s1[:, 0:512], s1[:, 512:1024], s2[:, 0:512], s2[:, 512:1024]]
            for tt in range(NT):
                for nn in range(2):
                    idx = tt * 2 + nn
                    if idx >= 4:
                        t.wait_ge(s_oac, ctx.cnt["a1_oac"] + idx - 3)
                    out = pb[idx % 4]
                    for kt in range(ND):
                        t.matmul(out, oT[:, kt * S + tt * 128: kt * S + (tt + 1) * 128],
                                 u1[:, kt * 1024 + nn * 512: kt * 1024 + (nn + 1) * 512],
                                 start=(kt == 0), stop=False)
                    ins = t.matmul(out, ones[0:1, 0:128],
                                   scr["boB"][0:1, nn * 512:(nn + 1) * 512],
                                   start=False, stop=True)
                    ins.then_inc(s_omm, 1)

        @b.vector
        def _(v):
            pb = [s1[:, 0:512], s1[:, 512:1024], s2[:, 0:512], s2[:, 512:1024]]
            for tt in range(NT):
                for nn in range(2):
                    idx = tt * 2 + nn
                    v.wait_ge(s_omm, ctx.cnt["a1_omm"] + idx + 1)
                    v.tensor_tensor(out=x_res[:, tt * D + nn * 512: tt * D + (nn + 1) * 512],
                                    in0=pb[idx % 4],
                                    in1=x_res[:, tt * D + nn * 512: tt * D + (nn + 1) * 512],
                                    op=ALU.add).then_inc(s_oac, 1)
    ctx.cnt["a1_omm"] += 32
    ctx.cnt["a1_oac"] += 32


def prep_l1attn(inp):
    wqT = inp["L1_attn_Wq"].T
    wkT = inp["L1_attn_Wk"].T
    wq = np.empty((8, 128, 2048), np.float32)
    wk = np.empty((8, 128, 2048), np.float32)
    for g in range(8):
        blk = wqT[:, g * 256:(g + 1) * 256]
        wq[g] = blk.reshape(8, 128, 256).transpose(1, 0, 2).reshape(128, 2048)
        blk = wkT[:, g * 256:(g + 1) * 256]
        wk[g] = blk.reshape(8, 128, 256).transpose(1, 0, 2).reshape(128, 2048)
    wv = inp["L1_attn_Wv"].T.reshape(8, 128, 1024).transpose(1, 0, 2).reshape(128, 8192)
    wo = inp["L1_attn_Wo"].T.reshape(8, 128, 1024).transpose(1, 0, 2).reshape(128, 8192)
    bq = np.ascontiguousarray(inp["L1_attn_bq"].reshape(16, 128).T)
    bk = np.ascontiguousarray(inp["L1_attn_bk"].reshape(16, 128).T)
    bv = inp["L1_attn_bv"].reshape(1, 1024)
    bo = inp["L1_attn_bo"].reshape(1, 1024)
    half = 32
    inv = 1.0 / 10000.0 ** (np.arange(half, dtype=np.float64) / half)
    ang = np.arange(S, dtype=np.float64)[:, None] * inv[None, :]
    cos = np.cos(ang).astype(np.float32).T
    sin = np.sin(ang).astype(np.float32).T
    C = np.empty((128, S), np.float32)
    Sg = np.empty((128, S), np.float32)
    for j in range(128):
        C[j] = cos[j % 32]
        Sg[j] = (-sin[j % 32]) if (j % 64) < 32 else sin[(j % 64) - 32]
    cst = np.zeros((128, 384), np.float32)
    cst[:, 0:128] = np.eye(128)
    m = np.zeros((128, 128), np.float32)
    iu = np.triu_indices(128, 1)
    m[iu] = -1e30
    cst[:, 128:256] = m
    pm = np.zeros((128, 128), np.float32)
    for i in range(128):
        j = i + 32 if (i % 64) < 32 else i - 32
        pm[i, j] = 1.0
    cst[:, 256:384] = pm
    ones = np.ones((1, 128), np.float32)
    lqk = np.concatenate([inp["L1_attn_lq1"], inp["L1_attn_lk1"],
                          inp["L1_attn_lq2"], inp["L1_attn_lk2"]], axis=1)
    hn = np.ascontiguousarray(inp["L1_attn_head_norm_w"].reshape(1024).reshape(8, 128).T)
    return {"wq": np.ascontiguousarray(wq.reshape(8 * 128, 2048)),
            "wk": np.ascontiguousarray(wk.reshape(8 * 128, 2048)),
            "wv": np.ascontiguousarray(wv), "wo": np.ascontiguousarray(wo),
            "bq": bq, "bk": bk, "bv": np.ascontiguousarray(bv),
            "bo": np.ascontiguousarray(bo),
            "C": C, "Sg": Sg, "cst": cst, "ones": ones,
            "lqk": np.ascontiguousarray(lqk), "hn": hn}



TC = 512
NTC = S // TC
H0, DK0 = 4, 256


def emit_ln_stream(ctx, xin, xhatT, scr, tag):
    nc = ctx.nc
    st = scr["st"]
    ld = ctx.sem("lns_ld_" + tag)
    hat = ctx.sem("lns_hat_" + tag)
    tr = ctx.sem("lns_tr_" + tag)
    sv = ctx.sem("lns_sv_" + tag)
    ss = ctx.sem("lns_ss_" + tag)
    xrot = scr["xrot"]
    b_ld = ctx.cnt["lns_ld_" + tag]
    b_hat = ctx.cnt["lns_hat_" + tag]
    b_tr = ctx.cnt["lns_tr_" + tag]
    b_sv = ctx.cnt["lns_sv_" + tag]
    b_ss = ctx.cnt["lns_ss_" + tag]

    with nc.Block() as b:
        @b.sync
        def _(sy):
            for t in range(NT):
                if t >= 2:
                    sy.wait_ge(sv, b_sv + t - 1)
                    sy.wait_ge(ss, b_ss + t - 1)
                sy.dma_start(out=xrot[:, (t % 2) * D:(t % 2 + 1) * D],
                             in_=xin[t * 128:(t + 1) * 128, :]).then_inc(ld, 16)

        @b.vector
        def _(v):
            for t in range(NT):
                v.wait_ge(ld, b_ld + (t + 1) * 16)
                v.tensor_reduce(out=st[:, t:t + 1], in_=xrot[:, (t % 2) * D:(t % 2 + 1) * D],
                                axis=AX.X, op=ALU.add)
                v.drain()
                v.nop().then_inc(sv, 1)

        @b.scalar
        def _(sc):
            for t in range(NT):
                sc.wait_ge(ld, b_ld + (t + 1) * 16)
                sc.activation(out=scr["sq"][:], in_=xrot[:, (t % 2) * D:(t % 2 + 1) * D],
                              func=AF.Square, accum_out=st[:, 16 + t:17 + t]).then_inc(ss, 1)
    ctx.cnt["lns_ld_" + tag] += NT * 16
    ctx.cnt["lns_sv_" + tag] += NT
    ctx.cnt["lns_ss_" + tag] += NT
    b_ld = ctx.cnt["lns_ld_" + tag]

    with nc.Block() as b:
        @b.vector
        def _(v):
            v.tensor_scalar(out=st[:, 32:48], in0=st[:, 0:16], scalar1=1.0 / D,
                            scalar2=None, op0=ALU.mult)
            v.tensor_scalar(out=st[:, 48:64], in0=st[:, 16:32], scalar1=1.0 / D,
                            scalar2=None, op0=ALU.mult)
            v.drain()
            v.tensor_tensor(out=st[:, 64:80], in0=st[:, 32:48], in1=st[:, 32:48], op=ALU.mult)
            v.drain()
            v.tensor_tensor(out=st[:, 80:96], in0=st[:, 48:64], in1=st[:, 64:80], op=ALU.subtract)
            v.drain()
            v.tensor_scalar(out=st[:, 80:96], in0=st[:, 80:96], scalar1=1e-5,
                            scalar2=None, op0=ALU.add)
            v.tensor_scalar(out=st[:, 0:16], in0=st[:, 32:48], scalar1=-1.0,
                            scalar2=None, op0=ALU.mult)

    with nc.Block() as b:
        @b.scalar
        def _(sc):
            sc.activation(out=st[:, 48:64], in_=st[:, 80:96], func=AF.Sqrt)

    with nc.Block() as b:
        @b.sync
        def _(sy):
            sy.dma_start(out=xrot[:, 0:D], in_=xin[0:128, :]).then_inc(ld, 16)
            sy.dma_start(out=xrot[:, D:2 * D], in_=xin[128:256, :]).then_inc(ld, 16)
            for t in range(NT):
                sy.wait_ge(hat, b_hat + t + 1)
                sy.dma_start(
                    out=xhatT[:].rearrange("p (k s) -> p k s", k=ND)[:, :, t * 128:(t + 1) * 128],
                    in_=scr["xhat2"][:, (t % 2) * D:(t % 2 + 1) * D],
                    transpose=True).then_inc(tr, 16)
                if t + 2 < NT:
                    sy.dma_start(out=xrot[:, ((t + 2) % 2) * D:((t + 2) % 2 + 1) * D],
                                 in_=xin[(t + 2) * 128:(t + 3) * 128, :]).then_inc(ld, 16)
            sy.wait_ge(tr, b_tr + NT * 16)

        @b.vector
        def _(v):
            v.reciprocal(out=st[:, 16:32], in_=st[:, 48:64])
            v.drain()
            for t in range(NT):
                v.wait_ge(ld, b_ld + (t + 1) * 16)
                if t >= 2:
                    v.wait_ge(tr, b_tr + (t - 1) * 16)
                v.tensor_scalar(out=scr["xhat2"][:, (t % 2) * D:(t % 2 + 1) * D],
                                in0=xrot[:, (t % 2) * D:(t % 2 + 1) * D],
                                scalar1=st[:, t:t + 1], scalar2=st[:, 16 + t:17 + t],
                                op0=ALU.add, op1=ALU.mult).then_inc(hat, 1)
    ctx.cnt["lns_ld_" + tag] += NT * 16
    ctx.cnt["lns_hat_" + tag] += NT
    ctx.cnt["lns_tr_" + tag] += NT * 16


def emit_l0attn(ctx, xin, xout, xhatT, aps, scr):
    nc = ctx.nc
    st = scr["st"]
    ones = scr["ones"]
    s_ld = ctx.sem("a0_ld")
    s_d1 = ctx.sem("a0_d1")
    ps_d = scr["ps_d"]

    with nc.Block() as b:
        @b.gpsimd
        def _(g):
            for key, dst in [("cst0", scr["cst0"][:]), ("ones", scr["ones"][:]),
                             ("qcw", scr["qcw"][:]), ("kcw", scr["kcw"][:]),
                             ("vcw", scr["vcw"][:]), ("firw", scr["firw"][:]),
                             ("w2", scr["w2B"][:]), ("b1", scr["b1B"][:]),
                             ("b2", scr["b2B"][:]), ("bpw", scr["bpW"][:]),
                             ("idb", scr["idbB"][:]), ("ltau", scr["ltau"][:]),
                             ("onw", scr["onwB"][:])]:
                g.dma_start(out=dst, in_=aps[key][:, :]).then_inc(s_ld, 16)
            g.memset(scr["S32"][:], 0)
            g.memset(scr["S_bf"][:], 0)
            g.memset(scr["histq"][:], 0)
            g.memset(scr["histk"][:], 0)
            g.memset(scr["histv"][:], 0)
            g.memset(scr["vc"][:], 0)
            g.wait_ge(s_ld, ctx.cnt["a0_ld"] + 13 * 16)
    ctx.cnt["a0_ld"] += 13 * 16

    with nc.Block() as b:
        @b.scalar
        def _(sc):
            sc.activation(out=st[0:1, 0:4], in_=scr["ltau"][0:1, 0:4], func=AF.Exp)

    with nc.Block() as b:
        @b.vector
        def _(v):
            v.tensor_scalar(out=st[0:1, 0:4], in0=st[0:1, 0:4], scalar1=0.5,
                            scalar2=None, op0=ALU.max)
            v.drain()
            v.reciprocal(out=st[0:1, 4:8], in_=st[0:1, 0:4])
            v.drain()
            v.tensor_copy(out=scr["rowb"][0:1, 0:4], in_=st[0:1, 4:8])
            v.drain()

    with nc.Block() as b:
        @b.tensor
        def _(t):
            t.matmul(ps_d[:, 0:4], ones[0:1, 0:128], scr["rowb"][0:1, 0:4],
                     start=True, stop=True).then_inc(s_d1, 1)

        @b.vector
        def _(v):
            v.wait_ge(s_d1, ctx.cnt["a0_d1"] + 1)
            v.tensor_copy(out=scr["invtauB"][:], in_=ps_d[:, 0:4])
    ctx.cnt["a0_d1"] += 1

    for tcid in range(NTC):
        emit_l0attn_tc(ctx, xin, xout, xhatT, aps, scr, tcid)


def emit_l0attn_tc(ctx, xin, xout, xhatT, aps, scr, tcid):
    nc = ctx.nc
    st = scr["st"]
    c0 = tcid * TC
    I128 = scr["cst0"][:, 0:128]
    maskLn = scr["cst0"][:, 128:256]
    maskUn = scr["cst0"][:, 256:384]
    maskUD = scr["cst0"][:, 384:512]
    ones = scr["ones"]
    pb0, pb1 = scr["ps_b0"], scr["ps_b1"]
    ps_a, ps_d, ps_f, ps_su = scr["ps_a"], scr["ps_d"], scr["ps_f"], scr["ps_su"]
    wbuf = scr["wbuf"]
    s_w = ctx.sem("a0_w"); s_pm = ctx.sem("a0_pm"); s_pe = ctx.sem("a0_pe")
    s_cv = ctx.sem("a0_cv"); s_tr = ctx.sem("a0_tr")
    s_d1 = ctx.sem("a0_d1"); s_d2 = ctx.sem("a0_d2"); s_d3 = ctx.sem("a0_d3")
    s_r1 = ctx.sem("a0_r1"); s_r2 = ctx.sem("a0_r2"); s_r3 = ctx.sem("a0_r3")
    s_ox = ctx.sem("a0_ox")

    tok_bufs = [scr["q_tok"], scr["k_tok"], scr["v_tok0"]]
    for ti in range(3):
        wkey = ["wqa", "wka", "wva"][ti]
        cw = [scr["qcw"], scr["kcw"], scr["vcw"]][ti]
        hist = [scr["histq"], scr["histk"], scr["histv"]][ti]
        raw = scr["rawfull"]
        conv = scr["convfull"]
        with nc.Block() as b:
            @b.gpsimd
            def _(g):
                g.dma_start(out=wbuf[:, 0:4096],
                            in_=aps[wkey][:, 0:4096]).then_inc(s_w, 16)
                g.dma_start(out=wbuf[:, 4096:8192],
                            in_=aps[wkey][:, 4096:8192]).then_inc(s_w, 16)

            @b.tensor
            def _(t):
                for m in range(8):
                    t.wait_ge(s_w, ctx.cnt["a0_w"] + (16 if m < 4 else 32))
                    if m >= 2:
                        t.wait_ge(s_pe, ctx.cnt["a0_pe"] + m - 1)
                    ps = pb0 if m % 2 == 0 else pb1
                    for kt in range(ND):
                        ins = t.matmul(ps[:, 0:512],
                                       wbuf[:, m * 1024 + kt * 128: m * 1024 + (kt + 1) * 128],
                                       xhatT[:, kt * S + c0: kt * S + c0 + TC],
                                       start=(kt == 0), stop=(kt == ND - 1))
                    ins.then_inc(s_pm, 1)

            @b.scalar
            def _(sc):
                for m in range(8):
                    sc.wait_ge(s_pm, ctx.cnt["a0_pm"] + m + 1)
                    sc.activation(out=raw[:, m * 544 + 32: m * 544 + 544],
                                  in_=(pb0 if m % 2 == 0 else pb1)[:, 0:512],
                                  func=AF.Identity).then_inc(s_pe, 1)

            @b.vector
            def _(v):
                for m in range(8):
                    v.tensor_copy(out=raw[:, m * 544: m * 544 + 32],
                                  in_=hist[:, m * 32:(m + 1) * 32])
                if ti == 2:
                    for m in range(8):
                        v.tensor_copy(out=scr["rs0"][:, 0:32],
                                      in_=scr["vc"][:, m * 544 + 512: m * 544 + 544])
                        v.drain()
                        v.tensor_copy(out=scr["vc"][:, m * 544: m * 544 + 32],
                                      in_=scr["rs0"][:, 0:32])
                        v.drain()
                v.drain()
                for m in range(8):
                    v.wait_ge(s_pe, ctx.cnt["a0_pe"] + m + 1)
                    v.tensor_copy(out=hist[:, m * 32:(m + 1) * 32],
                                  in_=raw[:, m * 544 + 512: m * 544 + 544])
                    v.drain()
                    dst = conv[:, m * 512:(m + 1) * 512]
                    for tau in range(4):
                        src = raw[:, m * 544 + 32 - 3 + tau: m * 544 + 32 - 3 + tau + 512]
                        if tau == 0:
                            v.tensor_scalar(out=dst, in0=src,
                                            scalar1=cw[:, m * 4:m * 4 + 1],
                                            scalar2=None, op0=ALU.mult)
                        else:
                            ins = v.scalar_tensor_tensor(out=dst, in0=src,
                                                   scalar=cw[:, m * 4 + tau:m * 4 + tau + 1],
                                                   in1=dst, op0=ALU.mult, op1=ALU.add)
                    ins.then_inc(s_cv, 1)
        ctx.cnt["a0_w"] += 32
        ctx.cnt["a0_pm"] += 8
        ctx.cnt["a0_pe"] += 8
        ctx.cnt["a0_cv"] += 8

        with nc.Block() as b:
            @b.scalar
            def _(sc):
                for m in range(8):
                    if ti < 2:
                        sc.activation(out=conv[:, m * 512:(m + 1) * 512],
                                      in_=conv[:, m * 512:(m + 1) * 512],
                                      func=AF.Silu).then_inc(s_cv, 1)
                    else:
                        sc.activation(out=scr["vc"][:, m * 544 + 32: m * 544 + 544],
                                      in_=conv[:, m * 512:(m + 1) * 512],
                                      func=AF.Silu).then_inc(s_cv, 1)

        ctx.cnt["a0_cv"] += 8
        with nc.Block() as b:
            @b.sync
            def _(sy):
                for m in range(8):
                    src = (conv[:, m * 512:(m + 1) * 512] if ti < 2
                           else scr["vc"][:, m * 544 + 32: m * 544 + 544])
                    sy.dma_start(
                        out=tok_bufs[ti][:].rearrange("p (t c) -> p t c", t=4)[:, :, m * 128:(m + 1) * 128],
                        in_=src, transpose=True).then_inc(s_tr, 16)
                sy.wait_ge(s_tr, ctx.cnt["a0_tr"] + 8 * 16)
        ctx.cnt["a0_tr"] += 8 * 16

    with nc.Block() as b:
        @b.tensor
        def _(t):
            for tt in range(4):
                for kt in range(ND):
                    t.matmul(ps_d[:, tt * 8:(tt + 1) * 8],
                             xhatT[:, kt * S + c0 + tt * 128: kt * S + c0 + (tt + 1) * 128],
                             scr["bpW"][:, kt * 8:(kt + 1) * 8],
                             start=(kt == 0), stop=False)
                ins = t.matmul(ps_d[:, tt * 8:(tt + 1) * 8], ones[0:1, 0:128],
                               scr["idbB"][0:1, 0:8], start=False, stop=True)
            ins.then_inc(s_pm, 1)

        @b.scalar
        def _(sc):
            sc.wait_ge(s_pm, ctx.cnt["a0_pm"] + 1)
            sc.activation(out=scr["bp"][:, 0:32], in_=ps_d[:, 0:32], func=AF.Sigmoid)
    ctx.cnt["a0_pm"] += 1

    with nc.Block() as b:
        @b.vector
        def _(v):
            for tt in range(4):
                for qk in range(2):
                    src = tok_bufs[qk][:, tt * 1024:(tt + 1) * 1024]
                    v.tensor_tensor(out=scr["scrot"][:, 0:1024], in0=src, in1=src,
                                    op=ALU.mult)
                    dstc = scr["ecol"] if qk == 0 else scr["dcol"]
                    v.tensor_reduce(out=dstc[:, tt * 4:(tt + 1) * 4],
                                    in_=scr["scrot"][:, 0:1024].rearrange("p (h c) -> p h c", h=4),
                                    axis=AX.X, op=ALU.add)
                    v.drain()
            v.tensor_scalar(out=scr["ecol"][:], in0=scr["ecol"][:], scalar1=1e-6,
                            scalar2=None, op0=ALU.add)
            v.tensor_scalar(out=scr["dcol"][:], in0=scr["dcol"][:], scalar1=1e-6,
                            scalar2=None, op0=ALU.add)

    with nc.Block() as b:
        @b.scalar
        def _(sc):
            sc.activation(out=scr["ecol"][:], in_=scr["ecol"][:], func=AF.Sqrt)
            sc.activation(out=scr["dcol"][:], in_=scr["dcol"][:], func=AF.Sqrt)

    with nc.Block() as b:
        @b.vector
        def _(v):
            v.reciprocal(out=scr["ecol"][:], in_=scr["ecol"][:])
            v.reciprocal(out=scr["dcol"][:], in_=scr["dcol"][:])
            v.drain()
            for tt in range(4):
                for h in range(4):
                    sl = slice(tt * 1024 + h * 256, tt * 1024 + (h + 1) * 256)
                    v.tensor_scalar(out=scr["scrot"][:, h * 256:(h + 1) * 256],
                                    in0=scr["q_tok"][:, sl.start:sl.stop],
                                    scalar1=scr["ecol"][:, tt * 4 + h:tt * 4 + h + 1],
                                    scalar2=None, op0=ALU.mult)
                    v.tensor_scalar(out=scr["kn_tok"][:, sl.start:sl.stop],
                                    in0=scr["k_tok"][:, sl.start:sl.stop],
                                    scalar1=scr["dcol"][:, tt * 4 + h:tt * 4 + h + 1],
                                    scalar2=None, op0=ALU.mult)
                for h in range(4):
                    ins = v.tensor_scalar(out=scr["scrot"][:, 1024 + h * 256: 1024 + (h + 1) * 256],
                                    in0=scr["kn_tok"][:, tt * 1024 + h * 256: tt * 1024 + (h + 1) * 256],
                                    scalar1=scr["bp"][:, tt * 8 + h: tt * 8 + h + 1],
                                    scalar2=None, op0=ALU.mult)
                ins.then_inc(s_cv, 1)

        @b.sync
        def _(sy):
            for tt in range(4):
                sy.wait_ge(s_cv, ctx.cnt["a0_cv"] + tt + 1)
                sy.dma_start(
                    out=scr["qnT"][:].rearrange("p (f c) -> p f c", f=8)[:, :, tt * 128:(tt + 1) * 128],
                    in_=scr["scrot"][:, 0:1024], transpose=True).then_inc(s_tr, 16)
                sy.dma_start(
                    out=scr["knT"][:].rearrange("p (f c) -> p f c", f=8)[:, :, tt * 128:(tt + 1) * 128],
                    in_=scr["kn_tok"][:, tt * 1024:(tt + 1) * 1024],
                    transpose=True).then_inc(s_tr, 16)
                sy.dma_start(
                    out=scr["kbnT"][:].rearrange("p (f c) -> p f c", f=8)[:, :, tt * 128:(tt + 1) * 128],
                    in_=scr["scrot"][:, 1024:2048], transpose=True).then_inc(s_tr, 16)
            sy.wait_ge(s_tr, ctx.cnt["a0_tr"] + 12 * 16)
    ctx.cnt["a0_cv"] += 4
    ctx.cnt["a0_tr"] += 12 * 16

    KTAPS = [(0, 3, 0), (1, 7, 3), (2, 15, 10), (3, 31, 25)]
    fir_ops = []
    for (p, K, po) in KTAPS:
        for m in range(8):
            for tau in range(K):
                fir_ops.append((p, m, K, po, tau))

    def emit_fir_op(v, op):
        p, m, K, po, tau = op
        dst = scr["firb"][p][:, m * 512:(m + 1) * 512]
        src = scr["vc"][:, m * 544 + 32 - (K - 1) + tau: m * 544 + 32 - (K - 1) + tau + 512]
        wcol = scr["firw"][:, m * 56 + po + tau: m * 56 + po + tau + 1]
        if tau == 0:
            v.tensor_scalar(out=dst, in0=src, scalar1=wcol,
                            scalar2=None, op0=ALU.mult)
        else:
            v.scalar_tensor_tensor(out=dst, in0=src, scalar=wcol,
                                   in1=dst, op0=ALU.mult, op1=ALU.add)

    cum1 = ctx.cnt["a0_d1"]
    cum2 = ctx.cnt["a0_d2"]
    cum3 = ctx.cnt["a0_d3"]
    cumT = ctx.cnt["a0_tr"]
    if "a0_smark" not in ctx.__dict__:
        ctx.a0_smark = {h: 0 for h in range(4)}
        ctx.a0_y0mark = 0
        ctx.a0_wTmark = 0
    pairs = [(i, h) for i in range(4) for h in range(4)]
    sched = []
    for (i, h) in pairs:
        sc_ = {"i": i, "h": h,
               "b1": cum1, "b2": cum2, "b3": cum3, "bT": cumT,
               "smark": ctx.a0_smark[h], "y0mark": ctx.a0_y0mark,
               "wTmark": ctx.a0_wTmark}
        sched.append(sc_)
        ctx.a0_y0mark = cum1 + 2
        ctx.a0_wTmark = cum1 + 15
        ctx.a0_smark[h] = cum2 + 10
        cum1 += 17
        cum2 += 10
        cum3 += 9
        cumT += 16
    ctx.cnt["a0_d1"] = cum1
    ctx.cnt["a0_d2"] = cum2
    ctx.cnt["a0_d3"] = cum3
    ctx.cnt["a0_tr"] = cumT

    s_d1h, s_d2h, s_d3h, s_trh = s_d1, s_d2, s_d3, s_tr
    delta_tok = scr["convfull"]
    with nc.Block() as b:
        @b.tensor
        def _(t):
            for sc_ in sched:
                i, h = sc_["i"], sc_["h"]
                b1, b2, b3, bT = sc_["b1"], sc_["b2"], sc_["b3"], sc_["bT"]
                f0, f1 = 2 * h, 2 * h + 1
                isl = slice(i * 128, (i + 1) * 128)

                def fmT(buf, f):
                    return buf[:, f * 512 + i * 128: f * 512 + (i + 1) * 128]
                t.wait_ge(s_d2h, b2)
                t.matmul(ps_a[:, 0:128], fmT(scr["kbnT"], f0), fmT(scr["knT"], f0),
                         start=True, stop=False)
                t.matmul(ps_a[:, 0:128], fmT(scr["kbnT"], f1), fmT(scr["knT"], f1),
                         start=False, stop=True)
                t.matmul(ps_a[:, 128:256], fmT(scr["knT"], f0), fmT(scr["kbnT"], f0),
                         start=True, stop=False)
                t.matmul(ps_a[:, 128:256], fmT(scr["knT"], f1), fmT(scr["kbnT"], f1),
                         start=False, stop=True)
                t.matmul(ps_a[:, 256:384], fmT(scr["knT"], f0), fmT(scr["qnT"], f0),
                         start=True, stop=False)
                ins = t.matmul(ps_a[:, 256:384], fmT(scr["knT"], f1), fmT(scr["qnT"], f1),
                               start=False, stop=True)
                ins.then_inc(s_d1h, 1)
                Pslot = [(scr["Pp"][:, 0:128], scr["Pp"][:, 128:256]),
                         (scr["Pp"][:, 256:384], scr["Pp"][:, 384:512])]
                tslot = [scr["tA"], scr["tB"]]
                for j in range(7):
                    rhs = scr["y0"][:] if j == 0 else tslot[(j - 1) % 2][:]
                    PT = Pslot[j % 2][1]
                    if j == 0:
                        t.wait_ge(s_d2h, b2 + 2)
                    else:
                        t.wait_ge(s_d3h, b3 + j)
                        t.wait_ge(s_d2h, b2 + 2 + j)
                    t.matmul(ps_f[:, 0:512] if j % 2 == 0 else ps_su[:, 0:512],
                             PT, rhs, start=True, stop=False)
                    ins = t.matmul(ps_f[:, 0:512] if j % 2 == 0 else ps_su[:, 0:512],
                                   I128, rhs, start=False, stop=True)
                    ins.then_inc(s_d1h, 1)
                    if j < 6:
                        jd = j + 1
                        P_prev, PT_prev = Pslot[j % 2]
                        dsl = ps_d[:, 0:256] if jd % 2 == 1 else pb0[:, 0:256]
                        t.matmul(dsl[:, 0:128], PT_prev, P_prev, start=True, stop=True)
                        ins = t.matmul(dsl[:, 128:256], P_prev, PT_prev, start=True, stop=True)
                        ins.then_inc(s_d1h, 1)
                t.wait_ge(s_trh, bT + 16)
                t.wait_ge(s_d2h, sc_["smark"])
                t.wait_ge(s_d3h, b3 + 7)
                t.matmul(ps_su[:, 0:256], scr["wnegT"][:, 0:128],
                         scr["S_bf"][:, h * 512:h * 512 + 256], start=True, stop=False)
                t.matmul(ps_su[:, 0:256], scr["wnegT"][:, 128:256],
                         scr["S_bf"][:, h * 512 + 256:h * 512 + 512], start=False, stop=False)
                ins = t.matmul(ps_su[:, 0:256], I128, scr["u_sb"][:],
                               start=False, stop=True)
                ins.then_inc(s_d1h, 1)
                t.wait_ge(s_d3h, b3 + 8)
                t.matmul(ps_f[:, 0:256], fmT(scr["qnT"], f0),
                         scr["S_bf"][:, h * 512:h * 512 + 256], start=True, stop=False)
                t.matmul(ps_f[:, 0:256], fmT(scr["qnT"], f1),
                         scr["S_bf"][:, h * 512 + 256:h * 512 + 512], start=False, stop=False)
                ins = t.matmul(ps_f[:, 0:256], scr["attT"][:], scr["uadj"][:],
                               start=False, stop=True)
                ins.then_inc(s_d1h, 1)
                t.wait_ge(s_d2h, b2 + 9)
                ksl = scr["kn_tok"][:, i * 1024 + h * 256: i * 1024 + (h + 1) * 256]
                t.matmul(ps_a[:, 0:256], ksl[:, 0:128], scr["uadj2"][:],
                         start=True, stop=True)
                ins = t.matmul(ps_a[:, 256:512], ksl[:, 128:256], scr["uadj2"][:],
                               start=True, stop=True)
                ins.then_inc(s_d1h, 1)

        @b.vector
        def _(v):
            fir_iter = iter(fir_ops)
            per_pair = (len(fir_ops) + len(sched) - 1) // len(sched)
            for pi, sc_ in enumerate(sched):
                for _ in range(per_pair):
                    op = next(fir_iter, None)
                    if op is not None:
                        emit_fir_op(v, op)
                i, h = sc_["i"], sc_["h"]
                b1, b2, b3 = sc_["b1"], sc_["b2"], sc_["b3"]
                v.wait_ge(s_d1h, sc_["y0mark"])
                vsl = scr["v_tok0"][:, i * 1024 + h * 256: i * 1024 + (h + 1) * 256]
                ksl = scr["kn_tok"][:, i * 1024 + h * 256: i * 1024 + (h + 1) * 256]
                bcol = scr["bp"][:, i * 8 + h: i * 8 + h + 1]
                v.tensor_scalar(out=scr["y0"][:, 0:256], in0=vsl, scalar1=bcol,
                                scalar2=None, op0=ALU.mult)
                v.tensor_scalar(out=scr["y0"][:, 256:512], in0=ksl, scalar1=bcol,
                                scalar2=None, op0=ALU.mult).then_inc(s_d2h, 1)
                v.wait_ge(s_d1h, b1 + 1)
                v.tensor_tensor(out=scr["Pp"][:, 0:128], in0=ps_a[:, 0:128],
                                in1=maskLn, op=ALU.mult)
                v.tensor_tensor(out=scr["Pp"][:, 128:256], in0=ps_a[:, 128:256],
                                in1=maskUn, op=ALU.mult)
                v.tensor_tensor(out=scr["attT"][:], in0=ps_a[:, 256:384],
                                in1=maskUD, op=ALU.mult).then_inc(s_d2h, 1)
                for jd in range(1, 7):
                    v.wait_ge(s_d1h, b1 + 2 * jd + 1)
                    if jd >= 2:
                        v.wait_ge(s_d1h, b1 + 2 * (jd - 1) + 1)
                    dsl = ps_d[:, 0:256] if jd % 2 == 1 else pb0[:, 0:256]
                    slot = jd % 2
                    v.tensor_copy(out=scr["Pp"][:, slot * 256: slot * 256 + 128],
                                  in_=dsl[:, 0:128])
                    v.tensor_copy(out=scr["Pp"][:, slot * 256 + 128: slot * 256 + 256],
                                  in_=dsl[:, 128:256]).then_inc(s_d2h, 1)
                v.wait_ge(s_d3h, b3 + 8)
                v.tensor_scalar(out=scr["uadj2"][:], in0=scr["uadj"][:],
                                scalar1=scr["dcol"][:, i * 4 + h: i * 4 + h + 1],
                                scalar2=None, op0=ALU.mult).then_inc(s_d2h, 1)
                v.wait_ge(s_d1h, b1 + 17)
                v.tensor_tensor(out=scr["S32"][:, h * 512:(h + 1) * 512],
                                in0=ps_a[:, 0:512],
                                in1=scr["S32"][:, h * 512:(h + 1) * 512],
                                op=ALU.add)
                v.tensor_copy(out=scr["S_bf"][:, h * 512:(h + 1) * 512],
                              in_=scr["S32"][:, h * 512:(h + 1) * 512]).then_inc(s_d2h, 1)

        @b.scalar
        def _(sc):
            for sc_ in sched:
                i, h = sc_["i"], sc_["h"]
                b1, b3 = sc_["b1"], sc_["b3"]
                tslot = [scr["tA"], scr["tB"]]
                for j in range(7):
                    sc.wait_ge(s_d1h, b1 + (2 if j == 0 else 2 * j + 2))
                    src = ps_f[:, 0:512] if j % 2 == 0 else ps_su[:, 0:512]
                    if j < 6:
                        sc.activation(out=tslot[j % 2][:], in_=src,
                                      func=AF.Identity).then_inc(s_d3h, 1)
                    else:
                        sc.activation(out=scr["u_sb"][:], in_=src[:, 0:256],
                                      func=AF.Identity)
                        sc.activation(out=scr["w_sb"][:], in_=src[:, 256:512],
                                      func=AF.Identity, scale=-1.0).then_inc(s_d3h, 1)
                sc.wait_ge(s_d1h, b1 + 15)
                sc.activation(out=scr["uadj"][:], in_=ps_su[:, 0:256],
                              func=AF.Identity).then_inc(s_d3h, 1)
                sc.wait_ge(s_d1h, b1 + 16)
                sc.activation(out=delta_tok[:, i * 1024 + h * 256: i * 1024 + (h + 1) * 256],
                              in_=ps_f[:, 0:256], func=AF.Identity).then_inc(s_d3h, 1)

        @b.sync
        def _(sy):
            for sc_ in sched:
                sy.wait_ge(s_d3h, sc_["b3"] + 7)
                if sc_["wTmark"] > 0:
                    sy.wait_ge(s_d1h, sc_["wTmark"])
                sy.dma_start(out=scr["wnegT"][:].rearrange("p (f c) -> p f c", f=2),
                             in_=scr["w_sb"][:], transpose=True).then_inc(s_trh, 16)

    rin = []
    for kt in range(8):
        rin.append(xhatT[:, kt * S + c0: kt * S + c0 + TC])
    for p in range(4):
        for kt in range(8):
            rin.append(scr["firb"][p][:, kt * 512:(kt + 1) * 512])
    deltaT = scr["rawfull"]
    for kt in range(8):
        rin.append(deltaT[:, kt * 512:(kt + 1) * 512])

    with nc.Block() as b:
        @b.sync
        def _(sy):
            for tt in range(4):
                sy.dma_start(
                    out=deltaT[:].rearrange("p (f c) -> p f c", f=8)[:, :, tt * 128:(tt + 1) * 128],
                    in_=delta_tok[:, tt * 1024:(tt + 1) * 1024],
                    transpose=True).then_inc(s_tr, 16)
            sy.wait_ge(s_tr, ctx.cnt["a0_tr"] + 4 * 16)
    ctx.cnt["a0_tr"] += 4 * 16

    b_w = ctx.cnt["a0_w"]
    b_r1 = ctx.cnt["a0_r1"]
    b_r2 = ctx.cnt["a0_r2"]
    b_r3 = ctx.cnt["a0_r3"]
    with nc.Block() as b:
        @b.gpsimd
        def _(g):
            for m in range(16):
                for hf in range(2):
                    idx = m * 2 + hf
                    if idx >= 2:
                        g.wait_ge(s_r1, b_r1 + idx - 1)
                    g.dma_start(out=wbuf[:, hf * 3072:(hf + 1) * 3072],
                                in_=aps["w1"][(idx) * 128:(idx + 1) * 128, :]
                                ).then_inc(s_w, 16)

        @b.tensor
        def _(t):
            for m in range(16):
                ps = pb0 if m % 2 == 0 else pb1
                if m >= 2:
                    t.wait_ge(s_r2, b_r2 + m - 1)
                for hf in range(2):
                    idx = m * 2 + hf
                    t.wait_ge(s_w, b_w + (idx + 1) * 16)
                    for ktl in range(24):
                        ins = t.matmul(ps[:, 0:512],
                                       wbuf[:, hf * 3072 + ktl * 128: hf * 3072 + (ktl + 1) * 128],
                                       rin[hf * 24 + ktl],
                                       start=(hf == 0 and ktl == 0),
                                       stop=(hf == 1 and ktl == 23))
                    ins.then_inc(s_r1, 1)
                if m >= 1:
                    mm = m - 1
                    t.wait_ge(s_r2, b_r2 + mm + 1)
                    lgb = [ps_d, ps_f, ps_su, ps_a]
                    for tsub in range(4):
                        ins = t.matmul(lgb[tsub][:, 0:20],
                                       scr["hmid"][:, (mm % 2) * 512 + tsub * 128: (mm % 2) * 512 + (tsub + 1) * 128],
                                       scr["w2B"][:, mm * 20:(mm + 1) * 20],
                                       start=(mm == 0), stop=False)
                    ins.then_inc(s_r3, 1)
            t.wait_ge(s_r2, b_r2 + 16)
            lgb = [ps_d, ps_f, ps_su, ps_a]
            for tsub in range(4):
                t.matmul(lgb[tsub][:, 0:20],
                         scr["hmid"][:, (15 % 2) * 512 + tsub * 128: (15 % 2) * 512 + (tsub + 1) * 128],
                         scr["w2B"][:, 15 * 20:16 * 20],
                         start=False, stop=False)
                ins = t.matmul(lgb[tsub][:, 0:20], ones[0:1, 0:128],
                               scr["b2B"][0:1, 0:20], start=False, stop=True)
            ins.then_inc(s_r3, 1)

        @b.scalar
        def _(sc):
            for m in range(16):
                sc.wait_ge(s_r1, b_r1 + 2 * m + 2)
                if m >= 2:
                    sc.wait_ge(s_r3, b_r3 + m - 1)
                ps = pb0 if m % 2 == 0 else pb1
                sc.activation(out=scr["hmid"][:, (m % 2) * 512:(m % 2 + 1) * 512],
                              in_=ps[:, 0:512], func=AF.Gelu,
                              bias=scr["b1B"][:, m:m + 1]).then_inc(s_r2, 1)
    ctx.cnt["a0_w"] += 32 * 16
    ctx.cnt["a0_r1"] += 32
    ctx.cnt["a0_r2"] += 16
    ctx.cnt["a0_r3"] += 16

    lgb = [ps_d, ps_f, ps_su, ps_a]
    with nc.Block() as b:
        @b.vector
        def _(v):
            for tsub in range(4):
                v.tensor_copy(out=scr["probs"][:, tsub * 20:(tsub + 1) * 20],
                              in_=lgb[tsub][:, 0:20])
            v.drain()
            for tsub in range(4):
                for h in range(4):
                    sl = scr["probs"][:, tsub * 20 + h * 5: tsub * 20 + (h + 1) * 5]
                    v.tensor_scalar(out=sl, in0=sl, scalar1=scr["invtauB"][:, h:h + 1],
                                    scalar2=None, op0=ALU.mult)
            v.drain()
            for tsub in range(4):
                v.tensor_reduce(out=st[:, tsub * 4:(tsub + 1) * 4],
                                in_=scr["probs"][:, tsub * 20:(tsub + 1) * 20].rearrange("p (h f) -> p h f", h=4),
                                axis=AX.X, op=ALU.max)
            v.drain()
            v.tensor_scalar(out=st[:, 0:16], in0=st[:, 0:16], scalar1=-1.0,
                            scalar2=None, op0=ALU.mult)
            v.drain()
            for tsub in range(4):
                for h in range(4):
                    sl = scr["probs"][:, tsub * 20 + h * 5: tsub * 20 + (h + 1) * 5]
                    v.tensor_scalar(out=sl, in0=sl, scalar1=st[:, tsub * 4 + h:tsub * 4 + h + 1],
                                    scalar2=None, op0=ALU.add)

    with nc.Block() as b:
        @b.scalar
        def _(sc):
            sc.activation(out=scr["probs"][:, 0:80], in_=scr["probs"][:, 0:80], func=AF.Exp)

    with nc.Block() as b:
        @b.vector
        def _(v):
            for tsub in range(4):
                v.tensor_reduce(out=st[:, tsub * 4:(tsub + 1) * 4],
                                in_=scr["probs"][:, tsub * 20:(tsub + 1) * 20].rearrange("p (h f) -> p h f", h=4),
                                axis=AX.X, op=ALU.add)
            v.drain()
            v.reciprocal(out=st[:, 0:16], in_=st[:, 0:16])
            v.drain()
            for tsub in range(4):
                for h in range(4):
                    sl = scr["probs"][:, tsub * 20 + h * 5: tsub * 20 + (h + 1) * 5]
                    v.tensor_scalar(out=sl, in0=sl, scalar1=st[:, tsub * 4 + h:tsub * 4 + h + 1],
                                    scalar2=None, op0=ALU.mult)
            v.drain()
            bp3 = scr["bp"][:, 0:32].rearrange("p (t g) -> p t g", t=4)[:, :, 4:8]
            st3 = st[:, 16:32].rearrange("p (t g) -> p t g", t=4)
            v.tensor_scalar(out=st3, in0=bp3,
                            scalar1=0.09, scalar2=0.9, op0=ALU.mult, op1=ALU.add)
            v.drain()
            v.tensor_tensor(out=st3, in0=st3, in1=bp3, op=ALU.mult)
            v.drain()
            v.tensor_scalar(out=st[:, 32:48], in0=st[:, 16:32], scalar1=-1.0,
                            scalar2=1.0, op0=ALU.mult, op1=ALU.add)
            v.drain()

    tokfir = [scr["firb"][1], scr["firb"][2], scr["firb"][3], scr["rawfull"]]
    with nc.Block() as b:
        @b.sync
        def _(sy):
            order = [3, 2, 1, 0]
            for oi, p in enumerate(order):
                if oi >= 1:
                    sy.wait_ge(s_tr, ctx.cnt["a0_tr"] + oi * 8 * 16)
                dstb = tokfir[p]
                for m in range(8):
                    sy.dma_start(
                        out=dstb[:].rearrange("p (t c) -> p t c", t=4)[:, :, m * 128:(m + 1) * 128],
                        in_=scr["firb"][p][:, m * 512:(m + 1) * 512],
                        transpose=True).then_inc(s_tr, 16)
            sy.wait_ge(s_tr, ctx.cnt["a0_tr"] + 32 * 16)

        @b.gpsimd
        def _(g):
            g.dma_start(out=wbuf[:], in_=aps["woa"][:, :]).then_inc(s_w, 16)
            g.wait_ge(s_w, ctx.cnt["a0_w"] + 16)
    ctx.cnt["a0_tr"] += 32 * 16
    ctx.cnt["a0_w"] += 16

    o_tok = scr["k_tok"]
    with nc.Block() as b:
        @b.vector
        def _(v):
            for kt in range(ND):
                v.tensor_scalar(out=wbuf[:, kt * 1024:(kt + 1) * 1024],
                                in0=wbuf[:, kt * 1024:(kt + 1) * 1024],
                                scalar1=scr["onwB"][:, kt:kt + 1],
                                scalar2=None, op0=ALU.mult)
            for tsub in range(4):
                for h in range(4):
                    osl = o_tok[:, tsub * 1024 + h * 256: tsub * 1024 + (h + 1) * 256]
                    for p in range(4):
                        src = tokfir[p][:, tsub * 1024 + h * 256: tsub * 1024 + (h + 1) * 256]
                        pcol = scr["probs"][:, tsub * 20 + h * 5 + p: tsub * 20 + h * 5 + p + 1]
                        if p == 0:
                            v.tensor_scalar(out=osl, in0=src, scalar1=pcol,
                                            scalar2=None, op0=ALU.mult)
                        else:
                            v.scalar_tensor_tensor(out=osl, in0=src, scalar=pcol,
                                                   in1=osl, op0=ALU.mult, op1=ALU.add)
                    dsl = delta_tok[:, tsub * 1024 + h * 256: tsub * 1024 + (h + 1) * 256]
                    pcol = scr["probs"][:, tsub * 20 + h * 5 + 4: tsub * 20 + h * 5 + 5]
                    v.scalar_tensor_tensor(out=osl, in0=dsl, scalar=pcol,
                                           in1=osl, op0=ALU.mult, op1=ALU.add)
                    vsl = scr["v_tok0"][:, tsub * 1024 + h * 256: tsub * 1024 + (h + 1) * 256]
                    v.tensor_scalar(out=scr["y0"][:, 0:256], in0=vsl,
                                    scalar1=st[:, 16 + tsub * 4 + h: 16 + tsub * 4 + h + 1],
                                    scalar2=None, op0=ALU.mult)
                    v.scalar_tensor_tensor(out=osl, in0=osl,
                                           scalar=st[:, 32 + tsub * 4 + h: 32 + tsub * 4 + h + 1],
                                           in1=scr["y0"][:, 0:256],
                                           op0=ALU.mult, op1=ALU.add)
            for tsub in range(4):
                v.tensor_tensor(out=scr["scrot"][:, 0:1024],
                                in0=o_tok[:, tsub * 1024:(tsub + 1) * 1024],
                                in1=o_tok[:, tsub * 1024:(tsub + 1) * 1024], op=ALU.mult)
                v.tensor_reduce(out=st[:, 48 + tsub * 4: 48 + (tsub + 1) * 4],
                                in_=scr["scrot"][:, 0:1024].rearrange("p (h c) -> p h c", h=4),
                                axis=AX.X, op=ALU.add)
                v.drain()
            v.tensor_scalar(out=st[:, 48:64], in0=st[:, 48:64], scalar1=1.0 / 256,
                            scalar2=1e-5, op0=ALU.mult, op1=ALU.add)

    with nc.Block() as b:
        @b.scalar
        def _(sc):
            sc.activation(out=st[:, 48:64], in_=st[:, 48:64], func=AF.Sqrt)

    s_ox = ctx.sem("a0_ox")
    b_ox = ctx.cnt["a0_ox"]
    b_om = ctx.cnt["a0_pm"]
    b_ol = ctx.cnt["a0_ld"]
    with nc.Block() as b:
        @b.vector
        def _(v):
            v.reciprocal(out=st[:, 48:64], in_=st[:, 48:64])
            v.drain()
            for tsub in range(4):
                for h in range(4):
                    osl = o_tok[:, tsub * 1024 + h * 256: tsub * 1024 + (h + 1) * 256]
                    ins = v.tensor_scalar(out=osl, in0=osl,
                                    scalar1=st[:, 48 + tsub * 4 + h: 48 + tsub * 4 + h + 1],
                                    scalar2=None, op0=ALU.mult)
            ins.then_inc(s_ox, 1)

        @b.sync
        def _(sy):
            sy.wait_ge(s_ox, b_ox + 1)
            for tsub in range(4):
                sy.dma_start(
                    out=scr["q_tok"][:].rearrange("p (f c) -> p f c", f=8)[:, :, tsub * 128:(tsub + 1) * 128],
                    in_=o_tok[:, tsub * 1024:(tsub + 1) * 1024],
                    transpose=True).then_inc(s_tr, 16)
            sy.wait_ge(s_tr, ctx.cnt["a0_tr"] + 4 * 16)
    ctx.cnt["a0_ox"] += 1
    ctx.cnt["a0_tr"] += 4 * 16

    oT = scr["q_tok"]
    s_st = ctx.sem("a0_st")
    b_st = ctx.cnt["a0_st"]
    b_xw = ctx.cnt["a0_w"]
    b_oxd = ctx.cnt["a0_ox"]
    with nc.Block() as b:
        @b.gpsimd
        def _(g):
            for tsub in range(4):
                if tsub >= 2:
                    g.wait_ge(s_ox, b_oxd + (tsub - 2) * 2 + 2)
                g.dma_start(out=scr["xrot"][:, (tsub % 2) * D:(tsub % 2 + 1) * D],
                            in_=xin[c0 + tsub * 128: c0 + (tsub + 1) * 128, :]
                            ).then_inc(s_w, 16)

        @b.tensor
        def _(t):
            for tsub in range(4):
                for nn in range(2):
                    idx = tsub * 2 + nn
                    ps = pb0 if idx % 2 == 0 else pb1
                    if idx >= 2:
                        t.wait_ge(s_ox, b_oxd + idx - 1)
                    for kt in range(ND):
                        ins = t.matmul(ps[:, 0:512],
                                       oT[:, kt * 512 + tsub * 128: kt * 512 + (tsub + 1) * 128],
                                       wbuf[:, kt * 1024 + nn * 512: kt * 1024 + (nn + 1) * 512],
                                       start=(kt == 0), stop=(kt == ND - 1))
                    ins.then_inc(s_pm, 1)

        @b.vector
        def _(v):
            for tsub in range(4):
                for nn in range(2):
                    idx = tsub * 2 + nn
                    ps = pb0 if idx % 2 == 0 else pb1
                    v.wait_ge(s_pm, b_om + idx + 1)
                    v.wait_ge(s_w, b_xw + (tsub + 1) * 16)
                    if idx >= 4:
                        v.wait_ge(s_st, b_st + (idx - 3) * 16)
                    v.tensor_tensor(out=scr["xo"][:, (idx % 4) * 512:(idx % 4 + 1) * 512],
                                    in0=ps[:, 0:512],
                                    in1=scr["xrot"][:, (tsub % 2) * D + nn * 512:(tsub % 2) * D + (nn + 1) * 512],
                                    op=ALU.add).then_inc(s_ox, 1)

        @b.sync
        def _(sy):
            for tsub in range(4):
                for nn in range(2):
                    idx = tsub * 2 + nn
                    sy.wait_ge(s_ox, b_oxd + idx + 1)
                    sy.dma_start(out=xout[c0 + tsub * 128: c0 + (tsub + 1) * 128,
                                          nn * 512:(nn + 1) * 512],
                                 in_=scr["xo"][:, (idx % 4) * 512:(idx % 4 + 1) * 512]
                                 ).then_inc(s_st, 16)
            sy.wait_ge(s_st, b_st + 8 * 16)
    ctx.cnt["a0_pm"] += 8
    ctx.cnt["a0_w"] += 64
    ctx.cnt["a0_ox"] += 8
    ctx.cnt["a0_st"] += 8 * 16


def prep_l0attn(inp):
    def prepW(w):
        return np.ascontiguousarray(w.reshape(8, 128, 1024).transpose(1, 0, 2).reshape(128, 8192))

    def prepWm(w):
        t = w.reshape(8, 128, 8, 128)
        return np.ascontiguousarray(t.transpose(1, 2, 0, 3).reshape(128, 8192))
    wqa = prepWm(inp["L0_attn_Wq"]); wka = prepWm(inp["L0_attn_Wk"])
    wva = prepWm(inp["L0_attn_Wv"]); woa = prepW(inp["L0_attn_Wo"])
    W1 = inp["L0_attn_router_W1"]
    w1p = np.empty((16, 2, 128, 3072), np.float32)
    W1r = W1.reshape(48, 128, 2048)
    for m in range(16):
        for hf in range(2):
            blk = W1r[hf * 24:(hf + 1) * 24, :, m * 128:(m + 1) * 128]
            w1p[m, hf] = blk.transpose(1, 0, 2).reshape(128, 3072)
    w1 = np.ascontiguousarray(w1p.reshape(32 * 128, 3072))
    w2 = np.ascontiguousarray(
        inp["L0_attn_router_W2"].reshape(16, 128, 20).transpose(1, 0, 2).reshape(128, 320))
    b1 = np.ascontiguousarray(inp["L0_attn_router_b1"].reshape(16, 128).T)
    b2 = inp["L0_attn_router_b2"].reshape(1, 20)
    bpw = np.concatenate([inp["L0_attn_Wb"], inp["L0_attn_id_gate_W"]], axis=1)
    bpw = np.ascontiguousarray(bpw.reshape(8, 128, 8).transpose(1, 0, 2).reshape(128, 64))
    idb = np.zeros((1, 8), np.float32)
    idb[0, 4:8] = inp["L0_attn_id_gate_b"]
    ltau = inp["L0_attn_log_tau"].reshape(1, 4)
    onw = np.ascontiguousarray(np.tile(inp["L0_attn_o_norm_w"], 4).reshape(8, 128).T)

    def prepC(w):
        return np.ascontiguousarray(w.reshape(8, 128, 4).transpose(1, 0, 2).reshape(128, 32))
    qcw = prepC(inp["L0_attn_q_conv_w"]); kcw = prepC(inp["L0_attn_k_conv_w"])
    vcw = prepC(inp["L0_attn_v_conv_w"])
    firw = np.zeros((8, 128, 56), np.float32)
    offs = {3: 0, 7: 3, 15: 10, 31: 25}
    for key, K in [("L0_attn_fir_w3", 3), ("L0_attn_fir_w7", 7),
                   ("L0_attn_fir_w15", 15), ("L0_attn_fir_w31", 31)]:
        f = inp[key].reshape(1024, K)
        fr = f.reshape(8, 128, K)
        firw[:, :, offs[K]:offs[K] + K] += fr
    firw = np.ascontiguousarray(firw.transpose(1, 0, 2).reshape(128, 448))
    cst0 = np.zeros((128, 512), np.float32)
    cst0[:, 0:128] = np.eye(128)
    cst0[:, 128:256] = -np.tril(np.ones((128, 128)), -1)
    cst0[:, 256:384] = -np.triu(np.ones((128, 128)), 1)
    cst0[:, 384:512] = np.triu(np.ones((128, 128)), 0)
    ones = np.ones((1, 128), np.float32)
    return {"wqa": wqa, "wka": wka, "wva": wva, "woa": woa, "w1": w1, "w2": w2,
            "b1": b1, "b2": np.ascontiguousarray(b2), "bpw": bpw,
            "idb": idb, "ltau": np.ascontiguousarray(ltau), "onw": onw,
            "qcw": qcw, "kcw": kcw, "vcw": vcw, "firw": firw,
            "cst0": cst0, "ones": ones}


def alloc_l0attn_scr(ctx):
    scr = {}
    convfull = ctx.sbuf("convfull", [128, 4096], BF16)
    scr["convfull"] = convfull
    scr["xhat2"] = convfull[:, 0:2048]
    scr["sq"] = convfull[:, 2048:3072]
    for name, shape, dt in [
            ("wbuf", [128, 8192], BF16), ("rawfull", [128, 4352], BF16),
            ("histq", [128, 256], BF16), ("histk", [128, 256], BF16),
            ("histv", [128, 256], BF16), ("vc", [128, 4352], BF16),
            ("q_tok", [128, 4096], BF16), ("k_tok", [128, 4096], BF16),
            ("v_tok0", [128, 4096], BF16), ("kn_tok", [128, 4096], BF16),
            ("scrot", [128, 2048], BF16),
            ("qnT", [128, 4096], BF16), ("knT", [128, 4096], BF16),
            ("kbnT", [128, 4096], BF16),
            ("y0", [128, 512], BF16), ("tA", [128, 512], BF16),
            ("tB", [128, 512], BF16), ("Pp", [128, 512], BF16),
            ("attT", [128, 128], BF16), ("u_sb", [128, 256], BF16),
            ("w_sb", [128, 256], BF16), ("wnegT", [128, 256], BF16),
            ("uadj", [128, 256], BF16), ("uadj2", [128, 256], BF16),
            ("S32", [128, 2048], F32), ("S_bf", [128, 2048], BF16),
            ("dcol", [128, 16], F32), ("ecol", [128, 16], F32),
            ("bp", [128, 32], F32), ("probs", [128, 80], F32),
            ("invtauB", [128, 4], F32), ("rowb", [1, 64], BF16),
            ("rs0", [128, 64], BF16),
            ("xrot", [128, 2048], F32), ("xo", [128, 2048], F32),
            ("st", [128, 96], F32),
            ("cst0", [128, 512], BF16), ("ones", [1, 128], BF16),
            ("qcw", [128, 32], F32), ("kcw", [128, 32], F32),
            ("vcw", [128, 32], F32), ("firw", [128, 448], F32),
            ("w2B", [128, 320], BF16), ("b1B", [128, 16], F32),
            ("b2B", [1, 20], BF16), ("bpW", [128, 64], BF16),
            ("idbB", [1, 8], BF16), ("ltau", [1, 4], F32),
            ("onwB", [128, 8], F32)]:
        scr[name] = ctx.sbuf(name, shape, dt)
    scr["hmid"] = scr["scrot"][:, 0:1024]
    scr["firb"] = [ctx.sbuf("fir%d" % p, [128, 4096], BF16) for p in range(4)]
    for name in ["ps_b0", "ps_b1", "ps_a", "ps_d", "ps_f", "ps_su"]:
        scr[name] = ctx.psum(name, [128, 512])
    return scr

_L0A_SHAPES = {"wqa": [128, 8192], "wka": [128, 8192], "wva": [128, 8192],
               "woa": [128, 8192], "w1": [4096, 3072], "w2": [128, 320],
               "b1": [128, 16], "b2": [1, 20], "bpw": [128, 64], "idb": [1, 8],
               "ltau": [1, 4], "onw": [128, 8], "qcw": [128, 32], "kcw": [128, 32],
               "vcw": [128, 32], "firw": [128, 448], "cst0": [128, 512],
               "ones": [1, 128]}

_L1A_SHAPES = {"wq": [8 * 128, 2048], "wk": [8 * 128, 2048], "wv": [128, 8192],
               "wo": [128, 8192], "bq": [128, 16], "bk": [128, 16],
               "bv": [1, 1024], "bo": [1, 1024], "C": [128, 2048],
               "Sg": [128, 2048], "cst": [128, 384], "ones": [1, 128],
               "lqk": [16, 256], "hn": [128, 8]}


def build_l0attn_prog():
    ctx = Ctx()
    nc = ctx.nc
    xin = nc.dram_tensor("xin", [S, D], F32, kind="ExternalInput").ap()
    dr = {k: nc.dram_tensor("i_" + k, shp, F32, kind="ExternalInput").ap()
          for k, shp in _L0A_SHAPES.items()}
    xout = nc.dram_tensor("xout", [S, D], F32, kind="ExternalOutput").ap()
    xhatT = ctx.sbuf("xhatT", [128, ND * S], BF16)
    scr = alloc_l0attn_scr(ctx)
    emit_ln_stream(ctx, xin, xhatT, scr, "ln0")
    emit_l0attn(ctx, xin, xout, xhatT, dr, scr)
    ctx.close()
    return ctx


def build_l0ffn_prog():
    ctx = Ctx()
    nc = ctx.nc
    xin = nc.dram_tensor("xin", [S, D], F32, kind="ExternalInput").ap()
    gw = nc.dram_tensor("i_gw", [128, 64], F32, kind="ExternalInput").ap()
    gu = nc.dram_tensor("i_gu", [8 * 16 * 128, 2048], F32, kind="ExternalInput").ap()
    dw = nc.dram_tensor("i_dw", [8 * 128, 16384], F32, kind="ExternalInput").ap()
    xout = nc.dram_tensor("xout", [S, D], F32, kind="ExternalOutput").ap()

    x_res = ctx.sbuf("x_res", [128, NT * D], F32)
    xhatT = ctx.sbuf("xhatT", [128, ND * S], BF16)
    h = ctx.sbuf("h", [128, 16 * 2048], BF16)
    scr = {
        "xhat2": h[:, 0:2 * D],
        "sq": h[:, 2 * D:3 * D],
        "st": ctx.sbuf("st", [128, 96], F32),
        "h": h,
        "guW": ctx.sbuf("guW", [128, 2 * 2048], BF16),
        "dwW": ctx.sbuf("dwW", [128, 16384], BF16),
        "sg": ctx.sbuf("sg", [128, 2 * 512], BF16),
        "comb": ctx.sbuf("comb", [128, 128], F32),
        "pex": ctx.sbuf("pex", [128, 16], F32),
        "gwB": ctx.sbuf("gwB", [128, 128], BF16),
        "pgu0": ctx.psum("pgu0", [128, 1024]),
        "pgu1": ctx.psum("pgu1", [128, 1024]),
        "pd0": ctx.psum("pd0", [128, 1024]),
        "pd1": ctx.psum("pd1", [128, 1024]),
    }
    s_io = ctx.sem("io")
    with ctx.nc.Block() as b:
        @b.sync
        def _(sy):
            sy.dma_start(out=x_res[:].rearrange("p (t d) -> p t d", t=NT),
                         in_=xin[:].rearrange("(t p) d -> p t d", p=128)).then_inc(s_io, 16)
            sy.wait_ge(s_io, 16)
    emit_ln(ctx, x_res, xhatT, scr, "ln1")
    emit_l0ffn(ctx, x_res, xhatT, {"gw": gw, "gu": gu, "dw": dw}, scr)
    with ctx.nc.Block() as b:
        @b.sync
        def _(sy):
            sy.dma_start(out=xout[:].rearrange("(t p) d -> p t d", p=128),
                         in_=x_res[:].rearrange("p (t d) -> p t d", t=NT)).then_inc(s_io, 16)
            sy.wait_ge(s_io, 32)
    ctx.close()
    return ctx


def build_l1attn_prog():
    ctx = Ctx()
    nc = ctx.nc
    xin = nc.dram_tensor("xin", [S, D], F32, kind="ExternalInput").ap()
    dr = {k: nc.dram_tensor("i_" + k, shp, F32, kind="ExternalInput").ap()
          for k, shp in _L1A_SHAPES.items()}
    xout = nc.dram_tensor("xout", [S, D], F32, kind="ExternalOutput").ap()

    x_res = ctx.sbuf("x_res", [128, NT * D], F32)
    xhatT = ctx.sbuf("xhatT", [128, ND * S], BF16)
    scr = {
        "u1": ctx.sbuf("u1", [128, 8192], BF16),
        "v_tok": ctx.sbuf("v_tok", [128, NT * 1024], BF16),
        "o_tok": ctx.sbuf("o_tok", [128, NT * 1024], BF16),
        "e1": ctx.sbuf("e1", [128, 2048], BF16),
        "e2": ctx.sbuf("e2", [128, 2048], BF16),
        "aS": ctx.sbuf("aS", [128, 2048], BF16),
        "aT": ctx.sbuf("aT", [128, 2048], BF16),
        "rs": ctx.sbuf("rs", [128, 512], BF16),
        "C": ctx.sbuf("C", [128, 2048], BF16),
        "Sg": ctx.sbuf("Sg", [128, 2048], BF16),
        "cst": ctx.sbuf("cst", [128, 384], BF16),
        "ones": ctx.sbuf("onesb", [1, 128], BF16),
        "bqB": ctx.sbuf("bqB", [128, 16], F32),
        "bkB": ctx.sbuf("bkB", [128, 16], F32),
        "bvB": ctx.sbuf("bvB", [1, 1024], BF16),
        "boB": ctx.sbuf("boB", [1, 1024], BF16),
        "hnB": ctx.sbuf("hnB", [128, 8], F32),
        "lamB": ctx.sbuf("lamB", [128, 16], F32),
        "ssqB": ctx.sbuf("ssqB", [128, 256], F32),
        "st": ctx.sbuf("st", [128, 96], F32),
        "ps_s1": ctx.psum("ps_s1", [128, 2048]),
        "ps_s2": ctx.psum("ps_s2", [128, 1536]),
        "ps_o": ctx.psum("ps_o", [128, 512]),
    }
    scr["lqk"] = scr["e1"][:, 0:512].bitcast(F32)
    scr["xhat2"] = scr["v_tok"][:, 0:2 * D]
    scr["sq"] = scr["v_tok"][:, 2 * D:3 * D]
    scr["oT"] = xhatT
    s_io = ctx.sem("io")
    with ctx.nc.Block() as b:
        @b.sync
        def _(sy):
            sy.dma_start(out=x_res[:].rearrange("p (t d) -> p t d", t=NT),
                         in_=xin[:].rearrange("(t p) d -> p t d", p=128)).then_inc(s_io, 16)
            sy.wait_ge(s_io, 16)
    emit_ln(ctx, x_res, xhatT, scr, "ln1")
    emit_l1attn(ctx, x_res, xhatT, dr, scr)
    with ctx.nc.Block() as b:
        @b.sync
        def _(sy):
            sy.dma_start(out=xout[:].rearrange("(t p) d -> p t d", p=128),
                         in_=x_res[:].rearrange("p (t d) -> p t d", t=NT)).then_inc(s_io, 16)
            sy.wait_ge(s_io, 32)
    ctx.close()
    return ctx


def build_l1ffn_prog():
    ctx = Ctx()
    nc = ctx.nc
    xin = nc.dram_tensor("xin", [S, D], F32, kind="ExternalInput").ap()
    gw = nc.dram_tensor("i_gw", [128, 128], F32, kind="ExternalInput").ap()
    shp = nc.dram_tensor("i_shp", [8 * 128, 2048], F32, kind="ExternalInput").ap()
    sho = nc.dram_tensor("i_sho", [128, 8192], F32, kind="ExternalInput").ap()
    ep = nc.dram_tensor("i_ep", [16 * 4 * 128, 2048], F32, kind="ExternalInput").ap()
    eo = nc.dram_tensor("i_eo", [16 * 128, 4096], F32, kind="ExternalInput").ap()
    xout = nc.dram_tensor("xout", [S, D], F32, kind="ExternalOutput").ap()

    x_res = ctx.sbuf("x_res", [128, NT * D], F32)
    xhatT = ctx.sbuf("xhatT", [128, ND * S], BF16)
    h = ctx.sbuf("h", [128, 16 * 2048], BF16)
    scr = {
        "xhat2": h[:, 0:2 * D],
        "sq": h[:, 2 * D:3 * D],
        "st": ctx.sbuf("st", [128, 96], F32),
        "h": h,
        "guW": ctx.sbuf("guW", [128, 2 * 2048], BF16),
        "dwW": ctx.sbuf("dwW", [128, 16384], BF16),
        "sg": ctx.sbuf("sg", [128, 2 * 512], BF16),
        "comb": ctx.sbuf("comb", [128, 256], F32),
        "pex": ctx.sbuf("pex", [128, 16], F32),
        "gwB": ctx.sbuf("gwB", [128, 128], BF16),
        "pgu0": ctx.psum("pgu0", [128, 1024]),
        "pgu1": ctx.psum("pgu1", [128, 1024]),
        "pd0": ctx.psum("pd0", [128, 1024]),
        "pd1": ctx.psum("pd1", [128, 1024]),
    }
    s_io = ctx.sem("io")
    with ctx.nc.Block() as b:
        @b.sync
        def _(sy):
            sy.dma_start(out=x_res[:].rearrange("p (t d) -> p t d", t=NT),
                         in_=xin[:].rearrange("(t p) d -> p t d", p=128)).then_inc(s_io, 16)
            sy.wait_ge(s_io, 16)
    emit_ln(ctx, x_res, xhatT, scr, "ln1")
    emit_l1ffn(ctx, x_res, xhatT,
               {"gw": gw, "shp": shp, "sho": sho, "ep": ep, "eo": eo}, scr)
    with ctx.nc.Block() as b:
        @b.sync
        def _(sy):
            sy.dma_start(out=xout[:].rearrange("(t p) d -> p t d", p=128),
                         in_=x_res[:].rearrange("p (t d) -> p t d", t=NT)).then_inc(s_io, 16)
            sy.wait_ge(s_io, 32)
    ctx.close()
    return ctx


_CACHE = {}


def kernel_unfused(**inputs):
    inputs = {k: np.ascontiguousarray(np.asarray(v, dtype=np.float32))
              for k, v in inputs.items()}
    if "progs" not in _CACHE:
        _CACHE["progs"] = (build_l0attn_prog(), build_l0ffn_prog(),
                           build_l1attn_prog(), build_l1ffn_prog())
    p0a, p0f, p1a, p1f = _CACHE["progs"]
    w0a = prep_l0attn(inputs)
    w0f = prep_l0ffn(inputs)
    w1a = prep_l1attn(inputs)
    w1f = prep_l1ffn(inputs)

    x = inputs["L0_attn_hidden_states"]
    ncore = 8

    def launch(ctx, wmap, xcur):
        maps = []
        for c in range(ncore):
            m = {"xin": np.ascontiguousarray(xcur[c])}
            m.update({"i_" + k: v for k, v in wmap.items()})
            maps.append(m)
        res = run_bass_kernel_spmd(ctx.nc, maps, core_ids=list(range(ncore)))
        return np.stack([res.results[c]["xout"] for c in range(ncore)], axis=0)

    x = launch(p0a, w0a, x)
    x = launch(p0f, w0f, x)
    x = launch(p1a, w1a, x)
    x = launch(p1f, w1f, x)
    return x



def _alloc_ffn_scr(ctx, comb_cols, suf):
    h = ctx.sbuf("h" + suf, [128, 16 * 2048], BF16)
    scr = {
        "xhat2": h[:, 0:2 * D],
        "sq": h[:, 2 * D:3 * D],
        "st": ctx.sbuf("st" + suf, [128, 96], F32),
        "h": h,
        "guW": ctx.sbuf("guW" + suf, [128, 2 * 2048], BF16),
        "dwW": ctx.sbuf("dwW" + suf, [128, 16384], BF16),
        "sg": ctx.sbuf("sg" + suf, [128, 2 * 512], BF16),
        "comb": ctx.sbuf("comb" + suf, [128, comb_cols], F32),
        "pex": ctx.sbuf("pex" + suf, [128, 16], F32),
        "gwB": ctx.sbuf("gwB" + suf, [128, 128], BF16),
        "pgu0": ctx.psum("pgu0" + suf, [128, 1024]),
        "pgu1": ctx.psum("pgu1" + suf, [128, 1024]),
        "pd0": ctx.psum("pd0" + suf, [128, 1024]),
        "pd1": ctx.psum("pd1" + suf, [128, 1024]),
    }
    return scr


def _alloc_l1attn_scr(ctx, xhatT):
    scr = {
        "u1": ctx.sbuf("u1", [128, 8192], BF16),
        "v_tok": ctx.sbuf("v_tok1", [128, NT * 1024], BF16),
        "o_tok": ctx.sbuf("o_tok", [128, NT * 1024], BF16),
        "e1": ctx.sbuf("e1", [128, 2048], BF16),
        "e2": ctx.sbuf("e2", [128, 2048], BF16),
        "aS": ctx.sbuf("aS", [128, 2048], BF16),
        "aT": ctx.sbuf("aT", [128, 2048], BF16),
        "rs": ctx.sbuf("rs", [128, 512], BF16),
        "C": ctx.sbuf("C", [128, 2048], BF16),
        "Sg": ctx.sbuf("Sg", [128, 2048], BF16),
        "cst": ctx.sbuf("cst", [128, 384], BF16),
        "ones": ctx.sbuf("onesb", [1, 128], BF16),
        "bqB": ctx.sbuf("bqB", [128, 16], F32),
        "bkB": ctx.sbuf("bkB", [128, 16], F32),
        "bvB": ctx.sbuf("bvB", [1, 1024], BF16),
        "boB": ctx.sbuf("boB", [1, 1024], BF16),
        "hnB": ctx.sbuf("hnB", [128, 8], F32),
        "lamB": ctx.sbuf("lamB", [128, 16], F32),
        "ssqB": ctx.sbuf("ssqB", [128, 256], F32),
        "st": ctx.sbuf("st1a", [128, 96], F32),
        "ps_s1": ctx.psum("ps_s1", [128, 2048]),
        "ps_s2": ctx.psum("ps_s2", [128, 1536]),
        "ps_o": ctx.psum("ps_o", [128, 512]),
    }
    scr["lqk"] = scr["e1"][:, 0:512].bitcast(F32)
    scr["xhat2"] = scr["v_tok"][:, 0:2 * D]
    scr["sq"] = scr["v_tok"][:, 2 * D:3 * D]
    scr["oT"] = xhatT
    return scr


def build_fused_prog():
    ctx = Ctx()
    nc = ctx.nc
    xin = nc.dram_tensor("xin", [S, D], F32, kind="ExternalInput").ap()
    xout = nc.dram_tensor("xout", [S, D], F32, kind="ExternalOutput").ap()
    xmid = nc.dram_tensor("xmid", [S, D], F32, kind="Internal").ap()
    dr = {}
    for k, shp in _L0A_SHAPES.items():
        dr["a0_" + k] = nc.dram_tensor("a0_" + k, shp, F32, kind="ExternalInput").ap()
    for k, shp in [("gw", [128, 64]), ("gu", [8 * 16 * 128, 2048]),
                   ("dw", [8 * 128, 16384])]:
        dr["f0_" + k] = nc.dram_tensor("f0_" + k, shp, F32, kind="ExternalInput").ap()
    for k, shp in _L1A_SHAPES.items():
        dr["a1_" + k] = nc.dram_tensor("a1_" + k, shp, F32, kind="ExternalInput").ap()
    for k, shp in [("gw", [128, 128]), ("shp", [8 * 128, 2048]),
                   ("sho", [128, 8192]), ("ep", [16 * 4 * 128, 2048]),
                   ("eo", [16 * 128, 4096])]:
        dr["f1_" + k] = nc.dram_tensor("f1_" + k, shp, F32, kind="ExternalInput").ap()

    xhatT = ctx.sbuf("xhatT", [128, ND * S], BF16)

    ctx.push_scope()
    scr = alloc_l0attn_scr(ctx)
    emit_ln_stream(ctx, xin, xhatT, scr, "ln0")
    emit_l0attn(ctx, xin, xmid, xhatT,
                {k: dr["a0_" + k] for k in _L0A_SHAPES}, scr)
    ctx.pop_scope()

    x_res = ctx.sbuf("x_res", [128, NT * D], F32)
    s_io = ctx.sem("io")
    ctx.push_scope()
    scr = _alloc_ffn_scr(ctx, 128, "f0")
    with nc.Block() as b:
        @b.sync
        def _(sy):
            sy.dma_start(out=x_res[:].rearrange("p (t d) -> p t d", t=NT),
                         in_=xmid[:].rearrange("(t p) d -> p t d", p=128)).then_inc(s_io, 16)
            sy.wait_ge(s_io, 16)
    emit_ln(ctx, x_res, xhatT, scr, "ln1")
    emit_l0ffn(ctx, x_res, xhatT,
               {"gw": dr["f0_gw"], "gu": dr["f0_gu"], "dw": dr["f0_dw"]}, scr)
    ctx.pop_scope()

    ctx.push_scope()
    scr = _alloc_l1attn_scr(ctx, xhatT)
    emit_ln(ctx, x_res, xhatT, scr, "ln2")
    emit_l1attn(ctx, x_res, xhatT, {k: dr["a1_" + k] for k in _L1A_SHAPES}, scr)
    ctx.pop_scope()

    ctx.push_scope()
    scr = _alloc_ffn_scr(ctx, 256, "f1")
    emit_ln(ctx, x_res, xhatT, scr, "ln3")
    emit_l1ffn(ctx, x_res, xhatT,
               {"gw": dr["f1_gw"], "shp": dr["f1_shp"], "sho": dr["f1_sho"],
                "ep": dr["f1_ep"], "eo": dr["f1_eo"]}, scr)
    ctx.pop_scope()

    with nc.Block() as b:
        @b.sync
        def _(sy):
            sy.dma_start(out=xout[:].rearrange("(t p) d -> p t d", p=128),
                         in_=x_res[:].rearrange("p (t d) -> p t d", t=NT)).then_inc(s_io, 16)
            sy.wait_ge(s_io, 32)

    ctx.close()
    return ctx


def kernel_fused(**inputs):
    inputs = {k: np.ascontiguousarray(np.asarray(v, dtype=np.float32))
              for k, v in inputs.items()}
    if "fprog" not in _CACHE:
        _CACHE["fprog"] = build_fused_prog()
    ctx = _CACHE["fprog"]
    w0a = prep_l0attn(inputs)
    w0f = prep_l0ffn(inputs)
    w1a = prep_l1attn(inputs)
    w1f = prep_l1ffn(inputs)
    wall = {}
    wall.update({"a0_" + k: v for k, v in w0a.items()})
    wall.update({"f0_" + k: v for k, v in w0f.items()})
    wall.update({"a1_" + k: v for k, v in w1a.items()})
    wall.update({"f1_" + k: v for k, v in w1f.items()})
    x = inputs["L0_attn_hidden_states"]
    maps = []
    for c in range(8):
        m = {"xin": np.ascontiguousarray(x[c])}
        m.update(wall)
        maps.append(m)
    res = run_bass_kernel_spmd(ctx.nc, maps, core_ids=list(range(8)))
    return np.stack([res.results[c]["xout"] for c in range(8)], axis=0)


def kernel(**inputs):
    return kernel_fused(**inputs)


# version 10
# speedup vs baseline: 1.0708x; 1.0256x over previous
import contextlib
import numpy as np
import concourse.bass as bass
import concourse.mybir as mybir
from concourse.bass_utils import run_bass_kernel_spmd

F32 = mybir.dt.float32
BF16 = mybir.dt.bfloat16
AF = mybir.ActivationFunctionType
ALU = mybir.AluOpType
AX = mybir.AxisListType

S = 2048
D = 1024
NT = S // 128
ND = D // 128


class Ctx:
    def __init__(self):
        self.nc = bass.Bass("TRN2", target_bir_lowering=False)
        self.es = contextlib.ExitStack()
        self.scopes = [self.es]
        self.sems = {}
        self.cnt = {}

    def push_scope(self):
        es = contextlib.ExitStack()
        self.scopes.append(es)
        return es

    def pop_scope(self):
        self.scopes.pop().close()

    def sem(self, name):
        if name not in self.sems:
            self.sems[name] = self.es.enter_context(self.nc.semaphore(name))
            self.cnt[name] = 0
        return self.sems[name]

    def sbuf(self, name, shape, dtype):
        return self.scopes[-1].enter_context(self.nc.sbuf_tensor(name, shape, dtype))

    def psum(self, name, shape, dtype=F32):
        return self.scopes[-1].enter_context(self.nc.psum_tensor(name, shape, dtype))

    def close(self):
        while len(self.scopes) > 1:
            self.pop_scope()
        self.es.close()



def emit_ln(ctx, x_res, xhatT, scr, tag):
    nc = ctx.nc
    st = scr["st"]
    hat = ctx.sem("hat_" + tag)
    tr = ctx.sem("tr_" + tag)
    aq = ctx.sem("aq_" + tag)

    with nc.Block() as b:
        @b.vector
        def _(v):
            for t in range(NT):
                v.tensor_reduce(out=st[:, t:t + 1], in_=x_res[:, t * D:(t + 1) * D],
                                axis=AX.X, op=ALU.add)

        @b.scalar
        def _(sc):
            for t in range(NT):
                sc.activation(out=scr["sq"][:], in_=x_res[:, t * D:(t + 1) * D],
                              func=AF.Square, accum_out=st[:, 16 + t:17 + t])

    with nc.Block() as b:
        @b.vector
        def _(v):
            v.tensor_scalar(out=st[:, 32:48], in0=st[:, 0:16], scalar1=1.0 / D,
                            scalar2=None, op0=ALU.mult)
            v.tensor_scalar(out=st[:, 48:64], in0=st[:, 16:32], scalar1=1.0 / D,
                            scalar2=None, op0=ALU.mult)
            v.drain()
            v.tensor_tensor(out=st[:, 64:80], in0=st[:, 32:48], in1=st[:, 32:48],
                            op=ALU.mult)
            v.drain()
            v.tensor_tensor(out=st[:, 80:96], in0=st[:, 48:64], in1=st[:, 64:80],
                            op=ALU.subtract)
            v.drain()
            v.tensor_scalar(out=st[:, 80:96], in0=st[:, 80:96], scalar1=1e-5,
                            scalar2=None, op0=ALU.add)
            v.tensor_scalar(out=st[:, 0:16], in0=st[:, 32:48], scalar1=-1.0,
                            scalar2=None, op0=ALU.mult)

    with nc.Block() as b:
        @b.scalar
        def _(sc):
            sc.activation(out=st[:, 48:64], in_=st[:, 80:96], func=AF.Sqrt)

    with nc.Block() as b:
        @b.vector
        def _(v):
            v.reciprocal(out=st[:, 16:32], in_=st[:, 48:64])
            v.drain()
            for t in range(NT):
                if t >= 2:
                    v.wait_ge(tr, ctx.cnt["tr_" + tag] + (t - 1) * 16)
                v.tensor_scalar(out=scr["xhat2"][:, (t % 2) * D:(t % 2 + 1) * D],
                                in0=x_res[:, t * D:(t + 1) * D],
                                scalar1=st[:, t:t + 1], scalar2=st[:, 16 + t:17 + t],
                                op0=ALU.add, op1=ALU.mult).then_inc(hat, 1)

        @b.sync
        def _(sy):
            for t in range(NT):
                sy.wait_ge(hat, ctx.cnt["hat_" + tag] + t + 1)
                sy.dma_start(
                    out=xhatT[:].rearrange("p (k s) -> p k s", k=ND)[:, :, t * 128:(t + 1) * 128],
                    in_=scr["xhat2"][:, (t % 2) * D:(t % 2 + 1) * D],
                    transpose=True).then_inc(tr, 16)
            sy.wait_ge(tr, ctx.cnt["tr_" + tag] + NT * 16).then_inc(aq, 1)

    ctx.cnt["hat_" + tag] += NT
    ctx.cnt["tr_" + tag] += NT * 16
    ctx.cnt["aq_" + tag] += 1



def emit_l0ffn(ctx, x_res, xhatT, aps, scr, experts=True):
    import os
    rpart = int(os.environ.get("RPART", "3"))
    """aps: dict with dram APs: gw [128,64], gu [8*16*128, 2048], dw [8*128, 16384].
    scr: h [128, 16*2048] bf16, guW [128, 2*2048] bf16, dwW [128, 16384] bf16,
         sg [128, 2*512] bf16, comb [128, 128] f32, st [128,96] f32 (reuse),
         pex [128, 8] f32 (exp p scratch), gwB [128, 64] bf16.
    psum: pgu 2x[128,1024], pd 2x[128,1024] (use scr['pgu0'],...)
    """
    nc = ctx.nc
    st = scr["st"]
    comb = scr["comb"]
    s_ld = ctx.sem("ffn_ld")
    s_gmm = ctx.sem("ffn_gmm")
    s_sg = ctx.sem("ffn_sg")
    s_h = ctx.sem("ffn_h")
    s_dmm = ctx.sem("ffn_dmm")
    s_acc = ctx.sem("ffn_acc")
    s_gu = ctx.sem("ffn_gu")
    s_dw = ctx.sem("ffn_dw")
    pgu = [scr["pgu0"], scr["pgu1"]]
    pd = [scr["pd0"], scr["pd1"]]

    s_rlog = ctx.sem("ffn_rlog")
    s_rmax = ctx.sem("ffn_rmax")
    s_rexp = ctx.sem("ffn_rexp")
    s_rdone = ctx.sem("ffn_rdone")
    with nc.Block() as b:
        @b.gpsimd
        def _(g):
            g.dma_start(out=scr["gwB"][:, 0:64], in_=aps["gw"][:, :]).then_inc(s_ld, 16)

        @b.tensor
        def _(t):
            t.wait_ge(s_ld, ctx.cnt["ffn_ld"] + 16)
            for ti in range(NT):
                if ti >= 2:
                    t.wait_ge(s_rdone, ti - 1)
                for kt in range(ND):
                    ins = t.matmul(pgu[ti % 2][:, 0:8],
                             xhatT[:, kt * S + ti * 128: kt * S + (ti + 1) * 128],
                             scr["gwB"][:, kt * 8:(kt + 1) * 8],
                             start=(kt == 0), stop=(kt == ND - 1))
                ins.then_inc(s_rlog, 1)

        @b.vector
        def _(v):
            for ti in range(NT):
                lg = pgu[ti % 2][:, 0:8]
                v.wait_ge(s_rlog, ti + 1)
                if rpart < 2:
                    v.tensor_copy(out=comb[:, ti * 8:(ti + 1) * 8], in_=lg).then_inc(s_rmax, 1)
                    v.nop().then_inc(s_rdone, 1)
                    continue
                v.tensor_reduce(out=st[:, 0:1], in_=lg, axis=AX.X, op=ALU.max)
                v.drain()
                v.tensor_scalar(out=st[:, 1:2], in0=st[:, 0:1], scalar1=-1.0,
                                scalar2=None, op0=ALU.mult).then_inc(s_rmax, 1)
                v.wait_ge(s_rexp, ti + 1)
                pex = scr["pex"][:, 0:8]
                if rpart < 3:
                    v.tensor_copy(out=comb[:, ti * 8:(ti + 1) * 8], in_=pex).then_inc(s_rdone, 1)
                    continue
                v.tensor_reduce(out=st[:, 2:3], in_=pex, axis=AX.X, op=ALU.max)
                v.drain()
                cslice = comb[:, ti * 8:(ti + 1) * 8]
                v.tensor_scalar(out=cslice, in0=pex, scalar1=st[:, 2:3],
                                scalar2=None, op0=ALU.is_ge)
                v.drain()
                v.scalar_tensor_tensor(out=cslice, in0=cslice, scalar=-1e30,
                                       in1=pex, op0=ALU.mult, op1=ALU.add)
                v.drain()
                v.tensor_reduce(out=st[:, 4:5], in_=cslice, axis=AX.X, op=ALU.max)
                v.drain()
                v.tensor_tensor(out=st[:, 5:6], in0=st[:, 2:3], in1=st[:, 4:5], op=ALU.add)
                v.drain()
                v.reciprocal(out=st[:, 6:7], in_=st[:, 5:6])
                v.drain()
                v.tensor_scalar(out=cslice, in0=pex, scalar1=st[:, 4:5],
                                scalar2=None, op0=ALU.is_ge)
                v.drain()
                v.tensor_tensor(out=cslice, in0=cslice, in1=pex, op=ALU.mult)
                v.drain()
                v.tensor_scalar(out=cslice, in0=cslice, scalar1=st[:, 6:7],
                                scalar2=None, op0=ALU.mult).then_inc(s_rdone, 1)

        @b.scalar
        def _(sc):
            for ti in range(NT):
                sc.wait_ge(s_rmax, ti + 1)
                if rpart < 2:
                    sc.nop().then_inc(s_rexp, 1)
                    continue
                sc.activation(out=scr["pex"][:, 0:8], in_=pgu[ti % 2][:, 0:8], func=AF.Exp,
                              bias=st[:, 1:2]).then_inc(s_rexp, 1)

    ctx.cnt["ffn_ld"] += 16

    if not experts:
        return

    E = 8
    with nc.Block() as b:
        @b.gpsimd
        def _(g):
            def load_pair(e, f):
                p = e * 16 + f
                if p >= 2:
                    g.wait_ge(s_gmm, ctx.cnt["ffn_gmm"] + (p - 1) * 4)
                g.dma_start(out=scr["guW"][:, (f % 2) * 2048:(f % 2 + 1) * 2048],
                            in_=aps["gu"][p * 128:(p + 1) * 128, :]).then_inc(s_gu, 16)

            for e in range(E):
                load_pair(e, 0)
                load_pair(e, 1)
                if e >= 1:
                    g.wait_ge(s_dmm, ctx.cnt["ffn_dmm"] + e * 16)
                g.dma_start(out=scr["dwW"][:],
                            in_=aps["dw"][e * 128:(e + 1) * 128, :]).then_inc(s_dw, 16)
                for f in range(2, 16):
                    load_pair(e, f)

        @b.tensor
        def _(t):
            for e in range(E):
                for f in range(16):
                    t.wait_ge(s_gu, ctx.cnt["ffn_gu"] + (e * 16 + f + 1) * 16)
                    for c in range(4):
                        cc = e * 64 + f * 4 + c
                        if cc >= 2:
                            t.wait_ge(s_h, ctx.cnt["ffn_h"] + cc - 1)
                        gub = (f % 2) * 2048
                        for kt in range(ND):
                            t.matmul(pgu[c % 2][:, 0:512],
                                     scr["guW"][:, gub + kt * 256: gub + kt * 256 + 128],
                                     xhatT[:, kt * S + c * 512: kt * S + (c + 1) * 512],
                                     start=(kt == 0), stop=(kt == ND - 1))
                        for kt in range(ND):
                            ins = t.matmul(pgu[c % 2][:, 512:1024],
                                     scr["guW"][:, gub + kt * 256 + 128: gub + kt * 256 + 256],
                                     xhatT[:, kt * S + c * 512: kt * S + (c + 1) * 512],
                                     start=(kt == 0), stop=(kt == ND - 1))
                        ins.then_inc(s_gmm, 1)
                t.wait_ge(s_h, ctx.cnt["ffn_h"] + (e + 1) * 64)
                t.wait_ge(s_dw, ctx.cnt["ffn_dw"] + (e + 1) * 16)
                for tt in range(16):
                    if e * 16 + tt >= 2:
                        t.wait_ge(s_acc, ctx.cnt["ffn_acc"] + e * 16 + tt - 1)
                    for ft in range(16):
                        for nn in range(2):
                            ins = t.matmul(pd[tt % 2][:, nn * 512:(nn + 1) * 512],
                                     scr["h"][:, ft * 2048 + tt * 128: ft * 2048 + (tt + 1) * 128],
                                     scr["dwW"][:, ft * 1024 + nn * 512: ft * 1024 + (nn + 1) * 512],
                                     start=(ft == 0), stop=(ft == 15))
                    ins.then_inc(s_dmm, 1)

        @b.scalar
        def _(sc):
            for e in range(E):
                for f in range(16):
                    for c in range(4):
                        cc = e * 64 + f * 4 + c
                        sc.wait_ge(s_gmm, ctx.cnt["ffn_gmm"] + cc + 1)
                        if cc >= 2:
                            sc.wait_ge(s_h, ctx.cnt["ffn_h"] + cc - 1)
                        sc.activation(out=scr["sg"][:, (c % 2) * 512:(c % 2 + 1) * 512],
                                      in_=pgu[c % 2][:, 0:512],
                                      func=AF.Silu).then_inc(s_sg, 1)

        @b.vector
        def _(v):
            for e in range(E):
                for f in range(16):
                    for c in range(4):
                        cc = e * 64 + f * 4 + c
                        v.wait_ge(s_sg, ctx.cnt["ffn_sg"] + cc + 1)
                        v.tensor_tensor(
                            out=scr["h"][:, f * 2048 + c * 512: f * 2048 + (c + 1) * 512],
                            in0=pgu[c % 2][:, 512:1024],
                            in1=scr["sg"][:, (c % 2) * 512:(c % 2 + 1) * 512],
                            op=ALU.mult).then_inc(s_h, 1)
                for tt in range(16):
                    v.wait_ge(s_dmm, ctx.cnt["ffn_dmm"] + e * 16 + tt + 1)
                    v.scalar_tensor_tensor(
                        out=x_res[:, tt * D:(tt + 1) * D],
                        in0=pd[tt % 2][:, :],
                        scalar=comb[:, tt * 8 + e: tt * 8 + e + 1],
                        in1=x_res[:, tt * D:(tt + 1) * D],
                        op0=ALU.mult, op1=ALU.add).then_inc(s_acc, 1)

    ctx.cnt["ffn_gu"] += E * 16 * 16
    ctx.cnt["ffn_dw"] += E * 16
    ctx.cnt["ffn_gmm"] += E * 64
    ctx.cnt["ffn_sg"] += E * 64
    ctx.cnt["ffn_h"] += E * 64
    ctx.cnt["ffn_dmm"] += E * 16
    ctx.cnt["ffn_acc"] += E * 16



def prep_l0ffn(inp):
    gw = np.ascontiguousarray(
        inp["L0_ffn_gate_w"].reshape(8, 128, 8).transpose(1, 0, 2).reshape(128, 64))
    guw = inp["L0_ffn_gate_up_w"]
    pairs = np.empty((8, 16, 128, 2048), dtype=np.float32)
    for f in range(16):
        blk = np.concatenate([guw[:, :, f * 128:(f + 1) * 128],
                              guw[:, :, 2048 + f * 128: 2048 + (f + 1) * 128]],
                             axis=2)
        pairs[:, f] = blk.reshape(8, 8, 128, 256).transpose(0, 2, 1, 3).reshape(8, 128, 2048)
    gu = pairs.reshape(8 * 16 * 128, 2048)
    dw = inp["L0_ffn_down_w"].reshape(8, 16, 128, 1024).transpose(0, 2, 1, 3).reshape(8 * 128, 16384)
    return {"gw": gw, "gu": np.ascontiguousarray(gu), "dw": np.ascontiguousarray(dw)}



def emit_l1ffn(ctx, x_res, xhatT, aps, scr, nexp=16):
    nc = ctx.nc
    st = scr["st"]
    comb = scr["comb"]
    pgu = [scr["pgu0"], scr["pgu1"]]
    pd = [scr["pd0"], scr["pd1"]]
    s_gu = ctx.sem("f1_gu")
    s_dw = ctx.sem("f1_dw")
    s_gmm = ctx.sem("f1_gmm")
    s_sg = ctx.sem("f1_sg")
    s_h = ctx.sem("f1_h")
    s_dmm = ctx.sem("f1_dmm")
    s_acc = ctx.sem("f1_acc")
    s_ld = ctx.sem("f1_ld")
    s_glog = ctx.sem("f1_glog")
    s_gsig = ctx.sem("f1_gsig")
    s_gdone = ctx.sem("f1_gdone")

    with nc.Block() as b:
        @b.gpsimd
        def _(g):
            g.dma_start(out=scr["gwB"][:, 0:128], in_=aps["gw"][:, :]).then_inc(s_ld, 16)

        @b.tensor
        def _(t):
            t.wait_ge(s_ld, 16)
            for ti in range(NT):
                if ti >= 2:
                    t.wait_ge(s_gdone, ti - 1)
                for kt in range(ND):
                    ins = t.matmul(pgu[ti % 2][:, 0:16],
                                   xhatT[:, kt * S + ti * 128: kt * S + (ti + 1) * 128],
                                   scr["gwB"][:, kt * 16:(kt + 1) * 16],
                                   start=(kt == 0), stop=(kt == ND - 1))
                ins.then_inc(s_glog, 1)

        @b.scalar
        def _(sc):
            for ti in range(NT):
                sc.wait_ge(s_glog, ti + 1)
                if ti >= 1:
                    sc.wait_ge(s_gdone, ti)
                sc.activation(out=scr["pex"][:, 0:16], in_=pgu[ti % 2][:, 0:16],
                              func=AF.Sigmoid).then_inc(s_gsig, 1)

        @b.vector
        def _(v):
            for ti in range(NT):
                v.wait_ge(s_gsig, ti + 1)
                gs = scr["pex"][:, 0:16]
                cs = comb[:, ti * 16:(ti + 1) * 16]
                v.tensor_copy(out=cs, in_=gs)
                v.drain()
                for k in range(4):
                    v.tensor_reduce(out=st[:, 9:10], in_=cs, axis=AX.X, op=ALU.max)
                    v.drain()
                    if k == 0:
                        v.tensor_copy(out=st[:, 8:9], in_=st[:, 9:10])
                    else:
                        v.tensor_tensor(out=st[:, 8:9], in0=st[:, 8:9], in1=st[:, 9:10],
                                        op=ALU.add)
                    v.drain()
                    if k < 3:
                        v.tensor_scalar(out=st[:, 10:26], in0=cs, scalar1=st[:, 9:10],
                                        scalar2=None, op0=ALU.is_ge)
                        v.drain()
                        v.scalar_tensor_tensor(out=cs, in0=st[:, 10:26], scalar=-1e30,
                                               in1=cs, op0=ALU.mult, op1=ALU.add)
                        v.drain()
                v.reciprocal(out=st[:, 8:9], in_=st[:, 8:9])
                v.drain()
                v.tensor_scalar(out=cs, in0=gs, scalar1=st[:, 9:10],
                                scalar2=None, op0=ALU.is_ge)
                v.drain()
                v.tensor_tensor(out=cs, in0=cs, in1=gs, op=ALU.mult)
                v.drain()
                v.tensor_scalar(out=cs, in0=cs, scalar1=st[:, 8:9],
                                scalar2=None, op0=ALU.mult).then_inc(s_gdone, 1)

    with nc.Block() as b:
        @b.gpsimd
        def _(g):
            def load_pair(p, src_row):
                if p >= 2:
                    g.wait_ge(s_gmm, (p - 1) * 4)
                g.dma_start(out=scr["guW"][:, (p % 2) * 2048:(p % 2 + 1) * 2048],
                            in_=aps[src_row[0]][src_row[1] * 128:(src_row[1] + 1) * 128, :]
                            ).then_inc(s_gu, 16)

            pairs = [("shp", f) for f in range(8)]
            for e in range(nexp):
                pairs += [("ep", e * 4 + f) for f in range(4)]
            pi = 0
            g.dma_start(out=scr["dwW"][:, 0:8192],
                        in_=aps["sho"][:, :]).then_inc(s_dw, 16)
            for f in range(8):
                load_pair(pi, pairs[pi]); pi += 1
            for e in range(nexp):
                if e >= 2:
                    g.wait_ge(s_dmm, (e - 1) * NT + NT)
                g.dma_start(out=scr["dwW"][:, 8192 + (e % 2) * 4096: 8192 + (e % 2 + 1) * 4096],
                            in_=aps["eo"][e * 128:(e + 1) * 128, :]).then_inc(s_dw, 16)
                for f in range(4):
                    load_pair(pi, pairs[pi]); pi += 1

        @b.tensor
        def _(t):
            pcnt = 0
            ccnt = 0
            dcnt = 0
            for f in range(8):
                t.wait_ge(s_gu, (pcnt + 1) * 16)
                for c in range(4):
                    if ccnt >= 2:
                        t.wait_ge(s_h, ccnt - 1)
                    gub = (pcnt % 2) * 2048
                    for kt in range(ND):
                        t.matmul(pgu[c % 2][:, 0:512],
                                 scr["guW"][:, gub + kt * 256: gub + kt * 256 + 128],
                                 xhatT[:, kt * S + c * 512: kt * S + (c + 1) * 512],
                                 start=(kt == 0), stop=(kt == ND - 1))
                    for kt in range(ND):
                        ins = t.matmul(pgu[c % 2][:, 512:1024],
                                       scr["guW"][:, gub + kt * 256 + 128: gub + kt * 256 + 256],
                                       xhatT[:, kt * S + c * 512: kt * S + (c + 1) * 512],
                                       start=(kt == 0), stop=(kt == ND - 1))
                    ins.then_inc(s_gmm, 1)
                    ccnt += 1
                pcnt += 1
            t.wait_ge(s_h, ccnt)
            t.wait_ge(s_dw, 16)
            for tt in range(NT):
                if tt >= 2:
                    t.wait_ge(s_acc, dcnt + tt - 1)
                for ft in range(8):
                    for nn in range(2):
                        ins = t.matmul(pd[tt % 2][:, nn * 512:(nn + 1) * 512],
                                       scr["h"][:, ft * 2048 + tt * 128: ft * 2048 + (tt + 1) * 128],
                                       scr["dwW"][:, ft * 1024 + nn * 512: ft * 1024 + (nn + 1) * 512],
                                       start=(ft == 0), stop=(ft == 7))
                ins.then_inc(s_dmm, 1)
            dcnt += NT
            for e in range(nexp):
                t.wait_ge(s_dw, 32 + e * 16)
                for f in range(4):
                    t.wait_ge(s_gu, (pcnt + 1) * 16)
                    for c in range(4):
                        if ccnt >= 2:
                            t.wait_ge(s_h, ccnt - 1)
                        gub = (pcnt % 2) * 2048
                        for kt in range(ND):
                            t.matmul(pgu[c % 2][:, 0:512],
                                     scr["guW"][:, gub + kt * 256: gub + kt * 256 + 128],
                                     xhatT[:, kt * S + c * 512: kt * S + (c + 1) * 512],
                                     start=(kt == 0), stop=(kt == ND - 1))
                        for kt in range(ND):
                            ins = t.matmul(pgu[c % 2][:, 512:1024],
                                           scr["guW"][:, gub + kt * 256 + 128: gub + kt * 256 + 256],
                                           xhatT[:, kt * S + c * 512: kt * S + (c + 1) * 512],
                                           start=(kt == 0), stop=(kt == ND - 1))
                        ins.then_inc(s_gmm, 1)
                        ccnt += 1
                    pcnt += 1
                t.wait_ge(s_h, ccnt)
                for tt in range(NT):
                    if dcnt + tt >= 2:
                        t.wait_ge(s_acc, dcnt + tt - 1)
                    ob = 8192 + (e % 2) * 4096
                    for ft in range(4):
                        for nn in range(2):
                            ins = t.matmul(pd[tt % 2][:, nn * 512:(nn + 1) * 512],
                                           scr["h"][:, ft * 2048 + tt * 128: ft * 2048 + (tt + 1) * 128],
                                           scr["dwW"][:, ob + ft * 1024 + nn * 512: ob + ft * 1024 + (nn + 1) * 512],
                                           start=(ft == 0), stop=(ft == 3))
                    ins.then_inc(s_dmm, 1)
                dcnt += NT

        @b.scalar
        def _(sc):
            ccnt = 0
            for blk in range(1 + nexp):
                npair = 8 if blk == 0 else 4
                for f in range(npair):
                    for c in range(4):
                        sc.wait_ge(s_gmm, ccnt + 1)
                        if ccnt >= 2:
                            sc.wait_ge(s_h, ccnt - 1)
                        sc.activation(out=scr["sg"][:, (c % 2) * 512:(c % 2 + 1) * 512],
                                      in_=pgu[c % 2][:, 0:512],
                                      func=AF.Silu).then_inc(s_sg, 1)
                        ccnt += 1

        @b.vector
        def _(v):
            ccnt = 0
            dcnt = 0
            for blk in range(1 + nexp):
                npair = 8 if blk == 0 else 4
                for f in range(npair):
                    for c in range(4):
                        v.wait_ge(s_sg, ccnt + 1)
                        v.tensor_tensor(
                            out=scr["h"][:, f * 2048 + c * 512: f * 2048 + (c + 1) * 512],
                            in0=pgu[c % 2][:, 512:1024],
                            in1=scr["sg"][:, (c % 2) * 512:(c % 2 + 1) * 512],
                            op=ALU.mult).then_inc(s_h, 1)
                        ccnt += 1
                for tt in range(NT):
                    v.wait_ge(s_dmm, dcnt + tt + 1)
                    if blk == 0:
                        v.tensor_tensor(out=x_res[:, tt * D:(tt + 1) * D],
                                        in0=pd[tt % 2][:, :],
                                        in1=x_res[:, tt * D:(tt + 1) * D],
                                        op=ALU.add).then_inc(s_acc, 1)
                    else:
                        e = blk - 1
                        v.scalar_tensor_tensor(
                            out=x_res[:, tt * D:(tt + 1) * D],
                            in0=pd[tt % 2][:, :],
                            scalar=comb[:, tt * 16 + e: tt * 16 + e + 1],
                            in1=x_res[:, tt * D:(tt + 1) * D],
                            op0=ALU.mult, op1=ALU.add).then_inc(s_acc, 1)
                dcnt += NT


def prep_l1ffn(inp):
    gw = np.ascontiguousarray(
        inp["L1_ffn_gate_w"].T.reshape(8, 128, 16).transpose(1, 0, 2).reshape(128, 128))
    wsi, vsi = inp["L1_ffn_ws_in"], inp["L1_ffn_vs_in"]
    shp = np.empty((8, 128, 2048), np.float32)
    for f in range(8):
        blk = np.concatenate([wsi[:, f * 128:(f + 1) * 128],
                              vsi[:, f * 128:(f + 1) * 128]], axis=1)
        shp[f] = blk.reshape(8, 128, 256).transpose(1, 0, 2).reshape(128, 2048)
    sho = inp["L1_ffn_ws_out"].reshape(8, 128, 1024).transpose(1, 0, 2).reshape(128, 8192)
    ewin, evin = inp["L1_ffn_e_win"], inp["L1_ffn_e_vin"]
    ep = np.empty((16, 4, 128, 2048), np.float32)
    for e in range(16):
        for f in range(4):
            blk = np.concatenate([ewin[e][:, f * 128:(f + 1) * 128],
                                  evin[e][:, f * 128:(f + 1) * 128]], axis=1)
            ep[e, f] = blk.reshape(8, 128, 256).transpose(1, 0, 2).reshape(128, 2048)
    eo = inp["L1_ffn_e_wout"].reshape(16, 4, 128, 1024).transpose(0, 2, 1, 3).reshape(16 * 128, 4096)
    return {"gw": gw, "shp": np.ascontiguousarray(shp.reshape(8 * 128, 2048)),
            "sho": np.ascontiguousarray(sho),
            "ep": np.ascontiguousarray(ep.reshape(16 * 4 * 128, 2048)),
            "eo": np.ascontiguousarray(eo)}



def emit_l1attn(ctx, x_res, xhatT, aps, scr):
    nc = ctx.nc
    st = scr["st"]
    u1 = scr["u1"]
    I128 = scr["cst"][:, 0:128]
    maskB = scr["cst"][:, 128:256]
    permB = scr["cst"][:, 256:384]
    ones = scr["ones"]
    s1 = scr["ps_s1"]
    s2 = scr["ps_s2"]
    G = 2
    NG = 16 // G

    s_ld = ctx.sem("a1_ld")
    s_vmm = ctx.sem("a1_vmm")
    s_vev = ctx.sem("a1_vev")
    s_lam = ctx.sem("a1_lam")

    with nc.Block() as b:
        @b.gpsimd
        def _(g):
            for key, dst in [("C", scr["C"][:]), ("Sg", scr["Sg"][:]),
                             ("cst", scr["cst"][:]), ("ones", scr["ones"][:]),
                             ("bq", scr["bqB"][:]), ("bk", scr["bkB"][:]),
                             ("bv", scr["bvB"][:]), ("bo", scr["boB"][:]),
                             ("lqk", scr["lqk"][0:16, :]), ("hn", scr["hnB"][:]),
                             ("wv", u1[:])]:
                g.dma_start(out=dst, in_=aps[key][:, :]).then_inc(s_ld, 16)

        @b.tensor
        def _(t):
            t.wait_ge(s_ld, ctx.cnt["a1_ld"] + 11 * 16)
            pb = [s1[:, 0:512], s1[:, 512:1024], s2[:, 0:512], s2[:, 512:1024]]
            for tt in range(NT):
                for nn in range(2):
                    idx = tt * 2 + nn
                    if idx >= 4:
                        t.wait_ge(s_vev, ctx.cnt["a1_vev"] + idx - 3)
                    out = pb[idx % 4]
                    for kt in range(ND):
                        t.matmul(out, xhatT[:, kt * S + tt * 128:kt * S + (tt + 1) * 128],
                                 u1[:, kt * 1024 + nn * 512: kt * 1024 + (nn + 1) * 512],
                                 start=(kt == 0), stop=False)
                    ins = t.matmul(out, ones[0:1, 0:128],
                                   scr["bvB"][0:1, nn * 512:(nn + 1) * 512],
                                   start=False, stop=True)
                    ins.then_inc(s_vmm, 1)

        @b.vector
        def _(v):
            pb = [s1[:, 0:512], s1[:, 512:1024], s2[:, 0:512], s2[:, 512:1024]]
            for tt in range(NT):
                for nn in range(2):
                    idx = tt * 2 + nn
                    v.wait_ge(s_vmm, ctx.cnt["a1_vmm"] + idx + 1)
                    v.tensor_copy(out=scr["v_tok"][:, tt * 1024 + nn * 512: tt * 1024 + (nn + 1) * 512],
                                  in_=pb[idx % 4]).then_inc(s_vev, 1)
            lqk = scr["lqk"]
            v.tensor_tensor(out=st[0:16, 32:96], in0=lqk[0:16, 0:64],
                            in1=lqk[0:16, 64:128], op=ALU.mult)
            v.tensor_reduce(out=st[0:16, 0:1], in_=st[0:16, 32:96], axis=AX.X, op=ALU.add)
            v.drain()
            v.tensor_tensor(out=st[0:16, 32:96], in0=lqk[0:16, 128:192],
                            in1=lqk[0:16, 192:256], op=ALU.mult)
            v.tensor_reduce(out=st[0:16, 1:2], in_=st[0:16, 32:96], axis=AX.X, op=ALU.add)
            v.drain()
            v.nop().then_inc(s_lam, 1)

    ctx.cnt["a1_ld"] += 11 * 16
    ctx.cnt["a1_vmm"] += 32
    ctx.cnt["a1_vev"] += 32

    with nc.Block() as b:
        @b.scalar
        def _(sc):
            sc.wait_ge(s_lam, 1)
            sc.activation(out=st[0:16, 2:3], in_=st[0:16, 0:1], func=AF.Exp)
            sc.activation(out=st[0:16, 3:4], in_=st[0:16, 1:2], func=AF.Exp)

    with nc.Block() as b:
        @b.vector
        def _(v):
            v.tensor_tensor(out=st[0:16, 4:5], in0=st[0:16, 2:3], in1=st[0:16, 3:4],
                            op=ALU.subtract)
            v.drain()
            v.tensor_scalar(out=st[0:16, 4:5], in0=st[0:16, 4:5], scalar1=0.2,
                            scalar2=None, op0=ALU.add)

    lam_d = nc.dram_tensor("lam_scratch", [16], F32, kind="Internal").ap()
    with nc.Block() as b:
        @b.sync
        def _(sy):
            sy.dma_start(out=lam_d[:], in_=st[0:16, 4:5]).then_inc(s_ld, 16)
            sy.wait_ge(s_ld, ctx.cnt["a1_ld"] + 16)

    with nc.Block() as b:
        @b.gpsimd
        def _(g):
            g.dma_start(out=scr["lamB"][:],
                        in_=lam_d[:].unsqueeze(0).broadcast_to([128, 16])).then_inc(s_ld, 16)
            g.wait_ge(s_ld, ctx.cnt["a1_ld"] + 32)
    ctx.cnt["a1_ld"] += 32

    s_pmm = ctx.sem("a1_pmm")
    s_pev = ctx.sem("a1_pev")
    s_rmm = ctx.sem("a1_rmm")
    s_rdv = ctx.sem("a1_rdv")
    s_qkc = ctx.sem("a1_qkc")
    s_smm = ctx.sem("a1_smm")
    s_mx = ctx.sem("a1_mx")
    s_pexp = ctx.sem("a1_pexp")
    s_att = ctx.sem("a1_att")
    s_trD = ctx.sem("a1_trD")
    s_av = ctx.sem("a1_av")
    s_oev = ctx.sem("a1_oev")
    s_wld = ctx.sem("a1_wld")
    stg = None

    for g_i in range(NG):
        stg = [scr["e1"], scr["e2"], scr["aS"], scr["aT"]]
        with nc.Block() as b:
            @b.gpsimd
            def _(g):
                g.dma_start(out=u1[:, 0:2048],
                            in_=aps["wq"][g_i * 128:(g_i + 1) * 128, :]).then_inc(s_wld, 16)
                g.dma_start(out=u1[:, 2048:4096],
                            in_=aps["wk"][g_i * 128:(g_i + 1) * 128, :]).then_inc(s_wld, 16)

            @b.tensor
            def _(t):
                t.wait_ge(s_wld, ctx.cnt["a1_wld"] + 32)
                pb = [s1[:, 0:512], s1[:, 512:1024], s1[:, 1024:1536], s1[:, 1536:2048],
                      s2[:, 0:512], s2[:, 512:1024], s2[:, 1024:1536], scr["ps_o"][:, 0:512]]
                cnt = 0
                for qk in range(2):
                    wsl = u1[:, 0:2048] if qk == 0 else u1[:, 2048:4096]
                    for j in range(G):
                        for c in range(4):
                            if cnt >= 8:
                                t.wait_ge(s_pev, ctx.cnt["a1_pev"] + cnt - 7)
                            out = pb[cnt % 8]
                            for kt in range(ND):
                                ins = t.matmul(
                                    out,
                                    wsl[:, kt * 256 + j * 128: kt * 256 + (j + 1) * 128],
                                    xhatT[:, kt * S + c * 512: kt * S + (c + 1) * 512],
                                    start=(kt == 0), stop=(kt == ND - 1))
                            ins.then_inc(s_pmm, 1)
                            cnt += 1

            @b.scalar
            def _(sc):
                pb = [s1[:, 0:512], s1[:, 512:1024], s1[:, 1024:1536], s1[:, 1536:2048],
                      s2[:, 0:512], s2[:, 512:1024], s2[:, 1024:1536], scr["ps_o"][:, 0:512]]
                cnt = 0
                for qk in range(2):
                    bias = scr["bqB"] if qk == 0 else scr["bkB"]
                    for j in range(G):
                        dst = stg[qk * G + j]
                        for c in range(4):
                            sc.wait_ge(s_pmm, ctx.cnt["a1_pmm"] + cnt + 1)
                            sc.activation(out=dst[:, c * 512:(c + 1) * 512],
                                          in_=pb[cnt % 8], func=AF.Identity,
                                          bias=bias[:, g_i * G + j: g_i * G + j + 1]
                                          ).then_inc(s_pev, 1)
                            cnt += 1

        ctx.cnt["a1_wld"] += 32
        ctx.cnt["a1_pmm"] += 16
        ctx.cnt["a1_pev"] += 16

        with nc.Block() as b:
            @b.tensor
            def _(t):
                pb = [s1[:, 0:512], s1[:, 512:1024], s1[:, 1024:1536], s1[:, 1536:2048],
                      s2[:, 0:512], s2[:, 512:1024], s2[:, 1024:1536], scr["ps_o"][:, 0:512]]
                cnt = 0
                for sidx in range(4):
                    src = stg[sidx]
                    for c in range(4):
                        if cnt >= 8:
                            t.wait_ge(s_rdv, ctx.cnt["a1_rdv"] + cnt - 7)
                        ins = t.matmul(pb[cnt % 8], permB, src[:, c * 512:(c + 1) * 512],
                                       start=True, stop=True)
                        ins.then_inc(s_rmm, 1)
                        cnt += 1

            @b.vector
            def _(v):
                pb = [s1[:, 0:512], s1[:, 512:1024], s1[:, 1024:1536], s1[:, 1536:2048],
                      s2[:, 0:512], s2[:, 512:1024], s2[:, 1024:1536], scr["ps_o"][:, 0:512]]
                cnt = 0
                for sidx in range(4):
                    src = stg[sidx]
                    for c in range(4):
                        sl = slice(c * 512, (c + 1) * 512)
                        v.wait_ge(s_rmm, ctx.cnt["a1_rmm"] + cnt + 1)
                        v.tensor_tensor(out=src[:, sl], in0=src[:, sl],
                                        in1=scr["C"][:, sl], op=ALU.mult)
                        v.tensor_tensor(out=scr["rs"][:, 0:512], in0=pb[cnt % 8],
                                        in1=scr["Sg"][:, sl], op=ALU.mult)
                        v.tensor_tensor(out=src[:, sl], in0=src[:, sl],
                                        in1=scr["rs"][:, 0:512],
                                        op=ALU.add).then_inc(s_rdv, 1)
                        cnt += 1
                for sidx in range(4):
                    v.tensor_copy(out=u1[:, sidx * 2048:(sidx + 1) * 2048],
                                  in_=stg[sidx][:]).then_inc(s_qkc, 1)

        ctx.cnt["a1_rmm"] += 16
        ctx.cnt["a1_rdv"] += 16
        ctx.cnt["a1_qkc"] += 4

        prs = [(r, j) for r in range(NT) for j in range(G)]
        with nc.Block() as b:
            @b.tensor
            def _(t):
                for idx, (r, j) in enumerate(prs):
                    h = g_i * G + j
                    q = u1[:, j * 2048:(j + 1) * 2048]
                    k = u1[:, (2 + j) * 2048:(3 + j) * 2048]
                    vc = (r + 1) * 128
                    nch = (vc + 511) // 512
                    if idx >= 1:
                        t.wait_ge(s_pexp, ctx.cnt["a1_pexp"] + 2 * idx - 1)
                    for c in range(nch):
                        c0, c1 = c * 512, min((c + 1) * 512, vc)
                        last = (c == nch - 1)
                        t.matmul(s1[:, c0:c1], q[0:64, r * 128:(r + 1) * 128],
                                 k[0:64, c0:c1], start=True, stop=not last)
                        if last:
                            ins = t.matmul(s1[:, vc - 128:vc], I128, maskB,
                                           start=False, stop=True)
                    ins.then_inc(s_smm, 1)
                    if idx >= 1:
                        t.wait_ge(s_pexp, ctx.cnt["a1_pexp"] + 2 * idx)
                    if vc > 1536:
                        t.wait_ge(s_pexp, ctx.cnt["a1_pexp"] + 2 * idx + 1)
                    def s2sl(c0, c1):
                        if c0 >= 1536:
                            return s1[:, c0:c1]
                        return s2[:, c0:c1]
                    for c in range(nch):
                        c0, c1 = c * 512, min((c + 1) * 512, vc)
                        last = (c == nch - 1)
                        t.matmul(s2sl(c0, c1), q[64:128, r * 128:(r + 1) * 128],
                                 k[64:128, c0:c1], start=True, stop=not last)
                        if last:
                            ins = t.matmul(s2sl(vc - 128, vc), I128, maskB,
                                           start=False, stop=True)
                    ins.then_inc(s_smm, 1)
                    t.wait_ge(s_trD, ctx.cnt["a1_trD"] + (idx + 1) * 16)
                    t.wait_ge(s_oev, ctx.cnt["a1_oev"] + idx)
                    tb = scr["aT"] if idx % 2 == 0 else scr["aS"]
                    for c in range(r + 1):
                        ins = t.matmul(scr["ps_o"][:, 0:64],
                                       tb[:, c * 128:(c + 1) * 128],
                                       scr["v_tok"][:, c * 1024 + h * 64: c * 1024 + h * 64 + 64],
                                       start=(c == 0), stop=(c == r))
                    ins.then_inc(s_av, 1)

            @b.vector
            def _(v):
                def oev(pidx):
                    pr, pj = prs[pidx]
                    ph = g_i * G + pj
                    v.wait_ge(s_av, ctx.cnt["a1_av"] + pidx + 1)
                    osl = scr["o_tok"][:, pr * 1024 + ph * 64: pr * 1024 + ph * 64 + 64]
                    v.tensor_copy(out=osl, in_=scr["ps_o"][:, 0:64]).then_inc(s_oev, 1)
                    v.drain()
                    v.tensor_tensor(out=scr["rs"][:, 0:64], in0=osl, in1=osl,
                                    op=ALU.mult)
                    v.drain()
                    v.tensor_reduce(out=scr["ssqB"][:, pr * 16 + ph: pr * 16 + ph + 1],
                                    in_=scr["rs"][:, 0:64], axis=AX.X, op=ALU.add)
                    v.drain()

                for idx, (r, j) in enumerate(prs):
                    h = g_i * G + j
                    vc = (r + 1) * 128
                    v.wait_ge(s_smm, ctx.cnt["a1_smm"] + 2 * idx + 1)
                    v.tensor_reduce(out=st[:, 8:9], in_=s1[:, 0:vc], axis=AX.X,
                                    op=ALU.max)
                    v.drain()
                    v.tensor_scalar(out=st[:, 9:10], in0=st[:, 8:9], scalar1=-0.125,
                                    scalar2=None, op0=ALU.mult).then_inc(s_mx, 1)
                    if idx >= 1:
                        oev(idx - 1)
                    v.wait_ge(s_smm, ctx.cnt["a1_smm"] + 2 * idx + 2)
                    v.tensor_reduce(out=st[:, 10:11], in_=s2[:, 0:min(vc, 1536)],
                                    axis=AX.X, op=ALU.max)
                    v.drain()
                    if vc > 1536:
                        v.tensor_reduce(out=st[:, 20:21], in_=s1[:, 1536:vc],
                                        axis=AX.X, op=ALU.max)
                        v.drain()
                        v.tensor_tensor(out=st[:, 10:11], in0=st[:, 10:11],
                                        in1=st[:, 20:21], op=ALU.max)
                    v.tensor_scalar(out=st[:, 11:12], in0=st[:, 10:11], scalar1=-0.125,
                                    scalar2=None, op0=ALU.mult).then_inc(s_mx, 1)
                    v.wait_ge(s_pexp, ctx.cnt["a1_pexp"] + 2 * idx + 2)
                    if vc > 1536:
                        v.tensor_tensor(out=st[:, 13:14], in0=st[:, 13:14],
                                        in1=st[:, 21:22], op=ALU.add)
                        v.drain()
                    v.reciprocal(out=st[:, 14:15], in_=st[:, 12:13])
                    v.reciprocal(out=st[:, 15:16], in_=st[:, 13:14])
                    v.drain()
                    v.tensor_tensor(out=st[:, 16:17], in0=st[:, 15:16],
                                    in1=scr["lamB"][:, h:h + 1], op=ALU.mult)
                    v.drain()
                    cb = scr["e1"] if idx % 2 == 0 else scr["e2"]
                    v.tensor_scalar(out=scr["e2"][:, 0:vc],
                                    in0=scr["e2"][:, 0:vc], scalar1=st[:, 16:17],
                                    scalar2=None, op0=ALU.mult)
                    v.scalar_tensor_tensor(out=cb[:, 0:vc],
                                           in0=scr["e1"][:, 0:vc],
                                           scalar=st[:, 14:15],
                                           in1=scr["e2"][:, 0:vc],
                                           op0=ALU.mult,
                                           op1=ALU.subtract).then_inc(s_att, 1)
                oev(len(prs) - 1)

            @b.scalar
            def _(sc):
                for idx, (r, j) in enumerate(prs):
                    vc = (r + 1) * 128
                    sc.wait_ge(s_mx, ctx.cnt["a1_mx"] + 2 * idx + 1)
                    if idx >= 1 and (idx - 1) % 2 == 0:
                        sc.wait_ge(s_trD, ctx.cnt["a1_trD"] + idx * 16)
                    elif idx >= 2:
                        sc.wait_ge(s_trD, ctx.cnt["a1_trD"] + (idx - 1) * 16)
                    sc.activation(out=scr["e1"][:, 0:vc], in_=s1[:, 0:vc],
                                  func=AF.Exp, bias=st[:, 9:10], scale=0.125,
                                  accum_out=st[:, 12:13]).then_inc(s_pexp, 1)
                    sc.wait_ge(s_mx, ctx.cnt["a1_mx"] + 2 * idx + 2)
                    if idx >= 1 and (idx - 1) % 2 == 1:
                        sc.wait_ge(s_trD, ctx.cnt["a1_trD"] + idx * 16)
                    elif idx >= 3:
                        sc.wait_ge(s_trD, ctx.cnt["a1_trD"] + (idx - 1) * 16)
                    if vc <= 1536:
                        sc.activation(out=scr["e2"][:, 0:vc], in_=s2[:, 0:vc],
                                      func=AF.Exp, bias=st[:, 11:12], scale=0.125,
                                      accum_out=st[:, 13:14]).then_inc(s_pexp, 1)
                    else:
                        sc.activation(out=scr["e2"][:, 0:1536], in_=s2[:, 0:1536],
                                      func=AF.Exp, bias=st[:, 11:12], scale=0.125,
                                      accum_out=st[:, 13:14])
                        sc.activation(out=scr["e2"][:, 1536:vc], in_=s1[:, 1536:vc],
                                      func=AF.Exp, bias=st[:, 11:12], scale=0.125,
                                      accum_out=st[:, 21:22]).then_inc(s_pexp, 1)

            @b.sync
            def _(sy):
                for idx, (r, j) in enumerate(prs):
                    vc = (r + 1) * 128
                    sy.wait_ge(s_att, ctx.cnt["a1_att"] + idx + 1)
                    if idx >= 2:
                        sy.wait_ge(s_av, ctx.cnt["a1_av"] + idx - 1)
                    cb = scr["e1"] if idx % 2 == 0 else scr["e2"]
                    tb = scr["aT"] if idx % 2 == 0 else scr["aS"]
                    sy.dma_start(
                        out=tb[:].rearrange("p (c j) -> p c j", c=NT)[:, 0:r + 1, :],
                        in_=cb[:, 0:vc],
                        transpose=True).then_inc(s_trD, 16)

        ctx.cnt["a1_smm"] += 2 * G * NT
        ctx.cnt["a1_mx"] += 2 * G * NT
        ctx.cnt["a1_pexp"] += 2 * G * NT
        ctx.cnt["a1_att"] += G * NT
        ctx.cnt["a1_trD"] += G * NT * 16
        ctx.cnt["a1_av"] += G * NT
        ctx.cnt["a1_oev"] += G * NT

    with nc.Block() as b:
        @b.vector
        def _(v):
            v.tensor_scalar(out=scr["ssqB"][:], in0=scr["ssqB"][:], scalar1=1.0 / 64,
                            scalar2=1e-6, op0=ALU.mult, op1=ALU.add)

    with nc.Block() as b:
        @b.scalar
        def _(sc):
            sc.activation(out=scr["ssqB"][:], in_=scr["ssqB"][:], func=AF.Sqrt)

    with nc.Block() as b:
        @b.vector
        def _(v):
            v.reciprocal(out=scr["ssqB"][:], in_=scr["ssqB"][:])
            v.drain()
            for r in range(NT):
                for h in range(16):
                    osl = scr["o_tok"][:, r * 1024 + h * 64: r * 1024 + h * 64 + 64]
                    v.tensor_scalar(out=osl, in0=osl,
                                    scalar1=scr["ssqB"][:, r * 16 + h: r * 16 + h + 1],
                                    scalar2=None, op0=ALU.mult)

    s_otr = ctx.sem("a1_otr")
    oT = scr["oT"]
    with nc.Block() as b:
        @b.sync
        def _(sy):
            for tt in range(NT):
                sy.dma_start(
                    out=oT[:].rearrange("p (k s) -> p k s", k=ND)[:, :, tt * 128:(tt + 1) * 128],
                    in_=scr["o_tok"][:, tt * 1024:(tt + 1) * 1024],
                    transpose=True).then_inc(s_otr, 16)
            sy.wait_ge(s_otr, ctx.cnt["a1_otr"] + NT * 16)

        @b.gpsimd
        def _(g):
            g.dma_start(out=u1[:], in_=aps["wo"][:, :]).then_inc(s_wld, 16)
            g.wait_ge(s_wld, ctx.cnt["a1_wld"] + 16)
    ctx.cnt["a1_otr"] += NT * 16
    ctx.cnt["a1_wld"] += 16

    with nc.Block() as b:
        @b.vector
        def _(v):
            for kt in range(ND):
                v.tensor_scalar(out=u1[:, kt * 1024:(kt + 1) * 1024],
                                in0=u1[:, kt * 1024:(kt + 1) * 1024],
                                scalar1=scr["hnB"][:, kt:kt + 1],
                                scalar2=0.8, op0=ALU.mult, op1=ALU.mult)

    s_omm = ctx.sem("a1_omm")
    s_oac = ctx.sem("a1_oac")
    with nc.Block() as b:
        @b.tensor
        def _(t):
            pb = [s1[:, 0:512], s1[:, 512:1024], s2[:, 0:512], s2[:, 512:1024]]
            for tt in range(NT):
                for nn in range(2):
                    idx = tt * 2 + nn
                    if idx >= 4:
                        t.wait_ge(s_oac, ctx.cnt["a1_oac"] + idx - 3)
                    out = pb[idx % 4]
                    for kt in range(ND):
                        t.matmul(out, oT[:, kt * S + tt * 128: kt * S + (tt + 1) * 128],
                                 u1[:, kt * 1024 + nn * 512: kt * 1024 + (nn + 1) * 512],
                                 start=(kt == 0), stop=False)
                    ins = t.matmul(out, ones[0:1, 0:128],
                                   scr["boB"][0:1, nn * 512:(nn + 1) * 512],
                                   start=False, stop=True)
                    ins.then_inc(s_omm, 1)

        @b.vector
        def _(v):
            pb = [s1[:, 0:512], s1[:, 512:1024], s2[:, 0:512], s2[:, 512:1024]]
            for tt in range(NT):
                for nn in range(2):
                    idx = tt * 2 + nn
                    v.wait_ge(s_omm, ctx.cnt["a1_omm"] + idx + 1)
                    v.tensor_tensor(out=x_res[:, tt * D + nn * 512: tt * D + (nn + 1) * 512],
                                    in0=pb[idx % 4],
                                    in1=x_res[:, tt * D + nn * 512: tt * D + (nn + 1) * 512],
                                    op=ALU.add).then_inc(s_oac, 1)
    ctx.cnt["a1_omm"] += 32
    ctx.cnt["a1_oac"] += 32


def prep_l1attn(inp):
    wqT = inp["L1_attn_Wq"].T
    wkT = inp["L1_attn_Wk"].T
    wq = np.empty((8, 128, 2048), np.float32)
    wk = np.empty((8, 128, 2048), np.float32)
    for g in range(8):
        blk = wqT[:, g * 256:(g + 1) * 256]
        wq[g] = blk.reshape(8, 128, 256).transpose(1, 0, 2).reshape(128, 2048)
        blk = wkT[:, g * 256:(g + 1) * 256]
        wk[g] = blk.reshape(8, 128, 256).transpose(1, 0, 2).reshape(128, 2048)
    wv = inp["L1_attn_Wv"].T.reshape(8, 128, 1024).transpose(1, 0, 2).reshape(128, 8192)
    wo = inp["L1_attn_Wo"].T.reshape(8, 128, 1024).transpose(1, 0, 2).reshape(128, 8192)
    bq = np.ascontiguousarray(inp["L1_attn_bq"].reshape(16, 128).T)
    bk = np.ascontiguousarray(inp["L1_attn_bk"].reshape(16, 128).T)
    bv = inp["L1_attn_bv"].reshape(1, 1024)
    bo = inp["L1_attn_bo"].reshape(1, 1024)
    half = 32
    inv = 1.0 / 10000.0 ** (np.arange(half, dtype=np.float64) / half)
    ang = np.arange(S, dtype=np.float64)[:, None] * inv[None, :]
    cos = np.cos(ang).astype(np.float32).T
    sin = np.sin(ang).astype(np.float32).T
    C = np.empty((128, S), np.float32)
    Sg = np.empty((128, S), np.float32)
    for j in range(128):
        C[j] = cos[j % 32]
        Sg[j] = (-sin[j % 32]) if (j % 64) < 32 else sin[(j % 64) - 32]
    cst = np.zeros((128, 384), np.float32)
    cst[:, 0:128] = np.eye(128)
    m = np.zeros((128, 128), np.float32)
    iu = np.triu_indices(128, 1)
    m[iu] = -1e30
    cst[:, 128:256] = m
    pm = np.zeros((128, 128), np.float32)
    for i in range(128):
        j = i + 32 if (i % 64) < 32 else i - 32
        pm[i, j] = 1.0
    cst[:, 256:384] = pm
    ones = np.ones((1, 128), np.float32)
    lqk = np.concatenate([inp["L1_attn_lq1"], inp["L1_attn_lk1"],
                          inp["L1_attn_lq2"], inp["L1_attn_lk2"]], axis=1)
    hn = np.ascontiguousarray(inp["L1_attn_head_norm_w"].reshape(1024).reshape(8, 128).T)
    return {"wq": np.ascontiguousarray(wq.reshape(8 * 128, 2048)),
            "wk": np.ascontiguousarray(wk.reshape(8 * 128, 2048)),
            "wv": np.ascontiguousarray(wv), "wo": np.ascontiguousarray(wo),
            "bq": bq, "bk": bk, "bv": np.ascontiguousarray(bv),
            "bo": np.ascontiguousarray(bo),
            "C": C, "Sg": Sg, "cst": cst, "ones": ones,
            "lqk": np.ascontiguousarray(lqk), "hn": hn}



TC = 512
NTC = S // TC
H0, DK0 = 4, 256


def emit_ln_stream(ctx, xin, xhatT, scr, tag):
    nc = ctx.nc
    st = scr["st"]
    ld = ctx.sem("lns_ld_" + tag)
    hat = ctx.sem("lns_hat_" + tag)
    tr = ctx.sem("lns_tr_" + tag)
    sv = ctx.sem("lns_sv_" + tag)
    ss = ctx.sem("lns_ss_" + tag)
    xrot = scr["xrot"]
    b_ld = ctx.cnt["lns_ld_" + tag]
    b_hat = ctx.cnt["lns_hat_" + tag]
    b_tr = ctx.cnt["lns_tr_" + tag]
    b_sv = ctx.cnt["lns_sv_" + tag]
    b_ss = ctx.cnt["lns_ss_" + tag]

    with nc.Block() as b:
        @b.sync
        def _(sy):
            for t in range(NT):
                if t >= 2:
                    sy.wait_ge(sv, b_sv + t - 1)
                    sy.wait_ge(ss, b_ss + t - 1)
                sy.dma_start(out=xrot[:, (t % 2) * D:(t % 2 + 1) * D],
                             in_=xin[t * 128:(t + 1) * 128, :]).then_inc(ld, 16)

        @b.vector
        def _(v):
            for t in range(NT):
                v.wait_ge(ld, b_ld + (t + 1) * 16)
                v.tensor_reduce(out=st[:, t:t + 1], in_=xrot[:, (t % 2) * D:(t % 2 + 1) * D],
                                axis=AX.X, op=ALU.add)
                v.drain()
                v.nop().then_inc(sv, 1)

        @b.scalar
        def _(sc):
            for t in range(NT):
                sc.wait_ge(ld, b_ld + (t + 1) * 16)
                sc.activation(out=scr["sq"][:], in_=xrot[:, (t % 2) * D:(t % 2 + 1) * D],
                              func=AF.Square, accum_out=st[:, 16 + t:17 + t]).then_inc(ss, 1)
    ctx.cnt["lns_ld_" + tag] += NT * 16
    ctx.cnt["lns_sv_" + tag] += NT
    ctx.cnt["lns_ss_" + tag] += NT
    b_ld = ctx.cnt["lns_ld_" + tag]

    with nc.Block() as b:
        @b.vector
        def _(v):
            v.tensor_scalar(out=st[:, 32:48], in0=st[:, 0:16], scalar1=1.0 / D,
                            scalar2=None, op0=ALU.mult)
            v.tensor_scalar(out=st[:, 48:64], in0=st[:, 16:32], scalar1=1.0 / D,
                            scalar2=None, op0=ALU.mult)
            v.drain()
            v.tensor_tensor(out=st[:, 64:80], in0=st[:, 32:48], in1=st[:, 32:48], op=ALU.mult)
            v.drain()
            v.tensor_tensor(out=st[:, 80:96], in0=st[:, 48:64], in1=st[:, 64:80], op=ALU.subtract)
            v.drain()
            v.tensor_scalar(out=st[:, 80:96], in0=st[:, 80:96], scalar1=1e-5,
                            scalar2=None, op0=ALU.add)
            v.tensor_scalar(out=st[:, 0:16], in0=st[:, 32:48], scalar1=-1.0,
                            scalar2=None, op0=ALU.mult)

    with nc.Block() as b:
        @b.scalar
        def _(sc):
            sc.activation(out=st[:, 48:64], in_=st[:, 80:96], func=AF.Sqrt)

    with nc.Block() as b:
        @b.sync
        def _(sy):
            sy.dma_start(out=xrot[:, 0:D], in_=xin[0:128, :]).then_inc(ld, 16)
            sy.dma_start(out=xrot[:, D:2 * D], in_=xin[128:256, :]).then_inc(ld, 16)
            for t in range(NT):
                sy.wait_ge(hat, b_hat + t + 1)
                sy.dma_start(
                    out=xhatT[:].rearrange("p (k s) -> p k s", k=ND)[:, :, t * 128:(t + 1) * 128],
                    in_=scr["xhat2"][:, (t % 2) * D:(t % 2 + 1) * D],
                    transpose=True).then_inc(tr, 16)
                if t + 2 < NT:
                    sy.dma_start(out=xrot[:, ((t + 2) % 2) * D:((t + 2) % 2 + 1) * D],
                                 in_=xin[(t + 2) * 128:(t + 3) * 128, :]).then_inc(ld, 16)
            sy.wait_ge(tr, b_tr + NT * 16)

        @b.vector
        def _(v):
            v.reciprocal(out=st[:, 16:32], in_=st[:, 48:64])
            v.drain()
            for t in range(NT):
                v.wait_ge(ld, b_ld + (t + 1) * 16)
                if t >= 2:
                    v.wait_ge(tr, b_tr + (t - 1) * 16)
                v.tensor_scalar(out=scr["xhat2"][:, (t % 2) * D:(t % 2 + 1) * D],
                                in0=xrot[:, (t % 2) * D:(t % 2 + 1) * D],
                                scalar1=st[:, t:t + 1], scalar2=st[:, 16 + t:17 + t],
                                op0=ALU.add, op1=ALU.mult).then_inc(hat, 1)
    ctx.cnt["lns_ld_" + tag] += NT * 16
    ctx.cnt["lns_hat_" + tag] += NT
    ctx.cnt["lns_tr_" + tag] += NT * 16


def emit_l0attn(ctx, xin, xout, xhatT, aps, scr):
    nc = ctx.nc
    st = scr["st"]
    ones = scr["ones"]
    s_ld = ctx.sem("a0_ld")
    s_d1 = ctx.sem("a0_d1")
    ps_d = scr["ps_d"]

    with nc.Block() as b:
        @b.gpsimd
        def _(g):
            for key, dst in [("cst0", scr["cst0"][:]), ("ones", scr["ones"][:]),
                             ("qcw", scr["qcw"][:]), ("kcw", scr["kcw"][:]),
                             ("vcw", scr["vcw"][:]), ("firw", scr["firw"][:]),
                             ("w2", scr["w2B"][:]), ("b1", scr["b1B"][:]),
                             ("b2", scr["b2B"][:]), ("bpw", scr["bpW"][:]),
                             ("idb", scr["idbB"][:]), ("ltau", scr["ltau"][:]),
                             ("onw", scr["onwB"][:])]:
                g.dma_start(out=dst, in_=aps[key][:, :]).then_inc(s_ld, 16)
            g.memset(scr["S32"][:], 0)
            g.memset(scr["S_bf"][:], 0)
            g.memset(scr["histq"][:], 0)
            g.memset(scr["histk"][:], 0)
            g.memset(scr["histv"][:], 0)
            g.memset(scr["vc"][:], 0)
            g.wait_ge(s_ld, ctx.cnt["a0_ld"] + 13 * 16)
    ctx.cnt["a0_ld"] += 13 * 16

    with nc.Block() as b:
        @b.scalar
        def _(sc):
            sc.activation(out=st[0:1, 0:4], in_=scr["ltau"][0:1, 0:4], func=AF.Exp)

    with nc.Block() as b:
        @b.vector
        def _(v):
            v.tensor_scalar(out=st[0:1, 0:4], in0=st[0:1, 0:4], scalar1=0.5,
                            scalar2=None, op0=ALU.max)
            v.drain()
            v.reciprocal(out=st[0:1, 4:8], in_=st[0:1, 0:4])
            v.drain()
            v.tensor_copy(out=scr["rowb"][0:1, 0:4], in_=st[0:1, 4:8])
            v.drain()

    with nc.Block() as b:
        @b.tensor
        def _(t):
            t.matmul(ps_d[:, 0:4], ones[0:1, 0:128], scr["rowb"][0:1, 0:4],
                     start=True, stop=True).then_inc(s_d1, 1)

        @b.vector
        def _(v):
            v.wait_ge(s_d1, ctx.cnt["a0_d1"] + 1)
            v.tensor_copy(out=scr["invtauB"][:], in_=ps_d[:, 0:4])
    ctx.cnt["a0_d1"] += 1

    for tcid in range(NTC):
        emit_l0attn_tc(ctx, xin, xout, xhatT, aps, scr, tcid)


def emit_l0attn_tc(ctx, xin, xout, xhatT, aps, scr, tcid):
    nc = ctx.nc
    st = scr["st"]
    c0 = tcid * TC
    I128 = scr["cst0"][:, 0:128]
    maskLn = scr["cst0"][:, 128:256]
    maskUn = scr["cst0"][:, 256:384]
    maskUD = scr["cst0"][:, 384:512]
    ones = scr["ones"]
    pb0, pb1 = scr["ps_b0"], scr["ps_b1"]
    ps_a, ps_d, ps_f, ps_su = scr["ps_a"], scr["ps_d"], scr["ps_f"], scr["ps_su"]
    wbuf = scr["wbuf"]
    s_w = ctx.sem("a0_w"); s_pm = ctx.sem("a0_pm"); s_pe = ctx.sem("a0_pe")
    s_cv = ctx.sem("a0_cv"); s_tr = ctx.sem("a0_tr")
    s_d1 = ctx.sem("a0_d1"); s_d2 = ctx.sem("a0_d2"); s_d3 = ctx.sem("a0_d3")
    s_r1 = ctx.sem("a0_r1"); s_r2 = ctx.sem("a0_r2"); s_r3 = ctx.sem("a0_r3")
    s_ox = ctx.sem("a0_ox")

    tok_bufs = [scr["q_tok"], scr["k_tok"], scr["v_tok0"]]
    for ti in range(3):
        wkey = ["wqa", "wka", "wva"][ti]
        cw = [scr["qcw"], scr["kcw"], scr["vcw"]][ti]
        hist = [scr["histq"], scr["histk"], scr["histv"]][ti]
        raw = scr["rawfull"]
        conv = scr["convfull"]
        with nc.Block() as b:
            @b.gpsimd
            def _(g):
                g.dma_start(out=wbuf[:, 0:4096],
                            in_=aps[wkey][:, 0:4096]).then_inc(s_w, 16)
                g.dma_start(out=wbuf[:, 4096:8192],
                            in_=aps[wkey][:, 4096:8192]).then_inc(s_w, 16)

            @b.tensor
            def _(t):
                for m in range(8):
                    t.wait_ge(s_w, ctx.cnt["a0_w"] + (16 if m < 4 else 32))
                    if m >= 2:
                        t.wait_ge(s_pe, ctx.cnt["a0_pe"] + m - 1)
                    ps = pb0 if m % 2 == 0 else pb1
                    for kt in range(ND):
                        ins = t.matmul(ps[:, 0:512],
                                       wbuf[:, m * 1024 + kt * 128: m * 1024 + (kt + 1) * 128],
                                       xhatT[:, kt * S + c0: kt * S + c0 + TC],
                                       start=(kt == 0), stop=(kt == ND - 1))
                    ins.then_inc(s_pm, 1)

            @b.scalar
            def _(sc):
                for m in range(8):
                    sc.wait_ge(s_pm, ctx.cnt["a0_pm"] + m + 1)
                    sc.activation(out=raw[:, m * 544 + 32: m * 544 + 544],
                                  in_=(pb0 if m % 2 == 0 else pb1)[:, 0:512],
                                  func=AF.Identity).then_inc(s_pe, 1)

            @b.vector
            def _(v):
                for m in range(8):
                    v.tensor_copy(out=raw[:, m * 544: m * 544 + 32],
                                  in_=hist[:, m * 32:(m + 1) * 32])
                if ti == 2:
                    for m in range(8):
                        v.tensor_copy(out=scr["rs0"][:, 0:32],
                                      in_=scr["vc"][:, m * 544 + 512: m * 544 + 544])
                        v.drain()
                        v.tensor_copy(out=scr["vc"][:, m * 544: m * 544 + 32],
                                      in_=scr["rs0"][:, 0:32])
                        v.drain()
                v.drain()
                for m in range(8):
                    v.wait_ge(s_pe, ctx.cnt["a0_pe"] + m + 1)
                    v.tensor_copy(out=hist[:, m * 32:(m + 1) * 32],
                                  in_=raw[:, m * 544 + 512: m * 544 + 544])
                    v.drain()
                    dst = conv[:, m * 512:(m + 1) * 512]
                    for tau in range(4):
                        src = raw[:, m * 544 + 32 - 3 + tau: m * 544 + 32 - 3 + tau + 512]
                        if tau == 0:
                            v.tensor_scalar(out=dst, in0=src,
                                            scalar1=cw[:, m * 4:m * 4 + 1],
                                            scalar2=None, op0=ALU.mult)
                        else:
                            ins = v.scalar_tensor_tensor(out=dst, in0=src,
                                                   scalar=cw[:, m * 4 + tau:m * 4 + tau + 1],
                                                   in1=dst, op0=ALU.mult, op1=ALU.add)
                    ins.then_inc(s_cv, 1)
        ctx.cnt["a0_w"] += 32
        ctx.cnt["a0_pm"] += 8
        ctx.cnt["a0_pe"] += 8
        ctx.cnt["a0_cv"] += 8

        with nc.Block() as b:
            @b.scalar
            def _(sc):
                for m in range(8):
                    if ti < 2:
                        sc.activation(out=conv[:, m * 512:(m + 1) * 512],
                                      in_=conv[:, m * 512:(m + 1) * 512],
                                      func=AF.Silu).then_inc(s_cv, 1)
                    else:
                        sc.activation(out=scr["vc"][:, m * 544 + 32: m * 544 + 544],
                                      in_=conv[:, m * 512:(m + 1) * 512],
                                      func=AF.Silu).then_inc(s_cv, 1)

        ctx.cnt["a0_cv"] += 8
        with nc.Block() as b:
            @b.sync
            def _(sy):
                for m in range(8):
                    src = (conv[:, m * 512:(m + 1) * 512] if ti < 2
                           else scr["vc"][:, m * 544 + 32: m * 544 + 544])
                    sy.dma_start(
                        out=tok_bufs[ti][:].rearrange("p (t c) -> p t c", t=4)[:, :, m * 128:(m + 1) * 128],
                        in_=src, transpose=True).then_inc(s_tr, 16)
                sy.wait_ge(s_tr, ctx.cnt["a0_tr"] + 8 * 16)
        ctx.cnt["a0_tr"] += 8 * 16

    with nc.Block() as b:
        @b.tensor
        def _(t):
            for tt in range(4):
                for kt in range(ND):
                    t.matmul(ps_d[:, tt * 8:(tt + 1) * 8],
                             xhatT[:, kt * S + c0 + tt * 128: kt * S + c0 + (tt + 1) * 128],
                             scr["bpW"][:, kt * 8:(kt + 1) * 8],
                             start=(kt == 0), stop=False)
                ins = t.matmul(ps_d[:, tt * 8:(tt + 1) * 8], ones[0:1, 0:128],
                               scr["idbB"][0:1, 0:8], start=False, stop=True)
            ins.then_inc(s_pm, 1)

        @b.scalar
        def _(sc):
            sc.wait_ge(s_pm, ctx.cnt["a0_pm"] + 1)
            sc.activation(out=scr["bp"][:, 0:32], in_=ps_d[:, 0:32], func=AF.Sigmoid)
    ctx.cnt["a0_pm"] += 1

    with nc.Block() as b:
        @b.vector
        def _(v):
            for tt in range(4):
                for qk in range(2):
                    src = tok_bufs[qk][:, tt * 1024:(tt + 1) * 1024]
                    v.tensor_tensor(out=scr["scrot"][:, 0:1024], in0=src, in1=src,
                                    op=ALU.mult)
                    dstc = scr["ecol"] if qk == 0 else scr["dcol"]
                    v.tensor_reduce(out=dstc[:, tt * 4:(tt + 1) * 4],
                                    in_=scr["scrot"][:, 0:1024].rearrange("p (h c) -> p h c", h=4),
                                    axis=AX.X, op=ALU.add)
                    v.drain()
            v.tensor_scalar(out=scr["ecol"][:], in0=scr["ecol"][:], scalar1=1e-6,
                            scalar2=None, op0=ALU.add)
            v.tensor_scalar(out=scr["dcol"][:], in0=scr["dcol"][:], scalar1=1e-6,
                            scalar2=None, op0=ALU.add)

    with nc.Block() as b:
        @b.scalar
        def _(sc):
            sc.activation(out=scr["ecol"][:], in_=scr["ecol"][:], func=AF.Sqrt)
            sc.activation(out=scr["dcol"][:], in_=scr["dcol"][:], func=AF.Sqrt)

    with nc.Block() as b:
        @b.vector
        def _(v):
            v.reciprocal(out=scr["ecol"][:], in_=scr["ecol"][:])
            v.reciprocal(out=scr["dcol"][:], in_=scr["dcol"][:])
            v.drain()
            for tt in range(4):
                for h in range(4):
                    sl = slice(tt * 1024 + h * 256, tt * 1024 + (h + 1) * 256)
                    v.tensor_scalar(out=scr["scrot"][:, h * 256:(h + 1) * 256],
                                    in0=scr["q_tok"][:, sl.start:sl.stop],
                                    scalar1=scr["ecol"][:, tt * 4 + h:tt * 4 + h + 1],
                                    scalar2=None, op0=ALU.mult)
                    v.tensor_scalar(out=scr["kn_tok"][:, sl.start:sl.stop],
                                    in0=scr["k_tok"][:, sl.start:sl.stop],
                                    scalar1=scr["dcol"][:, tt * 4 + h:tt * 4 + h + 1],
                                    scalar2=None, op0=ALU.mult)
                for h in range(4):
                    ins = v.tensor_scalar(out=scr["scrot"][:, 1024 + h * 256: 1024 + (h + 1) * 256],
                                    in0=scr["kn_tok"][:, tt * 1024 + h * 256: tt * 1024 + (h + 1) * 256],
                                    scalar1=scr["bp"][:, tt * 8 + h: tt * 8 + h + 1],
                                    scalar2=None, op0=ALU.mult)
                ins.then_inc(s_cv, 1)

        @b.sync
        def _(sy):
            for tt in range(4):
                sy.wait_ge(s_cv, ctx.cnt["a0_cv"] + tt + 1)
                sy.dma_start(
                    out=scr["qnT"][:].rearrange("p (f c) -> p f c", f=8)[:, :, tt * 128:(tt + 1) * 128],
                    in_=scr["scrot"][:, 0:1024], transpose=True).then_inc(s_tr, 16)
                sy.dma_start(
                    out=scr["knT"][:].rearrange("p (f c) -> p f c", f=8)[:, :, tt * 128:(tt + 1) * 128],
                    in_=scr["kn_tok"][:, tt * 1024:(tt + 1) * 1024],
                    transpose=True).then_inc(s_tr, 16)
                sy.dma_start(
                    out=scr["kbnT"][:].rearrange("p (f c) -> p f c", f=8)[:, :, tt * 128:(tt + 1) * 128],
                    in_=scr["scrot"][:, 1024:2048], transpose=True).then_inc(s_tr, 16)
            sy.wait_ge(s_tr, ctx.cnt["a0_tr"] + 12 * 16)
    ctx.cnt["a0_cv"] += 4
    ctx.cnt["a0_tr"] += 12 * 16

    KTAPS = [(0, 3, 0), (1, 7, 3), (2, 15, 10), (3, 31, 25)]
    fir_ops = []
    for (p, K, po) in KTAPS:
        for m in range(8):
            for tau in range(K):
                fir_ops.append((p, m, K, po, tau))

    def emit_fir_op(v, op):
        p, m, K, po, tau = op
        dst = scr["firb"][p][:, m * 512:(m + 1) * 512]
        src = scr["vc"][:, m * 544 + 32 - (K - 1) + tau: m * 544 + 32 - (K - 1) + tau + 512]
        wcol = scr["firw"][:, m * 56 + po + tau: m * 56 + po + tau + 1]
        if tau == 0:
            v.tensor_scalar(out=dst, in0=src, scalar1=wcol,
                            scalar2=None, op0=ALU.mult)
        else:
            v.scalar_tensor_tensor(out=dst, in0=src, scalar=wcol,
                                   in1=dst, op0=ALU.mult, op1=ALU.add)

    cum1 = ctx.cnt["a0_d1"]
    cum2 = ctx.cnt["a0_d2"]
    cum3 = ctx.cnt["a0_d3"]
    cumT = ctx.cnt["a0_tr"]
    if not hasattr(ctx, "a0_smark"):
        ctx.a0_smark = {h: 0 for h in range(4)}
        ctx.a0_y0mark = {0: 0, 1: 0}
        ctx.a0_wTmark = {0: 0, 1: 0}
        ctx.a0_amark = {0: 0, 1: 0}
        ctx.a0_fmark = {0: 0, 1: 0}
    qt = scr["q_tok"]
    SETS = [
        {"Pp": scr["Pp"][:], "tA": scr["tA"][:], "tB": scr["tB"][:],
         "y0": scr["y0"][:], "attT": scr["attT"][:], "u_sb": scr["u_sb"][:],
         "w_sb": scr["w_sb"][:], "wnegT": scr["wnegT"][:], "uadj": scr["uadj"][:],
         "uadj2": scr["uadj2"][:],
         "psA": ps_a, "psF": ps_f, "psD": ps_d},
        {"Pp": qt[:, 0:512], "tA": qt[:, 512:1024], "tB": qt[:, 1024:1536],
         "y0": qt[:, 1536:2048], "attT": qt[:, 2048:2176], "u_sb": qt[:, 2176:2432],
         "w_sb": qt[:, 2432:2688], "wnegT": qt[:, 2688:2944], "uadj": qt[:, 2944:3200],
         "uadj2": qt[:, 3200:3456],
         "psA": ps_su, "psF": pb0, "psD": pb1},
    ]
    pairs = [(i, h) for i in range(4) for h in range(4)]
    sched = []
    for pi, (i, h) in enumerate(pairs):
        st_ = pi % 2
        sc_ = {"i": i, "h": h, "set": st_,
               "b1": cum1, "b2": cum2, "b3": cum3, "bT": cumT,
               "smark": ctx.a0_smark[h], "y0mark": ctx.a0_y0mark[st_],
               "wTmark": ctx.a0_wTmark[st_], "amark": ctx.a0_amark[st_],
               "fmark": ctx.a0_fmark[st_]}
        sched.append(sc_)
        ctx.a0_y0mark[st_] = cum1 + 2
        ctx.a0_wTmark[st_] = cum1 + 15
        ctx.a0_amark[st_] = cum2 + 10
        ctx.a0_fmark[st_] = cum3 + 9
        ctx.a0_smark[h] = cum2 + 10
        cum1 += 17
        cum2 += 10
        cum3 += 9
        cumT += 16
    ctx.cnt["a0_d1"] = cum1
    ctx.cnt["a0_d2"] = cum2
    ctx.cnt["a0_d3"] = cum3
    ctx.cnt["a0_tr"] = cumT

    s_d1h, s_d2h, s_d3h, s_trh = s_d1, s_d2, s_d3, s_tr
    delta_tok = scr["convfull"]
    with nc.Block() as b:
        @b.tensor
        def _(t):
            for sc_ in sched:
                i, h = sc_["i"], sc_["h"]
                b1, b2, b3, bT = sc_["b1"], sc_["b2"], sc_["b3"], sc_["bT"]
                B = SETS[sc_["set"]]
                psA, psF, psD = B["psA"], B["psF"], B["psD"]
                f0, f1 = 2 * h, 2 * h + 1

                def fmT(buf, f):
                    return buf[:, f * 512 + i * 128: f * 512 + (i + 1) * 128]
                t.wait_ge(s_d2h, sc_["amark"])
                t.matmul(psA[:, 0:128], fmT(scr["kbnT"], f0), fmT(scr["knT"], f0),
                         start=True, stop=False)
                t.matmul(psA[:, 0:128], fmT(scr["kbnT"], f1), fmT(scr["knT"], f1),
                         start=False, stop=True)
                t.matmul(psA[:, 128:256], fmT(scr["knT"], f0), fmT(scr["kbnT"], f0),
                         start=True, stop=False)
                t.matmul(psA[:, 128:256], fmT(scr["knT"], f1), fmT(scr["kbnT"], f1),
                         start=False, stop=True)
                t.matmul(psA[:, 256:384], fmT(scr["knT"], f0), fmT(scr["qnT"], f0),
                         start=True, stop=False)
                ins = t.matmul(psA[:, 256:384], fmT(scr["knT"], f1), fmT(scr["qnT"], f1),
                               start=False, stop=True)
                ins.then_inc(s_d1h, 1)
                Pslot = [(B["Pp"][:, 0:128], B["Pp"][:, 128:256]),
                         (B["Pp"][:, 256:384], B["Pp"][:, 384:512])]
                tslot = [B["tA"], B["tB"]]
                for j in range(7):
                    rhs = B["y0"][:] if j == 0 else tslot[(j - 1) % 2][:]
                    PT = Pslot[j % 2][1]
                    if j == 0:
                        t.wait_ge(s_d2h, b2 + 2)
                        t.wait_ge(s_d3h, sc_["fmark"])
                    else:
                        t.wait_ge(s_d3h, b3 + j)
                        t.wait_ge(s_d2h, b2 + 2 + j)
                    t.matmul(psF[:, 0:512], PT, rhs, start=True, stop=False)
                    ins = t.matmul(psF[:, 0:512], I128, rhs, start=False, stop=True)
                    ins.then_inc(s_d1h, 1)
                    if j < 6:
                        jd = j + 1
                        P_prev, PT_prev = Pslot[j % 2]
                        dsl = psD[:, 0:256]
                        t.matmul(dsl[:, 0:128], PT_prev, P_prev, start=True, stop=True)
                        ins = t.matmul(dsl[:, 128:256], P_prev, PT_prev, start=True, stop=True)
                        ins.then_inc(s_d1h, 1)
                t.wait_ge(s_trh, bT + 16)
                t.wait_ge(s_d2h, sc_["smark"])
                t.wait_ge(s_d3h, b3 + 7)
                t.matmul(psF[:, 0:256], B["wnegT"][:, 0:128],
                         scr["S_bf"][:, h * 512:h * 512 + 256], start=True, stop=False)
                t.matmul(psF[:, 0:256], B["wnegT"][:, 128:256],
                         scr["S_bf"][:, h * 512 + 256:h * 512 + 512], start=False, stop=False)
                ins = t.matmul(psF[:, 0:256], I128, B["u_sb"][:],
                               start=False, stop=True)
                ins.then_inc(s_d1h, 1)
                t.wait_ge(s_d3h, b3 + 8)
                t.matmul(psF[:, 256:512], fmT(scr["qnT"], f0),
                         scr["S_bf"][:, h * 512:h * 512 + 256], start=True, stop=False)
                t.matmul(psF[:, 256:512], fmT(scr["qnT"], f1),
                         scr["S_bf"][:, h * 512 + 256:h * 512 + 512], start=False, stop=False)
                ins = t.matmul(psF[:, 256:512], B["attT"][:], B["uadj"][:],
                               start=False, stop=True)
                ins.then_inc(s_d1h, 1)
                t.wait_ge(s_d2h, b2 + 9)
                ksl = scr["kn_tok"][:, i * 1024 + h * 256: i * 1024 + (h + 1) * 256]
                t.matmul(psA[:, 0:256], ksl[:, 0:128], B["uadj2"][:],
                         start=True, stop=True)
                ins = t.matmul(psA[:, 256:512], ksl[:, 128:256], B["uadj2"][:],
                               start=True, stop=True)
                ins.then_inc(s_d1h, 1)

        @b.vector
        def _(v):
            fir_iter = iter(fir_ops)
            per_pair = (len(fir_ops) + len(sched) - 1) // len(sched)
            for pi, sc_ in enumerate(sched):
                for _ in range(per_pair):
                    op = next(fir_iter, None)
                    if op is not None:
                        emit_fir_op(v, op)
                i, h = sc_["i"], sc_["h"]
                b1, b2, b3 = sc_["b1"], sc_["b2"], sc_["b3"]
                B = SETS[sc_["set"]]
                psA, psF, psD = B["psA"], B["psF"], B["psD"]
                v.wait_ge(s_d1h, sc_["y0mark"])
                vsl = scr["v_tok0"][:, i * 1024 + h * 256: i * 1024 + (h + 1) * 256]
                ksl = scr["kn_tok"][:, i * 1024 + h * 256: i * 1024 + (h + 1) * 256]
                bcol = scr["bp"][:, i * 8 + h: i * 8 + h + 1]
                v.tensor_scalar(out=B["y0"][:, 0:256], in0=vsl, scalar1=bcol,
                                scalar2=None, op0=ALU.mult)
                v.tensor_scalar(out=B["y0"][:, 256:512], in0=ksl, scalar1=bcol,
                                scalar2=None, op0=ALU.mult).then_inc(s_d2h, 1)
                v.wait_ge(s_d1h, b1 + 1)
                v.tensor_tensor(out=B["Pp"][:, 0:128], in0=psA[:, 0:128],
                                in1=maskLn, op=ALU.mult)
                v.tensor_tensor(out=B["Pp"][:, 128:256], in0=psA[:, 128:256],
                                in1=maskUn, op=ALU.mult)
                v.tensor_tensor(out=B["attT"][:], in0=psA[:, 256:384],
                                in1=maskUD, op=ALU.mult).then_inc(s_d2h, 1)
                for jd in range(1, 7):
                    v.wait_ge(s_d1h, b1 + 2 * jd + 1)
                    dsl = psD[:, 0:256]
                    slot = jd % 2
                    v.tensor_copy(out=B["Pp"][:, slot * 256: slot * 256 + 128],
                                  in_=dsl[:, 0:128])
                    v.tensor_copy(out=B["Pp"][:, slot * 256 + 128: slot * 256 + 256],
                                  in_=dsl[:, 128:256]).then_inc(s_d2h, 1)
                v.wait_ge(s_d3h, b3 + 8)
                v.tensor_scalar(out=B["uadj2"][:], in0=B["uadj"][:],
                                scalar1=scr["dcol"][:, i * 4 + h: i * 4 + h + 1],
                                scalar2=None, op0=ALU.mult).then_inc(s_d2h, 1)
                v.wait_ge(s_d1h, b1 + 17)
                v.tensor_tensor(out=scr["S32"][:, h * 512:(h + 1) * 512],
                                in0=psA[:, 0:512],
                                in1=scr["S32"][:, h * 512:(h + 1) * 512],
                                op=ALU.add)
                v.drain()
                v.tensor_copy(out=scr["S_bf"][:, h * 512:(h + 1) * 512],
                              in_=scr["S32"][:, h * 512:(h + 1) * 512]).then_inc(s_d2h, 1)

        @b.scalar
        def _(sc):
            for sc_ in sched:
                i, h = sc_["i"], sc_["h"]
                b1, b3 = sc_["b1"], sc_["b3"]
                B = SETS[sc_["set"]]
                psF = B["psF"]
                tslot = [B["tA"], B["tB"]]
                for j in range(7):
                    sc.wait_ge(s_d1h, b1 + (2 if j == 0 else 2 * j + 2))
                    if j < 6:
                        sc.activation(out=tslot[j % 2][:], in_=psF[:, 0:512],
                                      func=AF.Identity).then_inc(s_d3h, 1)
                    else:
                        sc.activation(out=B["u_sb"][:], in_=psF[:, 0:256],
                                      func=AF.Identity)
                        sc.activation(out=B["w_sb"][:], in_=psF[:, 256:512],
                                      func=AF.Identity, scale=-1.0).then_inc(s_d3h, 1)
                sc.wait_ge(s_d1h, b1 + 15)
                sc.activation(out=B["uadj"][:], in_=psF[:, 0:256],
                              func=AF.Identity).then_inc(s_d3h, 1)
                sc.wait_ge(s_d1h, b1 + 16)
                sc.activation(out=delta_tok[:, i * 1024 + h * 256: i * 1024 + (h + 1) * 256],
                              in_=psF[:, 256:512], func=AF.Identity).then_inc(s_d3h, 1)

        @b.sync
        def _(sy):
            for sc_ in sched:
                B = SETS[sc_["set"]]
                sy.wait_ge(s_d3h, sc_["b3"] + 7)
                if sc_["wTmark"] > 0:
                    sy.wait_ge(s_d1h, sc_["wTmark"])
                sy.dma_start(out=B["wnegT"][:].rearrange("p (f c) -> p f c", f=2),
                             in_=B["w_sb"][:], transpose=True).then_inc(s_trh, 16)

    rin = []
    for kt in range(8):
        rin.append(xhatT[:, kt * S + c0: kt * S + c0 + TC])
    for p in range(4):
        for kt in range(8):
            rin.append(scr["firb"][p][:, kt * 512:(kt + 1) * 512])
    deltaT = scr["rawfull"]
    for kt in range(8):
        rin.append(deltaT[:, kt * 512:(kt + 1) * 512])

    with nc.Block() as b:
        @b.sync
        def _(sy):
            for tt in range(4):
                sy.dma_start(
                    out=deltaT[:].rearrange("p (f c) -> p f c", f=8)[:, :, tt * 128:(tt + 1) * 128],
                    in_=delta_tok[:, tt * 1024:(tt + 1) * 1024],
                    transpose=True).then_inc(s_tr, 16)
            sy.wait_ge(s_tr, ctx.cnt["a0_tr"] + 4 * 16)
    ctx.cnt["a0_tr"] += 4 * 16

    b_w = ctx.cnt["a0_w"]
    b_r1 = ctx.cnt["a0_r1"]
    b_r2 = ctx.cnt["a0_r2"]
    b_r3 = ctx.cnt["a0_r3"]
    with nc.Block() as b:
        @b.gpsimd
        def _(g):
            for m in range(16):
                for hf in range(2):
                    idx = m * 2 + hf
                    if idx >= 2:
                        g.wait_ge(s_r1, b_r1 + idx - 1)
                    g.dma_start(out=wbuf[:, hf * 3072:(hf + 1) * 3072],
                                in_=aps["w1"][(idx) * 128:(idx + 1) * 128, :]
                                ).then_inc(s_w, 16)

        @b.tensor
        def _(t):
            for m in range(16):
                ps = pb0 if m % 2 == 0 else pb1
                if m >= 2:
                    t.wait_ge(s_r2, b_r2 + m - 1)
                for hf in range(2):
                    idx = m * 2 + hf
                    t.wait_ge(s_w, b_w + (idx + 1) * 16)
                    for ktl in range(24):
                        ins = t.matmul(ps[:, 0:512],
                                       wbuf[:, hf * 3072 + ktl * 128: hf * 3072 + (ktl + 1) * 128],
                                       rin[hf * 24 + ktl],
                                       start=(hf == 0 and ktl == 0),
                                       stop=(hf == 1 and ktl == 23))
                    ins.then_inc(s_r1, 1)
                if m >= 1:
                    mm = m - 1
                    t.wait_ge(s_r2, b_r2 + mm + 1)
                    lgb = [ps_d, ps_f, ps_su, ps_a]
                    for tsub in range(4):
                        ins = t.matmul(lgb[tsub][:, 0:20],
                                       scr["hmid"][:, (mm % 2) * 512 + tsub * 128: (mm % 2) * 512 + (tsub + 1) * 128],
                                       scr["w2B"][:, mm * 20:(mm + 1) * 20],
                                       start=(mm == 0), stop=False)
                    ins.then_inc(s_r3, 1)
            t.wait_ge(s_r2, b_r2 + 16)
            lgb = [ps_d, ps_f, ps_su, ps_a]
            for tsub in range(4):
                t.matmul(lgb[tsub][:, 0:20],
                         scr["hmid"][:, (15 % 2) * 512 + tsub * 128: (15 % 2) * 512 + (tsub + 1) * 128],
                         scr["w2B"][:, 15 * 20:16 * 20],
                         start=False, stop=False)
                ins = t.matmul(lgb[tsub][:, 0:20], ones[0:1, 0:128],
                               scr["b2B"][0:1, 0:20], start=False, stop=True)
            ins.then_inc(s_r3, 1)

        @b.scalar
        def _(sc):
            for m in range(16):
                sc.wait_ge(s_r1, b_r1 + 2 * m + 2)
                if m >= 2:
                    sc.wait_ge(s_r3, b_r3 + m - 1)
                ps = pb0 if m % 2 == 0 else pb1
                sc.activation(out=scr["hmid"][:, (m % 2) * 512:(m % 2 + 1) * 512],
                              in_=ps[:, 0:512], func=AF.Gelu,
                              bias=scr["b1B"][:, m:m + 1]).then_inc(s_r2, 1)
    ctx.cnt["a0_w"] += 32 * 16
    ctx.cnt["a0_r1"] += 32
    ctx.cnt["a0_r2"] += 16
    ctx.cnt["a0_r3"] += 16

    lgb = [ps_d, ps_f, ps_su, ps_a]
    with nc.Block() as b:
        @b.vector
        def _(v):
            for tsub in range(4):
                v.tensor_copy(out=scr["probs"][:, tsub * 20:(tsub + 1) * 20],
                              in_=lgb[tsub][:, 0:20])
            v.drain()
            for tsub in range(4):
                for h in range(4):
                    sl = scr["probs"][:, tsub * 20 + h * 5: tsub * 20 + (h + 1) * 5]
                    v.tensor_scalar(out=sl, in0=sl, scalar1=scr["invtauB"][:, h:h + 1],
                                    scalar2=None, op0=ALU.mult)
            v.drain()
            for tsub in range(4):
                v.tensor_reduce(out=st[:, tsub * 4:(tsub + 1) * 4],
                                in_=scr["probs"][:, tsub * 20:(tsub + 1) * 20].rearrange("p (h f) -> p h f", h=4),
                                axis=AX.X, op=ALU.max)
            v.drain()
            v.tensor_scalar(out=st[:, 0:16], in0=st[:, 0:16], scalar1=-1.0,
                            scalar2=None, op0=ALU.mult)
            v.drain()
            for tsub in range(4):
                for h in range(4):
                    sl = scr["probs"][:, tsub * 20 + h * 5: tsub * 20 + (h + 1) * 5]
                    v.tensor_scalar(out=sl, in0=sl, scalar1=st[:, tsub * 4 + h:tsub * 4 + h + 1],
                                    scalar2=None, op0=ALU.add)

    with nc.Block() as b:
        @b.scalar
        def _(sc):
            sc.activation(out=scr["probs"][:, 0:80], in_=scr["probs"][:, 0:80], func=AF.Exp)

    with nc.Block() as b:
        @b.vector
        def _(v):
            for tsub in range(4):
                v.tensor_reduce(out=st[:, tsub * 4:(tsub + 1) * 4],
                                in_=scr["probs"][:, tsub * 20:(tsub + 1) * 20].rearrange("p (h f) -> p h f", h=4),
                                axis=AX.X, op=ALU.add)
            v.drain()
            v.reciprocal(out=st[:, 0:16], in_=st[:, 0:16])
            v.drain()
            for tsub in range(4):
                for h in range(4):
                    sl = scr["probs"][:, tsub * 20 + h * 5: tsub * 20 + (h + 1) * 5]
                    v.tensor_scalar(out=sl, in0=sl, scalar1=st[:, tsub * 4 + h:tsub * 4 + h + 1],
                                    scalar2=None, op0=ALU.mult)
            v.drain()
            bp3 = scr["bp"][:, 0:32].rearrange("p (t g) -> p t g", t=4)[:, :, 4:8]
            st3 = st[:, 16:32].rearrange("p (t g) -> p t g", t=4)
            v.tensor_scalar(out=st3, in0=bp3,
                            scalar1=0.09, scalar2=0.9, op0=ALU.mult, op1=ALU.add)
            v.drain()
            v.tensor_tensor(out=st3, in0=st3, in1=bp3, op=ALU.mult)
            v.drain()
            v.tensor_scalar(out=st[:, 32:48], in0=st[:, 16:32], scalar1=-1.0,
                            scalar2=1.0, op0=ALU.mult, op1=ALU.add)
            v.drain()

    tokfir = [scr["firb"][1], scr["firb"][2], scr["firb"][3], scr["rawfull"]]
    with nc.Block() as b:
        @b.sync
        def _(sy):
            order = [3, 2, 1, 0]
            for oi, p in enumerate(order):
                if oi >= 1:
                    sy.wait_ge(s_tr, ctx.cnt["a0_tr"] + oi * 8 * 16)
                dstb = tokfir[p]
                for m in range(8):
                    sy.dma_start(
                        out=dstb[:].rearrange("p (t c) -> p t c", t=4)[:, :, m * 128:(m + 1) * 128],
                        in_=scr["firb"][p][:, m * 512:(m + 1) * 512],
                        transpose=True).then_inc(s_tr, 16)
            sy.wait_ge(s_tr, ctx.cnt["a0_tr"] + 32 * 16)

        @b.gpsimd
        def _(g):
            g.dma_start(out=wbuf[:], in_=aps["woa"][:, :]).then_inc(s_w, 16)
            g.wait_ge(s_w, ctx.cnt["a0_w"] + 16)
    ctx.cnt["a0_tr"] += 32 * 16
    ctx.cnt["a0_w"] += 16

    o_tok = scr["k_tok"]
    with nc.Block() as b:
        @b.vector
        def _(v):
            for kt in range(ND):
                v.tensor_scalar(out=wbuf[:, kt * 1024:(kt + 1) * 1024],
                                in0=wbuf[:, kt * 1024:(kt + 1) * 1024],
                                scalar1=scr["onwB"][:, kt:kt + 1],
                                scalar2=None, op0=ALU.mult)
            for tsub in range(4):
                for h in range(4):
                    osl = o_tok[:, tsub * 1024 + h * 256: tsub * 1024 + (h + 1) * 256]
                    for p in range(4):
                        src = tokfir[p][:, tsub * 1024 + h * 256: tsub * 1024 + (h + 1) * 256]
                        pcol = scr["probs"][:, tsub * 20 + h * 5 + p: tsub * 20 + h * 5 + p + 1]
                        if p == 0:
                            v.tensor_scalar(out=osl, in0=src, scalar1=pcol,
                                            scalar2=None, op0=ALU.mult)
                        else:
                            v.scalar_tensor_tensor(out=osl, in0=src, scalar=pcol,
                                                   in1=osl, op0=ALU.mult, op1=ALU.add)
                    dsl = delta_tok[:, tsub * 1024 + h * 256: tsub * 1024 + (h + 1) * 256]
                    pcol = scr["probs"][:, tsub * 20 + h * 5 + 4: tsub * 20 + h * 5 + 5]
                    v.scalar_tensor_tensor(out=osl, in0=dsl, scalar=pcol,
                                           in1=osl, op0=ALU.mult, op1=ALU.add)
                    vsl = scr["v_tok0"][:, tsub * 1024 + h * 256: tsub * 1024 + (h + 1) * 256]
                    v.tensor_scalar(out=scr["y0"][:, 0:256], in0=vsl,
                                    scalar1=st[:, 16 + tsub * 4 + h: 16 + tsub * 4 + h + 1],
                                    scalar2=None, op0=ALU.mult)
                    v.scalar_tensor_tensor(out=osl, in0=osl,
                                           scalar=st[:, 32 + tsub * 4 + h: 32 + tsub * 4 + h + 1],
                                           in1=scr["y0"][:, 0:256],
                                           op0=ALU.mult, op1=ALU.add)
            for tsub in range(4):
                v.tensor_tensor(out=scr["scrot"][:, 0:1024],
                                in0=o_tok[:, tsub * 1024:(tsub + 1) * 1024],
                                in1=o_tok[:, tsub * 1024:(tsub + 1) * 1024], op=ALU.mult)
                v.tensor_reduce(out=st[:, 48 + tsub * 4: 48 + (tsub + 1) * 4],
                                in_=scr["scrot"][:, 0:1024].rearrange("p (h c) -> p h c", h=4),
                                axis=AX.X, op=ALU.add)
                v.drain()
            v.tensor_scalar(out=st[:, 48:64], in0=st[:, 48:64], scalar1=1.0 / 256,
                            scalar2=1e-5, op0=ALU.mult, op1=ALU.add)

    with nc.Block() as b:
        @b.scalar
        def _(sc):
            sc.activation(out=st[:, 48:64], in_=st[:, 48:64], func=AF.Sqrt)

    s_ox = ctx.sem("a0_ox")
    b_ox = ctx.cnt["a0_ox"]
    b_om = ctx.cnt["a0_pm"]
    b_ol = ctx.cnt["a0_ld"]
    with nc.Block() as b:
        @b.vector
        def _(v):
            v.reciprocal(out=st[:, 48:64], in_=st[:, 48:64])
            v.drain()
            for tsub in range(4):
                for h in range(4):
                    osl = o_tok[:, tsub * 1024 + h * 256: tsub * 1024 + (h + 1) * 256]
                    ins = v.tensor_scalar(out=osl, in0=osl,
                                    scalar1=st[:, 48 + tsub * 4 + h: 48 + tsub * 4 + h + 1],
                                    scalar2=None, op0=ALU.mult)
            ins.then_inc(s_ox, 1)

        @b.sync
        def _(sy):
            sy.wait_ge(s_ox, b_ox + 1)
            for tsub in range(4):
                sy.dma_start(
                    out=scr["q_tok"][:].rearrange("p (f c) -> p f c", f=8)[:, :, tsub * 128:(tsub + 1) * 128],
                    in_=o_tok[:, tsub * 1024:(tsub + 1) * 1024],
                    transpose=True).then_inc(s_tr, 16)
            sy.wait_ge(s_tr, ctx.cnt["a0_tr"] + 4 * 16)
    ctx.cnt["a0_ox"] += 1
    ctx.cnt["a0_tr"] += 4 * 16

    oT = scr["q_tok"]
    s_st = ctx.sem("a0_st")
    b_st = ctx.cnt["a0_st"]
    b_xw = ctx.cnt["a0_w"]
    b_oxd = ctx.cnt["a0_ox"]
    with nc.Block() as b:
        @b.gpsimd
        def _(g):
            for tsub in range(4):
                if tsub >= 2:
                    g.wait_ge(s_ox, b_oxd + (tsub - 2) * 2 + 2)
                g.dma_start(out=scr["xrot"][:, (tsub % 2) * D:(tsub % 2 + 1) * D],
                            in_=xin[c0 + tsub * 128: c0 + (tsub + 1) * 128, :]
                            ).then_inc(s_w, 16)

        @b.tensor
        def _(t):
            for tsub in range(4):
                for nn in range(2):
                    idx = tsub * 2 + nn
                    ps = pb0 if idx % 2 == 0 else pb1
                    if idx >= 2:
                        t.wait_ge(s_ox, b_oxd + idx - 1)
                    for kt in range(ND):
                        ins = t.matmul(ps[:, 0:512],
                                       oT[:, kt * 512 + tsub * 128: kt * 512 + (tsub + 1) * 128],
                                       wbuf[:, kt * 1024 + nn * 512: kt * 1024 + (nn + 1) * 512],
                                       start=(kt == 0), stop=(kt == ND - 1))
                    ins.then_inc(s_pm, 1)

        @b.vector
        def _(v):
            for tsub in range(4):
                for nn in range(2):
                    idx = tsub * 2 + nn
                    ps = pb0 if idx % 2 == 0 else pb1
                    v.wait_ge(s_pm, b_om + idx + 1)
                    v.wait_ge(s_w, b_xw + (tsub + 1) * 16)
                    if idx >= 4:
                        v.wait_ge(s_st, b_st + (idx - 3) * 16)
                    v.tensor_tensor(out=scr["xo"][:, (idx % 4) * 512:(idx % 4 + 1) * 512],
                                    in0=ps[:, 0:512],
                                    in1=scr["xrot"][:, (tsub % 2) * D + nn * 512:(tsub % 2) * D + (nn + 1) * 512],
                                    op=ALU.add).then_inc(s_ox, 1)

        @b.sync
        def _(sy):
            for tsub in range(4):
                for nn in range(2):
                    idx = tsub * 2 + nn
                    sy.wait_ge(s_ox, b_oxd + idx + 1)
                    sy.dma_start(out=xout[c0 + tsub * 128: c0 + (tsub + 1) * 128,
                                          nn * 512:(nn + 1) * 512],
                                 in_=scr["xo"][:, (idx % 4) * 512:(idx % 4 + 1) * 512]
                                 ).then_inc(s_st, 16)
            sy.wait_ge(s_st, b_st + 8 * 16)
    ctx.cnt["a0_pm"] += 8
    ctx.cnt["a0_w"] += 64
    ctx.cnt["a0_ox"] += 8
    ctx.cnt["a0_st"] += 8 * 16


def prep_l0attn(inp):
    def prepW(w):
        return np.ascontiguousarray(w.reshape(8, 128, 1024).transpose(1, 0, 2).reshape(128, 8192))

    def prepWm(w):
        t = w.reshape(8, 128, 8, 128)
        return np.ascontiguousarray(t.transpose(1, 2, 0, 3).reshape(128, 8192))
    wqa = prepWm(inp["L0_attn_Wq"]); wka = prepWm(inp["L0_attn_Wk"])
    wva = prepWm(inp["L0_attn_Wv"]); woa = prepW(inp["L0_attn_Wo"])
    W1 = inp["L0_attn_router_W1"]
    w1p = np.empty((16, 2, 128, 3072), np.float32)
    W1r = W1.reshape(48, 128, 2048)
    for m in range(16):
        for hf in range(2):
            blk = W1r[hf * 24:(hf + 1) * 24, :, m * 128:(m + 1) * 128]
            w1p[m, hf] = blk.transpose(1, 0, 2).reshape(128, 3072)
    w1 = np.ascontiguousarray(w1p.reshape(32 * 128, 3072))
    w2 = np.ascontiguousarray(
        inp["L0_attn_router_W2"].reshape(16, 128, 20).transpose(1, 0, 2).reshape(128, 320))
    b1 = np.ascontiguousarray(inp["L0_attn_router_b1"].reshape(16, 128).T)
    b2 = inp["L0_attn_router_b2"].reshape(1, 20)
    bpw = np.concatenate([inp["L0_attn_Wb"], inp["L0_attn_id_gate_W"]], axis=1)
    bpw = np.ascontiguousarray(bpw.reshape(8, 128, 8).transpose(1, 0, 2).reshape(128, 64))
    idb = np.zeros((1, 8), np.float32)
    idb[0, 4:8] = inp["L0_attn_id_gate_b"]
    ltau = inp["L0_attn_log_tau"].reshape(1, 4)
    onw = np.ascontiguousarray(np.tile(inp["L0_attn_o_norm_w"], 4).reshape(8, 128).T)

    def prepC(w):
        return np.ascontiguousarray(w.reshape(8, 128, 4).transpose(1, 0, 2).reshape(128, 32))
    qcw = prepC(inp["L0_attn_q_conv_w"]); kcw = prepC(inp["L0_attn_k_conv_w"])
    vcw = prepC(inp["L0_attn_v_conv_w"])
    firw = np.zeros((8, 128, 56), np.float32)
    offs = {3: 0, 7: 3, 15: 10, 31: 25}
    for key, K in [("L0_attn_fir_w3", 3), ("L0_attn_fir_w7", 7),
                   ("L0_attn_fir_w15", 15), ("L0_attn_fir_w31", 31)]:
        f = inp[key].reshape(1024, K)
        fr = f.reshape(8, 128, K)
        firw[:, :, offs[K]:offs[K] + K] += fr
    firw = np.ascontiguousarray(firw.transpose(1, 0, 2).reshape(128, 448))
    cst0 = np.zeros((128, 512), np.float32)
    cst0[:, 0:128] = np.eye(128)
    cst0[:, 128:256] = -np.tril(np.ones((128, 128)), -1)
    cst0[:, 256:384] = -np.triu(np.ones((128, 128)), 1)
    cst0[:, 384:512] = np.triu(np.ones((128, 128)), 0)
    ones = np.ones((1, 128), np.float32)
    return {"wqa": wqa, "wka": wka, "wva": wva, "woa": woa, "w1": w1, "w2": w2,
            "b1": b1, "b2": np.ascontiguousarray(b2), "bpw": bpw,
            "idb": idb, "ltau": np.ascontiguousarray(ltau), "onw": onw,
            "qcw": qcw, "kcw": kcw, "vcw": vcw, "firw": firw,
            "cst0": cst0, "ones": ones}


def alloc_l0attn_scr(ctx):
    scr = {}
    convfull = ctx.sbuf("convfull", [128, 4096], BF16)
    scr["convfull"] = convfull
    scr["xhat2"] = convfull[:, 0:2048]
    scr["sq"] = convfull[:, 2048:3072]
    for name, shape, dt in [
            ("wbuf", [128, 8192], BF16), ("rawfull", [128, 4352], BF16),
            ("histq", [128, 256], BF16), ("histk", [128, 256], BF16),
            ("histv", [128, 256], BF16), ("vc", [128, 4352], BF16),
            ("q_tok", [128, 4096], BF16), ("k_tok", [128, 4096], BF16),
            ("v_tok0", [128, 4096], BF16), ("kn_tok", [128, 4096], BF16),
            ("scrot", [128, 2048], BF16),
            ("qnT", [128, 4096], BF16), ("knT", [128, 4096], BF16),
            ("kbnT", [128, 4096], BF16),
            ("y0", [128, 512], BF16), ("tA", [128, 512], BF16),
            ("tB", [128, 512], BF16), ("Pp", [128, 512], BF16),
            ("attT", [128, 128], BF16), ("u_sb", [128, 256], BF16),
            ("w_sb", [128, 256], BF16), ("wnegT", [128, 256], BF16),
            ("uadj", [128, 256], BF16), ("uadj2", [128, 256], BF16),
            ("S32", [128, 2048], F32), ("S_bf", [128, 2048], BF16),
            ("dcol", [128, 16], F32), ("ecol", [128, 16], F32),
            ("bp", [128, 32], F32), ("probs", [128, 80], F32),
            ("invtauB", [128, 4], F32), ("rowb", [1, 64], BF16),
            ("rs0", [128, 64], BF16),
            ("xrot", [128, 2048], F32), ("xo", [128, 2048], F32),
            ("st", [128, 96], F32),
            ("cst0", [128, 512], BF16), ("ones", [1, 128], BF16),
            ("qcw", [128, 32], F32), ("kcw", [128, 32], F32),
            ("vcw", [128, 32], F32), ("firw", [128, 448], F32),
            ("w2B", [128, 320], BF16), ("b1B", [128, 16], F32),
            ("b2B", [1, 20], BF16), ("bpW", [128, 64], BF16),
            ("idbB", [1, 8], BF16), ("ltau", [1, 4], F32),
            ("onwB", [128, 8], F32)]:
        scr[name] = ctx.sbuf(name, shape, dt)
    scr["hmid"] = scr["scrot"][:, 0:1024]
    scr["firb"] = [ctx.sbuf("fir%d" % p, [128, 4096], BF16) for p in range(4)]
    for name in ["ps_b0", "ps_b1", "ps_a", "ps_d", "ps_f", "ps_su"]:
        scr[name] = ctx.psum(name, [128, 512])
    return scr

_L0A_SHAPES = {"wqa": [128, 8192], "wka": [128, 8192], "wva": [128, 8192],
               "woa": [128, 8192], "w1": [4096, 3072], "w2": [128, 320],
               "b1": [128, 16], "b2": [1, 20], "bpw": [128, 64], "idb": [1, 8],
               "ltau": [1, 4], "onw": [128, 8], "qcw": [128, 32], "kcw": [128, 32],
               "vcw": [128, 32], "firw": [128, 448], "cst0": [128, 512],
               "ones": [1, 128]}

_L1A_SHAPES = {"wq": [8 * 128, 2048], "wk": [8 * 128, 2048], "wv": [128, 8192],
               "wo": [128, 8192], "bq": [128, 16], "bk": [128, 16],
               "bv": [1, 1024], "bo": [1, 1024], "C": [128, 2048],
               "Sg": [128, 2048], "cst": [128, 384], "ones": [1, 128],
               "lqk": [16, 256], "hn": [128, 8]}


def build_l0attn_prog():
    ctx = Ctx()
    nc = ctx.nc
    xin = nc.dram_tensor("xin", [S, D], F32, kind="ExternalInput").ap()
    dr = {k: nc.dram_tensor("i_" + k, shp, F32, kind="ExternalInput").ap()
          for k, shp in _L0A_SHAPES.items()}
    xout = nc.dram_tensor("xout", [S, D], F32, kind="ExternalOutput").ap()
    xhatT = ctx.sbuf("xhatT", [128, ND * S], BF16)
    scr = alloc_l0attn_scr(ctx)
    emit_ln_stream(ctx, xin, xhatT, scr, "ln0")
    emit_l0attn(ctx, xin, xout, xhatT, dr, scr)
    ctx.close()
    return ctx


def build_l0ffn_prog():
    ctx = Ctx()
    nc = ctx.nc
    xin = nc.dram_tensor("xin", [S, D], F32, kind="ExternalInput").ap()
    gw = nc.dram_tensor("i_gw", [128, 64], F32, kind="ExternalInput").ap()
    gu = nc.dram_tensor("i_gu", [8 * 16 * 128, 2048], F32, kind="ExternalInput").ap()
    dw = nc.dram_tensor("i_dw", [8 * 128, 16384], F32, kind="ExternalInput").ap()
    xout = nc.dram_tensor("xout", [S, D], F32, kind="ExternalOutput").ap()

    x_res = ctx.sbuf("x_res", [128, NT * D], F32)
    xhatT = ctx.sbuf("xhatT", [128, ND * S], BF16)
    h = ctx.sbuf("h", [128, 16 * 2048], BF16)
    scr = {
        "xhat2": h[:, 0:2 * D],
        "sq": h[:, 2 * D:3 * D],
        "st": ctx.sbuf("st", [128, 96], F32),
        "h": h,
        "guW": ctx.sbuf("guW", [128, 2 * 2048], BF16),
        "dwW": ctx.sbuf("dwW", [128, 16384], BF16),
        "sg": ctx.sbuf("sg", [128, 2 * 512], BF16),
        "comb": ctx.sbuf("comb", [128, 128], F32),
        "pex": ctx.sbuf("pex", [128, 16], F32),
        "gwB": ctx.sbuf("gwB", [128, 128], BF16),
        "pgu0": ctx.psum("pgu0", [128, 1024]),
        "pgu1": ctx.psum("pgu1", [128, 1024]),
        "pd0": ctx.psum("pd0", [128, 1024]),
        "pd1": ctx.psum("pd1", [128, 1024]),
    }
    s_io = ctx.sem("io")
    with ctx.nc.Block() as b:
        @b.sync
        def _(sy):
            sy.dma_start(out=x_res[:].rearrange("p (t d) -> p t d", t=NT),
                         in_=xin[:].rearrange("(t p) d -> p t d", p=128)).then_inc(s_io, 16)
            sy.wait_ge(s_io, 16)
    emit_ln(ctx, x_res, xhatT, scr, "ln1")
    emit_l0ffn(ctx, x_res, xhatT, {"gw": gw, "gu": gu, "dw": dw}, scr)
    with ctx.nc.Block() as b:
        @b.sync
        def _(sy):
            sy.dma_start(out=xout[:].rearrange("(t p) d -> p t d", p=128),
                         in_=x_res[:].rearrange("p (t d) -> p t d", t=NT)).then_inc(s_io, 16)
            sy.wait_ge(s_io, 32)
    ctx.close()
    return ctx


def build_l1attn_prog():
    ctx = Ctx()
    nc = ctx.nc
    xin = nc.dram_tensor("xin", [S, D], F32, kind="ExternalInput").ap()
    dr = {k: nc.dram_tensor("i_" + k, shp, F32, kind="ExternalInput").ap()
          for k, shp in _L1A_SHAPES.items()}
    xout = nc.dram_tensor("xout", [S, D], F32, kind="ExternalOutput").ap()

    x_res = ctx.sbuf("x_res", [128, NT * D], F32)
    xhatT = ctx.sbuf("xhatT", [128, ND * S], BF16)
    scr = {
        "u1": ctx.sbuf("u1", [128, 8192], BF16),
        "v_tok": ctx.sbuf("v_tok", [128, NT * 1024], BF16),
        "o_tok": ctx.sbuf("o_tok", [128, NT * 1024], BF16),
        "e1": ctx.sbuf("e1", [128, 2048], BF16),
        "e2": ctx.sbuf("e2", [128, 2048], BF16),
        "aS": ctx.sbuf("aS", [128, 2048], BF16),
        "aT": ctx.sbuf("aT", [128, 2048], BF16),
        "rs": ctx.sbuf("rs", [128, 512], BF16),
        "C": ctx.sbuf("C", [128, 2048], BF16),
        "Sg": ctx.sbuf("Sg", [128, 2048], BF16),
        "cst": ctx.sbuf("cst", [128, 384], BF16),
        "ones": ctx.sbuf("onesb", [1, 128], BF16),
        "bqB": ctx.sbuf("bqB", [128, 16], F32),
        "bkB": ctx.sbuf("bkB", [128, 16], F32),
        "bvB": ctx.sbuf("bvB", [1, 1024], BF16),
        "boB": ctx.sbuf("boB", [1, 1024], BF16),
        "hnB": ctx.sbuf("hnB", [128, 8], F32),
        "lamB": ctx.sbuf("lamB", [128, 16], F32),
        "ssqB": ctx.sbuf("ssqB", [128, 256], F32),
        "st": ctx.sbuf("st", [128, 96], F32),
        "ps_s1": ctx.psum("ps_s1", [128, 2048]),
        "ps_s2": ctx.psum("ps_s2", [128, 1536]),
        "ps_o": ctx.psum("ps_o", [128, 512]),
    }
    scr["lqk"] = scr["e1"][:, 0:512].bitcast(F32)
    scr["xhat2"] = scr["v_tok"][:, 0:2 * D]
    scr["sq"] = scr["v_tok"][:, 2 * D:3 * D]
    scr["oT"] = xhatT
    s_io = ctx.sem("io")
    with ctx.nc.Block() as b:
        @b.sync
        def _(sy):
            sy.dma_start(out=x_res[:].rearrange("p (t d) -> p t d", t=NT),
                         in_=xin[:].rearrange("(t p) d -> p t d", p=128)).then_inc(s_io, 16)
            sy.wait_ge(s_io, 16)
    emit_ln(ctx, x_res, xhatT, scr, "ln1")
    emit_l1attn(ctx, x_res, xhatT, dr, scr)
    with ctx.nc.Block() as b:
        @b.sync
        def _(sy):
            sy.dma_start(out=xout[:].rearrange("(t p) d -> p t d", p=128),
                         in_=x_res[:].rearrange("p (t d) -> p t d", t=NT)).then_inc(s_io, 16)
            sy.wait_ge(s_io, 32)
    ctx.close()
    return ctx


def build_l1ffn_prog():
    ctx = Ctx()
    nc = ctx.nc
    xin = nc.dram_tensor("xin", [S, D], F32, kind="ExternalInput").ap()
    gw = nc.dram_tensor("i_gw", [128, 128], F32, kind="ExternalInput").ap()
    shp = nc.dram_tensor("i_shp", [8 * 128, 2048], F32, kind="ExternalInput").ap()
    sho = nc.dram_tensor("i_sho", [128, 8192], F32, kind="ExternalInput").ap()
    ep = nc.dram_tensor("i_ep", [16 * 4 * 128, 2048], F32, kind="ExternalInput").ap()
    eo = nc.dram_tensor("i_eo", [16 * 128, 4096], F32, kind="ExternalInput").ap()
    xout = nc.dram_tensor("xout", [S, D], F32, kind="ExternalOutput").ap()

    x_res = ctx.sbuf("x_res", [128, NT * D], F32)
    xhatT = ctx.sbuf("xhatT", [128, ND * S], BF16)
    h = ctx.sbuf("h", [128, 16 * 2048], BF16)
    scr = {
        "xhat2": h[:, 0:2 * D],
        "sq": h[:, 2 * D:3 * D],
        "st": ctx.sbuf("st", [128, 96], F32),
        "h": h,
        "guW": ctx.sbuf("guW", [128, 2 * 2048], BF16),
        "dwW": ctx.sbuf("dwW", [128, 16384], BF16),
        "sg": ctx.sbuf("sg", [128, 2 * 512], BF16),
        "comb": ctx.sbuf("comb", [128, 256], F32),
        "pex": ctx.sbuf("pex", [128, 16], F32),
        "gwB": ctx.sbuf("gwB", [128, 128], BF16),
        "pgu0": ctx.psum("pgu0", [128, 1024]),
        "pgu1": ctx.psum("pgu1", [128, 1024]),
        "pd0": ctx.psum("pd0", [128, 1024]),
        "pd1": ctx.psum("pd1", [128, 1024]),
    }
    s_io = ctx.sem("io")
    with ctx.nc.Block() as b:
        @b.sync
        def _(sy):
            sy.dma_start(out=x_res[:].rearrange("p (t d) -> p t d", t=NT),
                         in_=xin[:].rearrange("(t p) d -> p t d", p=128)).then_inc(s_io, 16)
            sy.wait_ge(s_io, 16)
    emit_ln(ctx, x_res, xhatT, scr, "ln1")
    emit_l1ffn(ctx, x_res, xhatT,
               {"gw": gw, "shp": shp, "sho": sho, "ep": ep, "eo": eo}, scr)
    with ctx.nc.Block() as b:
        @b.sync
        def _(sy):
            sy.dma_start(out=xout[:].rearrange("(t p) d -> p t d", p=128),
                         in_=x_res[:].rearrange("p (t d) -> p t d", t=NT)).then_inc(s_io, 16)
            sy.wait_ge(s_io, 32)
    ctx.close()
    return ctx


_CACHE = {}


def kernel_unfused(**inputs):
    inputs = {k: np.ascontiguousarray(np.asarray(v, dtype=np.float32))
              for k, v in inputs.items()}
    if "progs" not in _CACHE:
        _CACHE["progs"] = (build_l0attn_prog(), build_l0ffn_prog(),
                           build_l1attn_prog(), build_l1ffn_prog())
    p0a, p0f, p1a, p1f = _CACHE["progs"]
    w0a = prep_l0attn(inputs)
    w0f = prep_l0ffn(inputs)
    w1a = prep_l1attn(inputs)
    w1f = prep_l1ffn(inputs)

    x = inputs["L0_attn_hidden_states"]
    ncore = 8

    def launch(ctx, wmap, xcur):
        maps = []
        for c in range(ncore):
            m = {"xin": np.ascontiguousarray(xcur[c])}
            m.update({"i_" + k: v for k, v in wmap.items()})
            maps.append(m)
        res = run_bass_kernel_spmd(ctx.nc, maps, core_ids=list(range(ncore)))
        return np.stack([res.results[c]["xout"] for c in range(ncore)], axis=0)

    x = launch(p0a, w0a, x)
    x = launch(p0f, w0f, x)
    x = launch(p1a, w1a, x)
    x = launch(p1f, w1f, x)
    return x



def _alloc_ffn_scr(ctx, comb_cols, suf):
    h = ctx.sbuf("h" + suf, [128, 16 * 2048], BF16)
    scr = {
        "xhat2": h[:, 0:2 * D],
        "sq": h[:, 2 * D:3 * D],
        "st": ctx.sbuf("st" + suf, [128, 96], F32),
        "h": h,
        "guW": ctx.sbuf("guW" + suf, [128, 2 * 2048], BF16),
        "dwW": ctx.sbuf("dwW" + suf, [128, 16384], BF16),
        "sg": ctx.sbuf("sg" + suf, [128, 2 * 512], BF16),
        "comb": ctx.sbuf("comb" + suf, [128, comb_cols], F32),
        "pex": ctx.sbuf("pex" + suf, [128, 16], F32),
        "gwB": ctx.sbuf("gwB" + suf, [128, 128], BF16),
        "pgu0": ctx.psum("pgu0" + suf, [128, 1024]),
        "pgu1": ctx.psum("pgu1" + suf, [128, 1024]),
        "pd0": ctx.psum("pd0" + suf, [128, 1024]),
        "pd1": ctx.psum("pd1" + suf, [128, 1024]),
    }
    return scr


def _alloc_l1attn_scr(ctx, xhatT):
    scr = {
        "u1": ctx.sbuf("u1", [128, 8192], BF16),
        "v_tok": ctx.sbuf("v_tok1", [128, NT * 1024], BF16),
        "o_tok": ctx.sbuf("o_tok", [128, NT * 1024], BF16),
        "e1": ctx.sbuf("e1", [128, 2048], BF16),
        "e2": ctx.sbuf("e2", [128, 2048], BF16),
        "aS": ctx.sbuf("aS", [128, 2048], BF16),
        "aT": ctx.sbuf("aT", [128, 2048], BF16),
        "rs": ctx.sbuf("rs", [128, 512], BF16),
        "C": ctx.sbuf("C", [128, 2048], BF16),
        "Sg": ctx.sbuf("Sg", [128, 2048], BF16),
        "cst": ctx.sbuf("cst", [128, 384], BF16),
        "ones": ctx.sbuf("onesb", [1, 128], BF16),
        "bqB": ctx.sbuf("bqB", [128, 16], F32),
        "bkB": ctx.sbuf("bkB", [128, 16], F32),
        "bvB": ctx.sbuf("bvB", [1, 1024], BF16),
        "boB": ctx.sbuf("boB", [1, 1024], BF16),
        "hnB": ctx.sbuf("hnB", [128, 8], F32),
        "lamB": ctx.sbuf("lamB", [128, 16], F32),
        "ssqB": ctx.sbuf("ssqB", [128, 256], F32),
        "st": ctx.sbuf("st1a", [128, 96], F32),
        "ps_s1": ctx.psum("ps_s1", [128, 2048]),
        "ps_s2": ctx.psum("ps_s2", [128, 1536]),
        "ps_o": ctx.psum("ps_o", [128, 512]),
    }
    scr["lqk"] = scr["e1"][:, 0:512].bitcast(F32)
    scr["xhat2"] = scr["v_tok"][:, 0:2 * D]
    scr["sq"] = scr["v_tok"][:, 2 * D:3 * D]
    scr["oT"] = xhatT
    return scr


def build_fused_prog():
    ctx = Ctx()
    nc = ctx.nc
    xin = nc.dram_tensor("xin", [S, D], F32, kind="ExternalInput").ap()
    xout = nc.dram_tensor("xout", [S, D], F32, kind="ExternalOutput").ap()
    xmid = nc.dram_tensor("xmid", [S, D], F32, kind="Internal").ap()
    dr = {}
    for k, shp in _L0A_SHAPES.items():
        dr["a0_" + k] = nc.dram_tensor("a0_" + k, shp, F32, kind="ExternalInput").ap()
    for k, shp in [("gw", [128, 64]), ("gu", [8 * 16 * 128, 2048]),
                   ("dw", [8 * 128, 16384])]:
        dr["f0_" + k] = nc.dram_tensor("f0_" + k, shp, F32, kind="ExternalInput").ap()
    for k, shp in _L1A_SHAPES.items():
        dr["a1_" + k] = nc.dram_tensor("a1_" + k, shp, F32, kind="ExternalInput").ap()
    for k, shp in [("gw", [128, 128]), ("shp", [8 * 128, 2048]),
                   ("sho", [128, 8192]), ("ep", [16 * 4 * 128, 2048]),
                   ("eo", [16 * 128, 4096])]:
        dr["f1_" + k] = nc.dram_tensor("f1_" + k, shp, F32, kind="ExternalInput").ap()

    xhatT = ctx.sbuf("xhatT", [128, ND * S], BF16)

    ctx.push_scope()
    scr = alloc_l0attn_scr(ctx)
    emit_ln_stream(ctx, xin, xhatT, scr, "ln0")
    emit_l0attn(ctx, xin, xmid, xhatT,
                {k: dr["a0_" + k] for k in _L0A_SHAPES}, scr)
    ctx.pop_scope()

    x_res = ctx.sbuf("x_res", [128, NT * D], F32)
    s_io = ctx.sem("io")
    ctx.push_scope()
    scr = _alloc_ffn_scr(ctx, 128, "f0")
    with nc.Block() as b:
        @b.sync
        def _(sy):
            sy.dma_start(out=x_res[:].rearrange("p (t d) -> p t d", t=NT),
                         in_=xmid[:].rearrange("(t p) d -> p t d", p=128)).then_inc(s_io, 16)
            sy.wait_ge(s_io, 16)
    emit_ln(ctx, x_res, xhatT, scr, "ln1")
    emit_l0ffn(ctx, x_res, xhatT,
               {"gw": dr["f0_gw"], "gu": dr["f0_gu"], "dw": dr["f0_dw"]}, scr)
    ctx.pop_scope()

    ctx.push_scope()
    scr = _alloc_l1attn_scr(ctx, xhatT)
    emit_ln(ctx, x_res, xhatT, scr, "ln2")
    emit_l1attn(ctx, x_res, xhatT, {k: dr["a1_" + k] for k in _L1A_SHAPES}, scr)
    ctx.pop_scope()

    ctx.push_scope()
    scr = _alloc_ffn_scr(ctx, 256, "f1")
    emit_ln(ctx, x_res, xhatT, scr, "ln3")
    emit_l1ffn(ctx, x_res, xhatT,
               {"gw": dr["f1_gw"], "shp": dr["f1_shp"], "sho": dr["f1_sho"],
                "ep": dr["f1_ep"], "eo": dr["f1_eo"]}, scr)
    ctx.pop_scope()

    with nc.Block() as b:
        @b.sync
        def _(sy):
            sy.dma_start(out=xout[:].rearrange("(t p) d -> p t d", p=128),
                         in_=x_res[:].rearrange("p (t d) -> p t d", t=NT)).then_inc(s_io, 16)
            sy.wait_ge(s_io, 32)

    ctx.close()
    return ctx


def kernel_fused(**inputs):
    inputs = {k: np.ascontiguousarray(np.asarray(v, dtype=np.float32))
              for k, v in inputs.items()}
    if "fprog" not in _CACHE:
        _CACHE["fprog"] = build_fused_prog()
    ctx = _CACHE["fprog"]
    w0a = prep_l0attn(inputs)
    w0f = prep_l0ffn(inputs)
    w1a = prep_l1attn(inputs)
    w1f = prep_l1ffn(inputs)
    wall = {}
    wall.update({"a0_" + k: v for k, v in w0a.items()})
    wall.update({"f0_" + k: v for k, v in w0f.items()})
    wall.update({"a1_" + k: v for k, v in w1a.items()})
    wall.update({"f1_" + k: v for k, v in w1f.items()})
    x = inputs["L0_attn_hidden_states"]
    maps = []
    for c in range(8):
        m = {"xin": np.ascontiguousarray(x[c])}
        m.update(wall)
        maps.append(m)
    res = run_bass_kernel_spmd(ctx.nc, maps, core_ids=list(range(8)))
    return np.stack([res.results[c]["xout"] for c in range(8)], axis=0)


def kernel(**inputs):
    return kernel_fused(**inputs)
